# version 12
# baseline (speedup 1.0000x reference)
import numpy as np
import ml_dtypes
import concourse.bass as bass
import concourse.mybir as mybir
from concourse.bass_utils import run_bass_kernel_spmd
from contextlib import ExitStack

F32 = mybir.dt.float32
BF16 = mybir.dt.bfloat16
I32 = mybir.dt.int32
AF = mybir.ActivationFunctionType
ALU = mybir.AluOpType
AX = mybir.AxisListType
BF = ml_dtypes.bfloat16

T = 4096
D = 1024
NSUP = 8
TS = 512
NBLK = 32
NEG = -1e30

GUARD_ON = True
PROLOGUE = True
EPILOGUE = False

QPERM = [0, 4, 1, 5, 2, 6, 3, 7, 8, 12, 9, 13, 10, 14, 11, 15]
KVH = [h // 4 for h in QPERM]


class Plan:
    ENGS = ("sync", "gpsimd", "vector", "scalar", "tensor")

    def __init__(self):
        self.ops = {e: [] for e in self.ENGS}
        self.cnt = {}
        self.sat = {e: {} for e in self.ENGS}

    def add(self, eng, fn, waits=(), inc=None, inc_n=1):
        ws = {}
        for tk in waits:
            if tk is None:
                continue
            sem, val = tk
            if self.sat[eng].get(sem, 0) >= val:
                continue
            ws[sem] = max(ws.get(sem, 0), val)
        for sem, val in ws.items():
            self.sat[eng][sem] = val
        tok = None
        if inc is not None:
            self.cnt[inc] = self.cnt.get(inc, 0) + inc_n
            tok = (inc, self.cnt[inc])
        self.ops[eng].append((fn, sorted(ws.items()), inc, inc_n))
        return tok

    def barrier(self):
        for q in ("sync", "gpsimd"):
            ws = [(f"d_{q}{i}", self.cnt[f"d_{q}{i}"]) for i in range(8)
                  if f"d_{q}{i}" in self.cnt]
            if ws:
                self.add(q, None, waits=ws)
        toks = []
        for e in self.ENGS:
            toks.append(self.add(e, None, inc="bar_" + e))
        for e in self.ENGS:
            self.add(e, None, waits=toks)

    def check_deadlock(self):
        idx = {e: 0 for e in self.ENGS}
        vals = {}
        progress = True
        while progress:
            progress = False
            for e in self.ENGS:
                while idx[e] < len(self.ops[e]):
                    fn, ws, inc, inc_n = self.ops[e][idx[e]]
                    if any(vals.get(s, 0) < v for s, v in ws):
                        break
                    if inc:
                        vals[inc] = vals.get(inc, 0) + inc_n
                    idx[e] += 1
                    progress = True
        stuck = {e: (idx[e], len(self.ops[e])) for e in self.ENGS if idx[e] < len(self.ops[e])}
        if stuck:
            for e, (i, n) in stuck.items():
                fn, ws, inc, _ = self.ops[e][i]
                need = [(s, v, vals.get(s, 0)) for s, v in ws if vals.get(s, 0) < v]
                raise RuntimeError(f"deadlock: {e} op {i}/{n} needs {need}")

    def n_ops(self):
        return sum(len(v) for v in self.ops.values())


class Buf:
    __slots__ = ("wr", "rd", "dly")

    def __init__(self):
        self.wr = None
        self.rd = []
        self.dly = 0


class B:

    def __init__(self, nc, plan):
        self.nc = nc
        self.p = plan
        self.sems = {}

    def step(self, eng, fn, reads=(), writes=(), extra=()):
        waits = list(extra)
        for b in reads:
            if b.wr is not None:
                waits.append(b.wr)
        for b in writes:
            if b.wr is not None:
                waits.append(b.wr)
            waits.extend(b.rd)
        if eng == "tensor":
            waits = [w for w in waits if w is None or w[0] != "s_tensor"]
        tok = self.p.add(eng, fn, waits=waits, inc="s_" + eng)
        for b in reads:
            b.rd.append(tok)
        for b in writes:
            b.wr = tok
            b.rd = []
            b.dly = 0
        return tok

    def v(self, fn, reads=(), writes=(), extra=()):
        return self.step("vector", fn, reads, writes, extra)

    def s(self, fn, reads=(), writes=(), extra=()):
        return self.step("scalar", fn, reads, writes, extra)

    def t(self, fn, reads=(), writes=(), extra=(), guard=0):
        tok = self.step("tensor", fn, reads, writes, extra)
        if guard and GUARD_ON:
            gtok = self.p.add("tensor", lambda e, g=guard: e.nop(cycle_cnt=g, nofuse=True),
                              waits=(), inc="s_tensor")
            for bb in writes:
                bb.wr = gtok
            return gtok
        return tok

    LANES = 8

    def dma(self, queue, fn, reads=(), writes=(), extra=()):
        waits = list(extra)
        for b in reads:
            if b.wr is not None:
                waits.append(b.wr)
        for b in writes:
            if b.wr is not None:
                waits.append(b.wr)
            waits.extend(b.rd)
        key = "_lane_" + queue
        idx = self.p.cnt.get(key, 0)
        self.p.cnt[key] = idx + 1
        sem = f"d_{queue}{idx % self.LANES}"
        waits = [w for w in waits if w is None or not w[0].startswith("d_" + queue)]
        tok = self.p.add(queue, fn, waits=waits, inc=sem, inc_n=16)
        for b in reads:
            b.rd.append(tok)
        for b in writes:
            b.wr = tok
            b.rd = []
        return tok


def emit_block(nc, plan, stack):
    sems = {}
    for name in plan.cnt:
        if name.startswith("_"):
            continue
        sems[name] = stack.enter_context(nc.semaphore(name))
    block = stack.enter_context(nc.Block())

    def runner(eng_name):
        def run(eng):
            if PROLOGUE:
                if eng_name == "gpsimd":
                    for h in sems.values():
                        eng.sem_clear(h)
                else:
                    for _ in range(8):
                        eng.nop(cycle_cnt=5000, nofuse=True)
            for fn, ws, inc, inc_n in plan.ops[eng_name]:
                for sem, val in ws:
                    eng.wait_ge(sems[sem], val)
                if fn == "CLEAR_NONBAR":
                    for nm, h in sems.items():
                        if not nm.startswith("bar_"):
                            eng.sem_clear(h)
                    continue
                if fn == "CLEAR_BAR":
                    for nm, h in sems.items():
                        if nm.startswith("bar_"):
                            eng.sem_clear(h)
                    continue
                if fn is None:
                    if inc is not None:
                        eng.nop().then_inc(sems[inc], inc_n)
                    continue
                ins = fn(eng)
                if inc is not None:
                    ins.then_inc(sems[inc], inc_n)
        return run

    block.sync(runner("sync"))
    block.gpsimd(runner("gpsimd"))
    block.vector(runner("vector"))
    block.scalar(runner("scalar"))
    block.tensor(runner("tensor"))


def prep_inputs(inp):
    f = np.ascontiguousarray
    d = {}
    Wq = np.asarray(inp["L0_attn_W_q"])
    Wo = np.asarray(inp["L0_attn_W_o"])
    qcols = np.concatenate([np.arange(h * 64, (h + 1) * 64) for h in QPERM])
    d["wq"] = f(Wq[:, qcols])
    d["wo"] = f(Wo[qcols, :])
    d["wk"] = f(np.asarray(inp["L0_attn_W_k"]))
    d["wv"] = f(np.asarray(inp["L0_attn_W_v"]))
    w1 = np.asarray(inp["L0_ffn_w1"])
    d["w1xT"] = f(w1[:, :D].T)
    d["w1mT"] = f(w1[:, D:].T)
    d["b1"] = f(np.asarray(inp["L0_ffn_b1"]).reshape(4, 128))
    d["w2T"] = f(np.asarray(inp["L0_ffn_w2"]).T)
    d["b2"] = f(np.asarray(inp["L0_ffn_b2"]).reshape(1, 64))
    d["tau"] = f(np.asarray(inp["L0_ffn_tau"]).reshape(1, 1))
    d["slot_init"] = f(np.asarray(inp["L0_ffn_slot_init"]))
    d["slot_scale"] = f(np.asarray(inp["L0_ffn_slot_scale"]))
    d["wihT"] = f(np.asarray(inp["L0_ffn_gru_wih"]).T)
    d["whhT"] = f(np.asarray(inp["L0_ffn_gru_whh"]).T)
    d["bih"] = f(np.asarray(inp["L0_ffn_gru_bih"]).reshape(1, 3072))
    d["bhh"] = f(np.asarray(inp["L0_ffn_gru_bhh"]).reshape(1, 3072))
    d["wv0T"] = f(np.asarray(inp["L0_ffn_wv"]).T)
    d["bv0"] = f(np.asarray(inp["L0_ffn_bv"]).reshape(1, 1024))
    d["wo0T"] = f(np.asarray(inp["L0_ffn_wo"]).T)
    d["bo0"] = f(np.asarray(inp["L0_ffn_bo"]).reshape(1, 1024))
    d["lng"] = f(np.asarray(inp["L1_attn_ln_gamma"]).reshape(1, 1024))
    d["lnb"] = f(np.asarray(inp["L1_attn_ln_beta"]).reshape(1, 1024))
    d["inpT"] = f(np.asarray(inp["L1_attn_in_proj_w"]).T)
    d["convw"] = f(np.asarray(inp["L1_attn_conv_w"]))
    d["convb"] = f(np.asarray(inp["L1_attn_conv_b"]).reshape(16, 128))
    xp = np.asarray(inp["L1_attn_x_proj_w"])
    xpp = np.zeros((2048, 128), np.float32)
    xpp[:, 0:16] = xp[64 + 16:96, :].T
    xpp[:, 32:48] = xp[64:64 + 16, :].T
    xpp[:, 64:128] = xp[0:64, :].T
    d["xprojTp"] = xpp
    d["dtprojT"] = f(np.asarray(inp["L1_attn_dt_proj_w"]).T)
    d["dtb"] = f(np.asarray(inp["L1_attn_dt_proj_b"]).reshape(16, 128))
    d["Alog"] = f(np.asarray(inp["L1_attn_A_log"]))
    d["Dp"] = f(np.asarray(inp["L1_attn_D_param"]).reshape(16, 128))
    d["outpT"] = f(np.asarray(inp["L1_attn_out_proj_w"]).T)
    d["wfcT"] = f(np.asarray(inp["L1_ffn_w_fc"]).T)
    d["wfcO"] = f(np.asarray(inp["L1_ffn_w_fc"]))
    d["wprojO"] = f(np.asarray(inp["L1_ffn_w_proj"]))
    d["bfcq"] = f(np.asarray(inp["L1_ffn_b_fc"]).reshape(1, 4096))
    d["wprojT"] = f(np.asarray(inp["L1_ffn_w_proj"]).T)
    d["bprojq"] = f(np.asarray(inp["L1_ffn_b_proj"]).reshape(1, 1024))
    d["ident"] = np.eye(128, dtype=np.float32)
    r = np.arange(128)
    d["mask1"] = np.where(r[:, None] >= r[None, :], 0.0, NEG).astype(np.float32)
    mT = np.zeros((4, 128, 512), np.float32)
    for pos in range(4):
        j = np.arange(512)
        mT[pos] = np.where(j[None, :] >= pos * 128 + r[:, None], 0.0, NEG)
    d["maskT"] = mT
    return d


class Arena:

    def __init__(self, handle, nbytes):
        self.h = handle
        self.n = nbytes
        self.off = 0

    def mark(self):
        return self.off

    def reset(self, mark):
        self.off = mark

    def alloc(self, shape, dtype):
        esz = mybir.dt.size(dtype)
        per = int(np.prod(shape[1:])) * esz
        off = (self.off + 31) // 32 * 32
        self.off = off + per
        assert self.off <= self.n, f"arena overflow: {self.off} > {self.n}"
        ap = self.h[:, off:off + per].bitcast(dtype)
        if len(shape) == 2:
            ap = ap[0:shape[0], :]
        else:
            kw = {}
            names = "abcdef"[: len(shape) - 1]
            pat = f"p ({' '.join(names)}) -> p {' '.join(names)}"
            for nm, sz in zip(names, shape[1:]):
                kw[nm] = sz
            ap = ap.rearrange(pat, **kw)[0:shape[0]]
        return ap


def new_buf():
    return Buf()


class Prog:

    def __init__(self):
        self.nc = bass.Bass("TRN2", target_bir_lowering=False)
        self.plan = Plan()
        self.b = B(self.nc, self.plan)
        self.stack = ExitStack()
        self.hbm = {}
        self.out_names = []
        self._ar_h = None
        self._ps_h = None
        self._pstr_h = None

    def arena(self, tag):
        if self._ar_h is None:
            self._ar_h = self.stack.enter_context(
                self.nc.sbuf_tensor("arena", [128, 212200], mybir.dt.uint8))
        return Arena(self._ar_h[:], 212200)

    def psum(self):
        if self._ps_h is None:
            self._ps_h = [self.stack.enter_context(
                self.nc.psum_tensor(f"ps{i}", [128, 512], F32)) for i in range(7)]
            self._pstr_h = self.stack.enter_context(
                self.nc.psum_tensor("pstr", [128, 1024], BF16))
        ps = [(h[:], Buf()) for h in self._ps_h]
        pstr = (self._pstr_h[:], Buf())
        return ps, pstr

    def dram_in(self, name, shape, dtype=F32):
        ap = self.nc.dram_tensor(name, list(shape), dtype, kind="ExternalInput").ap()
        self.hbm[name] = (ap, Buf())
        return ap

    def dram_out(self, name, shape, dtype=F32):
        ap = self.nc.dram_tensor(name, list(shape), dtype, kind="ExternalOutput").ap()
        self.hbm[name] = (ap, Buf())
        self.out_names.append(name)
        return ap

    def dram_tmp(self, name, shape, dtype=F32):
        ap = self.nc.dram_tensor(name, list(shape), dtype, kind="Internal").ap()
        self.hbm[name] = (ap, Buf())
        return ap

    def finish(self):
        b = self.b
        toks = []
        for name, (ap, buf) in self.hbm.items():
            if buf.wr is not None:
                toks.append(buf.wr)
        self.plan.add("sync", None, waits=toks)
        if EPILOGUE:
            self.plan.barrier()
            self.plan.add("gpsimd", "CLEAR_NONBAR")
            self.plan.barrier()
            self.plan.add("gpsimd", "CLEAR_BAR")
        self.plan.check_deadlock()


def build_phase_a(pr, io):
    nc, b = pr.nc, pr.b
    st = pr.stack

    ar = pr.arena("a")

    def A(shape, dtype=F32):
        return (ar.alloc(shape, dtype), Buf())

    wq = A([128, 8, 1024], BF16)
    wk = A([128, 8, 256], BF16)
    wv = A([128, 8, 256], BF16)
    wo2 = A([64, 16, 1024], BF16)
    w1x = A([128, 8, 512], BF16)
    w2 = A([128, 4, 64], BF16)
    idb = A([128, 128], BF16)
    idf = A([128, 128], F32)
    mask1 = A([128, 128], BF16)
    maskT = A([128, 4, 512], BF16)
    b2s = A([128, 64], F32)
    b2r = A([128, 64], F32)
    stau = A([128, 1], F32)
    b1f = A([128, 4], F32)
    beff = A([128, 4], F32)
    sib = A([64, 1024], F32)
    onesr = A([1, 128], BF16)
    onesc = A([128, 1], BF16)
    onesr64 = A([65, 128], BF16)

    kaug = A([65, 4, 4096], BF16)
    vaug = A([128, 32, 260], BF16)
    qfm = A([64, 16, 512], BF16)
    Ofm = A([64, 16, 512], BF16)
    qaug = A([65, 512], BF16)

    xt = A([128, 3, 1024], F32)
    xn_tm = A([128, 4, 1024], BF16)
    xn_fm = A([128, 8, 512], BF16)
    hfm = A([128, 4, 512], BF16)

    mval = A([128, 1], F32); mcol = A([128, 1], F32); ssum = A([128, 1], F32)
    v1 = A([128, 1], F32); lnv = A([128, 1], F32); rstd = A([128, 1], F32)
    lg = A([128, 64], F32); cur = A([128, 64], F32); meq = A([128, 64], F32)
    pex = A([128, 64], F32); m8 = A([128, 8], F32); w8 = A([128, 8], F32)
    negm0 = A([128, 1], F32); ws8 = A([128, 1], F32); rw = A([128, 1], F32)
    alp = A([128, 64], F32); alpb = A([128, 64], BF16)

    _mark = ar.mark()
    w1m = A([128, 8, 512], BF16)
    b1s = A([4, 128], F32)
    sinit = A([64, 1024], F32)
    sscale = A([64, 1], F32)
    hsbf = A([64, 1024], BF16)
    smf = A([128, 8], F32)
    smbf = A([128, 8], BF16)

    ps, pstr = pr.psum()

    def wload(dst, src_name, pat=None, **kw):
        ap, hb = io[src_name]
        src = ap.rearrange(pat, **kw) if pat else ap
        b.dma("gpsimd", lambda g, o=dst[0], s=src: g.dma_start(out=o, in_=s),
              reads=[hb], writes=[dst[1]])

    def fload(dst, src_name, pat=None, bcast=None, **kw):
        ap, hb = io[src_name]
        src = ap.rearrange(pat, **kw) if pat else ap
        if bcast:
            src = src.broadcast_to(bcast)
        b.dma("sync", lambda g, o=dst[0], s=src: g.dma_start(out=o, in_=s),
              reads=[hb], writes=[dst[1]])

    wload(wq, "wq", "(c p) q -> p c q", p=128)
    wload(wk, "wk", "(c p) q -> p c q", p=128)
    wload(wv, "wv", "(c p) q -> p c q", p=128)
    wload(wo2, "wo", "(s c) d -> c s d", c=64)
    wload(w1x, "w1xT", "(c p) q -> p c q", p=128)
    wload(w1m, "w1mT", "(c p) q -> p c q", p=128)
    wload(w2, "w2T", "(c p) q -> p c q", p=128)
    wload(idb, "ident")
    wload(mask1, "mask1")
    wload(maskT, "maskT", "a p q -> p a q")
    fload(idf, "ident")
    fload(b2r, "b2", bcast=[128, 64])
    fload(stau, "tau", bcast=[128, 1])
    fload(sinit, "slot_init")
    fload(sscale, "slot_scale")
    fload(b1s, "b1")

    b.v(lambda v: v.memset(kaug[0][64:65, :, :], 1.0), writes=[kaug[1]])
    b.v(lambda v: v.memset(
        vaug[0].rearrange("p t (k c) -> p t k c", c=65)[:, :, :, 64:65], 1.0),
        writes=[vaug[1]])
    b.v(lambda v: v.memset(onesr[0], 1.0), writes=[onesr[1]])
    b.v(lambda v: v.memset(onesc[0], 1.0), writes=[onesc[1]])
    b.v(lambda v: v.memset(onesr64[0][64:65, :], 1.0), writes=[onesr64[1]])
    b.v(lambda v: v.memset(sib[0], 0.0), writes=[sib[1]])

    b.s(lambda sc: sc.activation(stau[0], stau[0], AF.Abs),
        reads=[stau[1]], writes=[stau[1]])
    b.v(lambda v: v.tensor_scalar(stau[0], stau[0], 0.1, None, ALU.add),
        reads=[stau[1]], writes=[stau[1]])
    b.v(lambda v: v.reciprocal(stau[0], stau[0]), reads=[stau[1]], writes=[stau[1]])
    b.v(lambda v: v.tensor_scalar(b2s[0], b2r[0], stau[0], None, ALU.mult),
        reads=[b2r[1], stau[1]], writes=[b2s[1]])
    b.v(lambda v: v.tensor_scalar(sinit[0], sinit[0], sscale[0][0:64], None, ALU.mult),
        reads=[sinit[1], sscale[1]], writes=[sinit[1]])
    b.v(lambda v: v.tensor_copy(hsbf[0], sinit[0]), reads=[sinit[1]], writes=[hsbf[1]])
    for dc in range(8):
        b.t(lambda t, dc=dc: t.matmul(ps[0][0][:, dc:dc + 1],
                                      hsbf[0][0:64, dc * 128:(dc + 1) * 128],
                                      onesc[0][0:64, :], start=True, stop=True),
            reads=[hsbf[1], onesc[1]], writes=[ps[0][1]], guard=2000)
    b.v(lambda v: v.tensor_scalar(smf[0], ps[0][0][:, 0:8], 1.0 / 64, None, ALU.mult),
        reads=[ps[0][1]], writes=[smf[1]])
    b.v(lambda v: v.tensor_copy(smbf[0], smf[0]), reads=[smf[1]], writes=[smbf[1]])
    b.t(lambda t: t.transpose(ps[1][0][0:128, 0:4], b1s[0], idf[0][0:4, 0:4]),
        reads=[b1s[1], idf[1]], writes=[ps[1][1]], guard=2000)
    beffr = A([1, 512], F32)
    for dc in range(8):
        pp = ps[2 + dc % 2]
        b.t(lambda t, dc=dc, pp=pp: t.matmul(
            pp[0][0:1, :], smbf[0][:, dc:dc + 1], w1m[0][:, dc, :],
            start=True, stop=True),
            reads=[w1m[1], smbf[1]], writes=[pp[1]], guard=2000)
        if dc == 0:
            b.v(lambda v, pp=pp: v.tensor_copy(beffr[0], pp[0][0:1, :]),
                reads=[pp[1]], writes=[beffr[1]])
        else:
            b.v(lambda v, pp=pp: v.tensor_tensor(beffr[0], beffr[0], pp[0][0:1, :], ALU.add),
                reads=[pp[1], beffr[1]], writes=[beffr[1]])
    for jc in range(4):
        b.t(lambda t, jc=jc: t.transpose(
            ps[3 + jc][0][0:128, 0:1], beffr[0][0:1, jc * 128:(jc + 1) * 128],
            idf[0][0:1, 0:1]),
            reads=[beffr[1], idf[1]], writes=[ps[3 + jc][1]], guard=2000)
    for jc in range(4):
        b.v(lambda v, jc=jc: v.tensor_copy(beff[0][:, jc:jc + 1], ps[3 + jc][0][:, 0:1]),
            reads=[ps[3 + jc][1]], writes=[beff[1]])
    b.v(lambda v: v.tensor_copy(b1f[0], ps[1][0][0:128, 0:4]),
        reads=[ps[1][1]], writes=[b1f[1]])
    b.v(lambda v: v.tensor_tensor(beff[0], beff[0], b1f[0], ALU.add),
        reads=[b1f[1], beff[1]], writes=[beff[1]])
    gate = beff[1].wr
    ar.reset(_mark)

    def AG(shape, dtype=F32):
        bb = Buf()
        bb.wr = gate
        return (ar.alloc(shape, dtype), bb)

    mrowP = [AG([128, 16], F32), AG([128, 16], F32)]
    mtP = [AG([128, 65], F32), AG([128, 65], F32)]
    dscr1 = AG([128, 128], F32)
    dscrP = [dscr1, dscr1]
    ptile = [AG([128, 512], BF16) for _ in range(4)]
    qaugP = [qaug, (ar.alloc([65, 512], BF16), Buf())]
    rlbf1 = AG([65, 512], BF16)
    rlb1 = AG([64, 512], BF16)
    rlbfP = [rlbf1, rlbf1]
    rlbP = [rlb1, rlb1]
    accs = AG([128, 2, 1024], F32)
    accb = [accs[1], Buf()]
    accb[1].wr = gate
    qaugP[1][1].wr = gate
    for P in range(2):
        b.v(lambda v, P=P: v.memset(mtP[P][0], 0.0), reads=(), writes=[mtP[P][1]])

    x_ap, x_hb = io["x"]
    acc_ap, acc_hb = io["acc"]
    alpha_ap, alpha_hb = io["alpha_h"]

    xtB = [Buf(), Buf(), Buf()]

    def emit_ln_loads(s2, tbs=(0, 1, 2)):
        for tb in tbs:
            gb = 4 * s2 + tb
            b.dma("sync", lambda g, gb=gb, tb=tb: g.dma_start(
                out=xt[0][:, tb % 3, :], in_=x_ap[gb * 128:(gb + 1) * 128, :]),
                reads=[x_hb], writes=[xtB[tb % 3]])

    def ln_ops(s2, tponly=False):
        ops = []

        def pre(tb):
            def f(tb=tb):
                b.v(lambda v, tb=tb: v.reduce_sum(mval[0], xt[0][:, tb % 3, :], AX.X),
                    reads=[xtB[tb % 3]], writes=[mval[1]])
                b.v(lambda v: v.tensor_scalar(mcol[0], mval[0], 1.0 / D, None, ALU.mult),
                    reads=[mval[1]], writes=[mcol[1]])
                b.v(lambda v: v.tensor_scalar(mval[0], mcol[0], -1.0, None, ALU.mult),
                    reads=[mcol[1]], writes=[mval[1]])
                b.s(lambda sc, tb=tb: sc.activation(xn_tm[0][:, tb, :], xt[0][:, tb % 3, :],
                                                    AF.Square, bias=mval[0], accum_out=ssum[0]),
                    reads=[xtB[tb % 3], mval[1]], writes=[xn_tm[1], ssum[1]])
                b.v(lambda v: v.tensor_scalar(v1[0], ssum[0], 1.0 / D, 1e-5, ALU.mult, ALU.add),
                    reads=[ssum[1]], writes=[v1[1]])
                b.s(lambda sc: sc.activation(lnv[0], v1[0], AF.Ln),
                    reads=[v1[1]], writes=[lnv[1]])
                b.s(lambda sc: sc.activation(rstd[0], lnv[0], AF.Exp, scale=-0.5),
                    reads=[lnv[1]], writes=[rstd[1]])
                b.v(lambda v, tb=tb: v.tensor_scalar(xn_tm[0][:, tb, :], xt[0][:, tb % 3, :],
                                                     mcol[0], rstd[0], ALU.subtract, ALU.mult),
                    reads=[xtB[tb % 3], mcol[1], rstd[1]], writes=[xn_tm[1]])
            ops.append(f)

        def tpb(tb):
            def f(tb=tb):
                for dc in range(8):
                    b.t(lambda t, tb=tb, dc=dc: t.transpose(
                        pstr[0][:, dc * 128:(dc + 1) * 128],
                        xn_tm[0][:, tb, dc * 128:(dc + 1) * 128], idb[0]),
                        reads=[xn_tm[1], idb[1]], writes=[pstr[1]],
                        guard=(256 if dc == 7 else 0))
                b.v(lambda v, tb=tb: v.tensor_copy(
                    xn_fm[0][:, :, tb * 128:(tb + 1) * 128],
                    pstr[0].rearrange("p (c q) -> p c q", q=128)),
                    reads=[pstr[1]], writes=[xn_fm[1]])
            ops.append(f)

        if tponly:
            tpb(0)
            tpb(1)
            tpb(2)
            tpb(3)
        else:
            pre(0)
            ops.append(lambda: emit_ln_loads(s2, tbs=(3,)))
            pre(1)
            pre(2)
            pre(3)
        return ops

    emit_ln_loads(0)
    for f in ln_ops(0):
        f()
    for f in ln_ops(0, tponly=True):
        f()
    for s in range(NSUP):
        if s > 0:
            if LN_PRE_TOP:
                for f in ln_ops(s):
                    f()
            for f in ln_ops(s, tponly=True):
                f()
        for slot in range(16):
            pp = ps[slot % 2]
            for dc in range(8):
                b.t(lambda t, slot=slot, dc=dc, pp=pp: t.matmul(
                    pp[0][0:64, :], wq[0][:, dc, slot * 64:(slot + 1) * 64],
                    xn_fm[0][:, dc, :], start=(dc == 0), stop=(dc == 7)),
                    reads=[wq[1], xn_fm[1]], writes=[pp[1]])
            b.v(lambda v, slot=slot, pp=pp: v.tensor_scalar(
                qfm[0][0:64, slot, :], pp[0][0:64, :], 0.125, None, ALU.mult),
                reads=[pp[1]], writes=[qfm[1]])
        for kv in range(4):
            pp = ps[2 + kv % 2]
            for dc in range(8):
                b.t(lambda t, kv=kv, dc=dc, pp=pp: t.matmul(
                    pp[0][0:64, :], wk[0][:, dc, kv * 64:(kv + 1) * 64],
                    xn_fm[0][:, dc, :], start=(dc == 0), stop=(dc == 7)),
                    reads=[wk[1], xn_fm[1]], writes=[pp[1]])
            b.v(lambda v, kv=kv, s=s, pp=pp: v.tensor_copy(
                kaug[0][0:64, kv, s * 512:(s + 1) * 512], pp[0][0:64, :]),
                reads=[pp[1]], writes=[kaug[1]])
        for tb in range(4):
            pp = ps[4 + tb % 2]
            for dc in range(8):
                b.t(lambda t, tb=tb, dc=dc, pp=pp: t.matmul(
                    pp[0][:, 0:256], xn_fm[0][:, dc, tb * 128:(tb + 1) * 128],
                    wv[0][:, dc, :], start=(dc == 0), stop=(dc == 7)),
                    reads=[xn_fm[1], wv[1]], writes=[pp[1]], guard=(384 if dc == 7 else 0))
            b.v(lambda v, s=s, tb=tb, pp=pp: v.tensor_copy(
                vaug[0].rearrange("p t (k c) -> p t k c", c=65)[:, s * 4 + tb, :, 0:64],
                pp[0][:, 0:256].rearrange("p (k c) -> p k c", c=64)),
                reads=[pp[1]], writes=[vaug[1]])
        for hc in range(4):
            pp = ps[hc % 2]
            for dc in range(8):
                b.t(lambda t, hc=hc, dc=dc, pp=pp: t.matmul(
                    pp[0], w1x[0][:, dc, hc * 128:(hc + 1) * 128], xn_fm[0][:, dc, :],
                    start=(dc == 0), stop=(dc == 7)),
                    reads=[w1x[1], xn_fm[1]], writes=[pp[1]])
            b.s(lambda sc, hc=hc, pp=pp: sc.activation(
                hfm[0][:, hc, :], pp[0], AF.Gelu, bias=beff[0][:, hc:hc + 1]),
                reads=[pp[1], beff[1]], writes=[hfm[1]])
        for tb in range(4):
            gb = 4 * s + tb
            pp = ps[2 + tb % 2]
            for hc in range(4):
                b.t(lambda t, tb=tb, hc=hc, pp=pp: t.matmul(
                    pp[0][:, 0:64], hfm[0][:, hc, tb * 128:(tb + 1) * 128],
                    w2[0][:, hc, :], start=(hc == 0), stop=(hc == 3)),
                    reads=[hfm[1], w2[1]], writes=[pp[1]], guard=(384 if hc == 3 else 0))
            b.v(lambda v, pp=pp: v.scalar_tensor_tensor(
                lg[0], pp[0][:, 0:64], stau[0], b2s[0], ALU.mult, ALU.add),
                reads=[pp[1], stau[1], b2s[1]], writes=[lg[1]])
            b.v(lambda v: v.max(m8[0], lg[0]), reads=[lg[1]], writes=[m8[1]])
            b.v(lambda v: v.match_replace(cur[0], m8[0], lg[0], NEG),
                reads=[m8[1], lg[1]], writes=[cur[1]])
            b.v(lambda v: v.tensor_scalar(negm0[0], m8[0][:, 0:1], -1.0, None, ALU.mult),
                reads=[m8[1]], writes=[negm0[1]])
            b.s(lambda sc: sc.activation(w8[0], m8[0], AF.Exp, bias=negm0[0], accum_out=ws8[0]),
                reads=[m8[1], negm0[1]], writes=[w8[1], ws8[1]])
            b.s(lambda sc: sc.activation(pex[0], lg[0], AF.Exp, bias=negm0[0]),
                reads=[lg[1], negm0[1]], writes=[pex[1]])
            b.v(lambda v: v.tensor_scalar(meq[0], cur[0], NEG, None, ALU.is_equal),
                reads=[cur[1]], writes=[meq[1]])
            b.v(lambda v: v.reciprocal(rw[0], ws8[0]), reads=[ws8[1]], writes=[rw[1]])
            b.v(lambda v: v.tensor_tensor(alp[0], pex[0], meq[0], ALU.mult),
                reads=[pex[1], meq[1]], writes=[alp[1]])
            b.v(lambda v: v.tensor_scalar(alp[0], alp[0], rw[0], None, ALU.mult),
                reads=[alp[1], rw[1]], writes=[alp[1]])
            b.v(lambda v: v.tensor_copy(alpb[0], alp[0]), reads=[alp[1]], writes=[alpb[1]])
            b.dma("sync", lambda g, gb=gb: g.dma_start(
                out=alpha_ap[gb * 128:(gb + 1) * 128, :], in_=alp[0]),
                reads=[alp[1]], writes=[alpha_hb])
            if s == 0 and tb == 0 and "dbg" in io:
                dbg_ap, dbg_hb = io["dbg"]
                b.dma("sync", lambda g: g.dma_start(out=dbg_ap[:, 0:64], in_=lg[0]),
                      reads=[lg[1]], writes=[dbg_hb])
                b.dma("sync", lambda g: g.dma_start(out=dbg_ap[:, 64:72], in_=m8[0]),
                      reads=[m8[1]], writes=[dbg_hb])
                b.dma("sync", lambda g: g.dma_start(out=dbg_ap[:, 72:76], in_=beff[0]),
                      reads=[beff[1]], writes=[dbg_hb])
                b.v(lambda v: v.tensor_copy(mrowP[0][0][:, 8:9], stau[0]),
                    reads=[stau[1]], writes=[mrowP[0][1]])
                b.v(lambda v: v.tensor_copy(mrowP[0][0][:, 9:10], ws8[0]),
                    reads=[ws8[1]], writes=[mrowP[0][1]])
                b.dma("sync", lambda g: g.dma_start(out=dbg_ap[:, 76:78], in_=mrowP[0][0][:, 8:10]),
                      reads=[mrowP[0][1]], writes=[dbg_hb])
                b.dma("sync", lambda g: g.dma_start(out=dbg_ap[:, 78:142], in_=b2s[0]),
                      reads=[b2s[1]], writes=[dbg_hb])
                b.dma("sync", lambda g: g.dma_start(out=dbg_ap[:, 142:206], in_=meq[0]),
                      reads=[meq[1]], writes=[dbg_hb])
                b.dma("gpsimd", lambda g: g.dma_start(out=dbg_ap[:, 206:334], in_=xn_tm[0][:, 0, 0:128]),
                      reads=[xn_tm[1]], writes=[dbg_hb])
                b.dma("sync", lambda g: g.dma_start(out=dbg_ap[:, 590:598], in_=smf[0]),
                      reads=[smf[1]], writes=[dbg_hb])
                b.dma("sync", lambda g: g.dma_start(out=dbg_ap[:, 598:602], in_=b1f[0]),
                      reads=[b1f[1]], writes=[dbg_hb])
                b.dma("gpsimd", lambda g: g.dma_start(out=dbg_ap[:, 334:462], in_=hfm[0][:, 0, 0:128]),
                      reads=[hfm[1]], writes=[dbg_hb])
                b.dma("gpsimd", lambda g: g.dma_start(out=dbg_ap[:, 462:590], in_=xn_fm[0][:, 0, 0:128]),
                      reads=[xn_fm[1]], writes=[dbg_hb])
            for half in range(2):
                b.t(lambda t, tb=tb, half=half: t.matmul(
                    ps[5 + half][0][0:64, :], alpb[0],
                    xn_tm[0][:, tb, half * 512:(half + 1) * 512],
                    start=(tb == 0), stop=(tb == 3)),
                    reads=[alpb[1], xn_tm[1]], writes=[ps[5 + half][1]],
                    guard=(256 if tb == 3 else 0))
        for half in range(2):
            b.v(lambda v, half=half: v.tensor_tensor(
                sib[0][:, half * 512:(half + 1) * 512],
                sib[0][:, half * 512:(half + 1) * 512],
                ps[5 + half][0][0:64, :], ALU.add),
                reads=[ps[5 + half][1], sib[1]], writes=[sib[1]])

        def pass1_ops(slot, s=s):
            P = slot % 2
            kv = KVH[slot]
            mrow, qg, dscr = mrowP[P], qaugP[P], dscrP[P]
            rot = [2 * P]
            ops = []

            def mmtile(tb, ci):
                pp = ps[2 * P + rot[0] % 2]
                rot[0] += 1

                def f(tb=tb, ci=ci, pp=pp):
                    b.t(lambda t, slot=slot, tb=tb, ci=ci, kv=kv, pp=pp: t.matmul(
                        pp[0], qfm[0][0:64, slot, tb * 128:(tb + 1) * 128],
                        kaug[0][0:64, kv, ci * 512:(ci + 1) * 512], start=True, stop=True),
                        reads=[qfm[1], kaug[1]], writes=[pp[1]])
                    b.v(lambda v, ci=ci, pp=pp, mrow=mrow: v.reduce_max(
                        mrow[0][:, ci:ci + 1], pp[0], AX.X),
                        reads=[pp[1]], writes=[mrow[1]])
                ops.append(f)

            def mmdiag(tb, ncol):
                pp = ps[2 * P + rot[0] % 2]
                rot[0] += 1
                mt = mtP[tb % 2]

                def f(tb=tb, ncol=ncol, pp=pp, mt=mt):
                    w = (tb + 1) * 128
                    nc2 = ncol
                    b.t(lambda t, slot=slot, tb=tb, s=s, kv=kv, w=w, pp=pp: t.matmul(
                        pp[0][:, 0:w], qfm[0][0:64, slot, tb * 128:(tb + 1) * 128],
                        kaug[0][0:64, kv, s * 512:s * 512 + w], start=True, stop=True),
                        reads=[qfm[1], kaug[1]], writes=[pp[1]])
                    if tb > 0:
                        b.v(lambda v, nc2=nc2, tb=tb, pp=pp, mrow=mrow: v.reduce_max(
                            mrow[0][:, nc2:nc2 + 1], pp[0][:, 0:tb * 128], AX.X),
                            reads=[pp[1]], writes=[mrow[1]])
                        nc2 += 1
                    b.v(lambda v, tb=tb, w=w, pp=pp, dscr=dscr: v.tensor_tensor(
                        dscr[0], pp[0][:, tb * 128:w], mask1[0], ALU.add),
                        reads=[pp[1], mask1[1]], writes=[dscr[1]])
                    b.v(lambda v, nc2=nc2, mrow=mrow, dscr=dscr: v.reduce_max(
                        mrow[0][:, nc2:nc2 + 1], dscr[0], AX.X),
                        reads=[dscr[1]], writes=[mrow[1]])
                    b.v(lambda v, nc2=nc2, mrow=mrow, mt=mt: v.reduce_max(
                        mt[0][:, 64:65], mrow[0][:, 0:nc2 + 1], AX.X, negate=True),
                        reads=[mrow[1]], writes=[mt[1]])
                ops.append(f)

            def tp(tb):
                mt = mtP[tb % 2]

                def f(tb=tb, mt=mt):
                    pt = ps[4]
                    b.t(lambda t, pt=pt, mt=mt: t.transpose(pt[0][0:65, 0:128], mt[0], idf[0]),
                        reads=[mt[1], idf[1]], writes=[pt[1]])
                    b.s(lambda sc, tb=tb, pt=pt, qg=qg: sc.copy(
                        qg[0][64:65, tb * 128:(tb + 1) * 128], pt[0][64:65, 0:128]),
                        reads=[pt[1]], writes=[qg[1]])
                ops.append(f)

            for tb in range(4):
                for ci in range(s):
                    mmtile(tb, ci)
                mmdiag(tb, s)
                if tb >= 1:
                    tp(tb - 1)
            tp(3)

            def fq():
                b.v(lambda v, slot=slot, qg=qg: v.tensor_copy(
                    qg[0][0:64, :], qfm[0][0:64, slot, :]),
                    reads=[qfm[1]], writes=[qg[1]])
            ops.append(fq)
            return ops

        def pass2_ops(slot, s=s):
            P = slot % 2
            kv = KVH[slot]
            qg = qaugP[P]
            av = ps[6 - P]
            nblk = 4 * (s + 1)
            ops = []

            def st(gtk):
                def f(gtk=gtk):
                    pp = ps[2 * P + gtk % 2]
                    b.t(lambda t, gtk=gtk, kv=kv, pp=pp, qg=qg: t.matmul(
                        pp[0], kaug[0][:, kv, gtk * 128:(gtk + 1) * 128], qg[0],
                        start=True, stop=True),
                        reads=[kaug[1], qg[1]], writes=[pp[1]])
                    if gtk >= 4 * s:
                        pos = gtk - 4 * s
                        b.v(lambda v, pos=pos, pp=pp: v.tensor_tensor(
                            pp[0], pp[0], maskT[0][:, pos, :], ALU.add),
                            reads=[pp[1], maskT[1]], writes=[pp[1]])
                    pt = ptile[2 * P + gtk % 2]
                    b.s(lambda sc, pp=pp, pt=pt: sc.activation(pt[0], pp[0], AF.Exp),
                        reads=[pp[1]], writes=[pt[1]])
                ops.append(f)

            def avm(gtk):
                def f(gtk=gtk):
                    pt = ptile[2 * P + gtk % 2]
                    b.t(lambda t, gtk=gtk, kv=kv, nblk=nblk, pt=pt, av=av: t.matmul(
                        av[0][0:65, :], vaug[0][:, gtk, kv * 65:(kv + 1) * 65], pt[0],
                        start=(gtk == 0), stop=(gtk == nblk - 1)),
                        reads=[vaug[1], pt[1]], writes=[av[1]])
                ops.append(f)

            for gtk in range(nblk):
                st(gtk)
                if gtk >= 1:
                    avm(gtk - 1)
            avm(nblk - 1)
            return ops

        def weave(main, extra):
            if not extra:
                for f in main:
                    f()
                return
            n, m = len(main), len(extra)
            j = 0
            for i, f in enumerate(main):
                f()
                while j * n <= (i + 1) * m and j < m:
                    extra[j]()
                    j += 1
            while j < m:
                extra[j]()
                j += 1

        def emit_epi(slot, s=s):
            P = slot % 2
            av = ps[6 - P]
            rlbf, rlb = rlbfP[P], rlbP[P]
            b.s(lambda sc, av=av, rlbf=rlbf: sc.copy(
                rlbf[0][64:65, :], av[0][64:65, :]),
                reads=[av[1]], writes=[rlbf[1]])
            b.t(lambda t, rlbf=rlbf: t.matmul(ps[4][0][0:64, :], onesr64[0][64:65, 0:64],
                                              rlbf[0][64:65, :], start=True, stop=True),
                reads=[onesr64[1], rlbf[1]], writes=[ps[4][1]])

            def _recip(v, rlb=rlb):
                with nc.allow_low_precision(reason="1/l feeds bf16 mult"):
                    return v.reciprocal(rlb[0], ps[4][0][0:64, :])
            b.v(_recip, reads=[ps[4][1]], writes=[rlb[1]])
            b.v(lambda v, slot=slot, av=av, rlb=rlb: v.tensor_tensor(
                Ofm[0][0:64, slot, :], av[0][0:64, :], rlb[0], ALU.mult),
                reads=[av[1], rlb[1]], writes=[Ofm[1]])
            if s == 0 and slot == 0 and "dbg2" in io:
                d2_ap, d2_hb = io["dbg2"]
                b.dma("gpsimd", lambda g: g.dma_start(out=d2_ap[0:64, 0:512], in_=Ofm[0][0:64, 0, :]),
                      reads=[Ofm[1]], writes=[d2_hb])
                b.dma("gpsimd", lambda g, rlbf=rlbf: g.dma_start(out=d2_ap[0:65, 512:1024], in_=rlbf[0]),
                      reads=[rlbf[1]], writes=[d2_hb])
                b.dma("gpsimd", lambda g: g.dma_start(out=d2_ap[0:65, 1024:1536], in_=qaugP[0][0]),
                      reads=[qaugP[0][1]], writes=[d2_hb])
                b.dma("gpsimd", lambda g: g.dma_start(out=d2_ap[0:64, 1536:2048], in_=kaug[0][0:64, 0, 0:512]),
                      reads=[kaug[1]], writes=[d2_hb])
                b.dma("gpsimd", lambda g: g.dma_start(out=d2_ap[64:128, 0:512], in_=qfm[0][0:64, 0, :]),
                      reads=[qfm[1]], writes=[d2_hb])
                b.dma("gpsimd", lambda g: g.dma_start(
                    out=d2_ap[0:64, 2241:2305], in_=vaug[0][0:64, 0, 0:64]),
                      reads=[vaug[1]], writes=[d2_hb])

        if s + 1 < NSUP:
            emit_ln_loads(s + 1)
        for f in pass1_ops(0):
            f()
        pend = None
        for slot in range(16):
            p2 = pass2_ops(slot)
            extra = []
            if pend is not None:
                extra.append(pend)
            if slot < 15:
                extra.extend(pass1_ops(slot + 1))
            weave(p2, extra)
            if EPI_DELAY:
                pend = (lambda sl=slot: emit_epi(sl))
            else:
                emit_epi(slot)
        if pend is not None:
            pend()
        if (not LN_PRE_TOP) and s + 1 < NSUP:
            for f in ln_ops(s + 1):
                f()
        for tb in range(4):
            gb = 4 * s + tb
            b.dma("sync", lambda g, gb=gb, tb=tb: g.dma_start(
                out=accs[0][:, tb % 2, :], in_=x_ap[gb * 128:(gb + 1) * 128, :]),
                reads=[x_hb], writes=[accb[tb % 2]])
            for oc in range(2):
                pp = ps[(tb % 2) * 2 + oc]
                for slot in range(16):
                    b.t(lambda t, tb=tb, oc=oc, slot=slot, pp=pp: t.matmul(
                        pp[0], Ofm[0][0:64, slot, tb * 128:(tb + 1) * 128],
                        wo2[0][0:64, slot, oc * 512:(oc + 1) * 512],
                        start=(slot == 0), stop=(slot == 15)),
                        reads=[Ofm[1], wo2[1]], writes=[pp[1]])
                b.v(lambda v, tb=tb, oc=oc, pp=pp: v.tensor_tensor(
                    accs[0][:, tb % 2, oc * 512:(oc + 1) * 512],
                    accs[0][:, tb % 2, oc * 512:(oc + 1) * 512],
                    pp[0], ALU.add),
                    reads=[pp[1], accb[tb % 2]], writes=[accb[tb % 2]])
            b.dma("sync", lambda g, gb=gb, tb=tb: g.dma_start(
                out=acc_ap[gb * 128:(gb + 1) * 128, :], in_=accs[0][:, tb % 2, :]),
                reads=[accb[tb % 2]], writes=[acc_hb])

    si_ap, si_hb = io["slotin_h"]
    b.dma("sync", lambda g: g.dma_start(out=si_ap, in_=sib[0]),
          reads=[sib[1]], writes=[si_hb])
    return ar.off


LN_INLINE = False
EPI_DELAY = True
LN_PRE_TOP = True


WN_A = ["wq", "wk", "wv", "wo", "w1xT", "w1mT", "b1", "w2T", "b2", "tau",
        "slot_init", "slot_scale", "ident", "mask1", "maskT"]


def make_prog_a(debug=False):
    pr = Prog()
    w_shapes = {
        "wq": [1024, 1024], "wk": [1024, 256], "wv": [1024, 256], "wo": [1024, 1024],
        "w1xT": [1024, 512], "w1mT": [1024, 512], "b1": [4, 128], "w2T": [512, 64],
        "b2": [1, 64], "tau": [1, 1], "slot_init": [64, 1024], "slot_scale": [64, 1],
        "ident": [128, 128], "mask1": [128, 128], "maskT": [4, 128, 512],
    }
    for n in WN_A:
        pr.dram_in(n, w_shapes[n])
    pr.dram_in("x", [T, D])
    pr.dram_out("acc", [T, D])
    pr.dram_out("alpha_h", [T, 64])
    pr.dram_out("slotin_h", [64, 1024])
    if debug:
        pr.dram_out("dbg", [128, 640])
        pr.dram_out("dbg2", [128, 2816])
    build_phase_a(pr, pr.hbm)
    pr.finish()
    emit_block(pr.nc, pr.plan, pr.stack)
    return pr


def run_launch(pr, in_maps, core_ids=None):
    if core_ids is None:
        core_ids = list(range(len(in_maps)))
    res = run_bass_kernel_spmd(pr.nc, in_maps, core_ids=core_ids)
    return res.results


def build_phase_b(pr, io):
    nc, b = pr.nc, pr.b
    st = pr.stack
    ar = pr.arena("b")

    def A(shape, dtype=F32):
        return (ar.alloc(shape, dtype), Buf())

    wih = A([128, 8, 3072], BF16)
    whh = A([128, 8, 3072], BF16)
    wv0 = A([128, 8, 1024], BF16)
    wo0 = A([128, 8, 1024], BF16)
    bih = A([1, 3072], BF16)
    bhh = A([1, 3072], BF16)
    bv0 = A([1, 1024], BF16)
    bo0 = A([1, 1024], BF16)
    idb = A([128, 128], BF16)
    onesc = A([128, 1], BF16)
    onesr = A([1, 128], BF16)
    sinit = A([64, 1024], F32)
    sscale = A([64, 1], F32)
    hs = A([64, 1024], F32)
    hsbf = A([64, 1024], BF16)
    slotin = A([64, 1024], F32)
    at = A([128, 64], BF16)
    alphaT = A([64, 4096], BF16)
    wsum = A([64, 1], F32)
    gi = A([64, 1024], F32)
    gibf = A([64, 1024], BF16)
    giT = A([128, 8, 64], BF16)
    hsT = A([128, 8, 64], BF16)
    snT = A([128, 8, 64], BF16)
    sv = A([64, 1024], BF16)
    _markb = ar.mark()
    gxp = A([64, 1024], F32)
    ghp = A([64, 1024], F32)
    r_t = A([64, 1024], F32)
    z_t = A([64, 1024], F32)
    n_t = A([64, 1024], F32)

    ps, pstr = pr.psum()

    def wload(dst, src_name, pat=None, **kw):
        ap, hb = io[src_name]
        src = ap.rearrange(pat, **kw) if pat else ap
        b.dma("gpsimd", lambda g, o=dst[0], s=src: g.dma_start(out=o, in_=s),
              reads=[hb], writes=[dst[1]])

    def fload(dst, src_name, pat=None, bcast=None, sl=None, **kw):
        ap, hb = io[src_name]
        src = ap.rearrange(pat, **kw) if pat else ap
        if sl is not None:
            src = src[sl]
        if bcast:
            src = src.broadcast_to(bcast)
        b.dma("sync", lambda g, o=dst[0], s=src: g.dma_start(out=o, in_=s),
              reads=[hb], writes=[dst[1]])

    wload(wih, "wihT", "(c p) q -> p c q", p=128)
    wload(whh, "whhT", "(c p) q -> p c q", p=128)
    wload(wv0, "wv0T", "(c p) q -> p c q", p=128)
    wload(wo0, "wo0T", "(c p) q -> p c q", p=128)
    wload(bih, "bih"); wload(bhh, "bhh"); wload(bv0, "bv0"); wload(bo0, "bo0")
    wload(idb, "ident")
    fload(sinit, "slot_init"); fload(sscale, "slot_scale")
    fload(slotin, "slotin_h")
    b.v(lambda v: v.memset(onesc[0], 1.0), writes=[onesc[1]])
    b.v(lambda v: v.memset(onesr[0], 1.0), writes=[onesr[1]])

    alpha_ap, alpha_hb = io["alpha_h"]
    acc_ap, acc_hb = io["acc"]
    x1_ap, x1_hb = io["x1"]

    for i in range(NBLK):
        b.dma("gpsimd", lambda g, i=i: g.dma_start(
            out=at[0], in_=alpha_ap[i * 128:(i + 1) * 128, :]),
            reads=[alpha_hb], writes=[at[1]])
        b.t(lambda t, i=i: t.matmul(ps[0][0][0:64, 0:1], at[0], onesc[0],
                                    start=True, stop=True),
            reads=[at[1], onesc[1]], writes=[ps[0][1]], guard=512)
        if i == 0:
            b.v(lambda v: v.tensor_copy(wsum[0], ps[0][0][0:64, 0:1]),
                reads=[ps[0][1]], writes=[wsum[1]])
        else:
            b.v(lambda v: v.tensor_tensor(wsum[0], wsum[0], ps[0][0][0:64, 0:1], ALU.add),
                reads=[ps[0][1], wsum[1]], writes=[wsum[1]])
        b.t(lambda t, i=i: t.transpose(pstr[0][0:64, 0:128], at[0], idb[0]),
            reads=[at[1], idb[1]], writes=[pstr[1]], guard=384)
        b.v(lambda v, i=i: v.tensor_copy(alphaT[0][0:64, i * 128:(i + 1) * 128],
                                         pstr[0][0:64, 0:128]),
            reads=[pstr[1]], writes=[alphaT[1]])
    b.v(lambda v: v.tensor_scalar(wsum[0], wsum[0], 1e-8, None, ALU.add),
        reads=[wsum[1]], writes=[wsum[1]])
    b.v(lambda v: v.reciprocal(wsum[0], wsum[0]), reads=[wsum[1]], writes=[wsum[1]])
    b.v(lambda v: v.tensor_scalar(gi[0], slotin[0], wsum[0][0:64], None, ALU.mult),
        reads=[slotin[1], wsum[1]], writes=[gi[1]])
    b.v(lambda v: v.tensor_copy(gibf[0], gi[0]), reads=[gi[1]], writes=[gibf[1]])
    b.v(lambda v: v.tensor_scalar(hs[0], sinit[0], sscale[0][0:64], None, ALU.mult),
        reads=[sinit[1], sscale[1]], writes=[hs[1]])
    b.v(lambda v: v.tensor_copy(hsbf[0], hs[0]), reads=[hs[1]], writes=[hsbf[1]])
    for dc in range(8):
        b.t(lambda t, dc=dc: t.transpose(pstr[0][:, 0:64], gibf[0][0:64, dc * 128:(dc + 1) * 128],
                                         idb[0][0:64, 0:64]),
            reads=[gibf[1], idb[1]], writes=[pstr[1]], guard=384)
        b.v(lambda v, dc=dc: v.tensor_copy(giT[0][:, dc, :], pstr[0][:, 0:64]),
            reads=[pstr[1]], writes=[giT[1]])
        b.t(lambda t, dc=dc: t.transpose(pstr[0][:, 64:128], hsbf[0][0:64, dc * 128:(dc + 1) * 128],
                                         idb[0][0:64, 0:64]),
            reads=[hsbf[1], idb[1]], writes=[pstr[1]], guard=384)
        b.v(lambda v, dc=dc: v.tensor_copy(hsT[0][:, dc, :], pstr[0][:, 64:128]),
            reads=[pstr[1]], writes=[hsT[1]])
    for part, dst in ((0, r_t), (1, z_t), (2, n_t)):
        for which, src2, wmat, bias, acc2 in ((0, giT, wih, bih, gxp), (1, hsT, whh, bhh, ghp)):
            for nch in range(2):
                pp = ps[nch % 2]
                off = part * 1024 + nch * 512
                b.t(lambda t, off=off, bias=bias, pp=pp: t.matmul(
                    pp[0][0:64, :], onesr[0][0:1, 0:64], bias[0][0:1, off:off + 512],
                    start=True, stop=False),
                    reads=[onesr[1], bias[1]], writes=[pp[1]])
                for dc in range(8):
                    b.t(lambda t, off=off, dc=dc, src2=src2, wmat=wmat, pp=pp: t.matmul(
                        pp[0][0:64, :], src2[0][:, dc, :], wmat[0][:, dc, off:off + 512],
                        start=False, stop=(dc == 7)),
                        reads=[src2[1], wmat[1]], writes=[pp[1]])
                b.v(lambda v, nch=nch, acc2=acc2, pp=pp: v.tensor_copy(
                    acc2[0][:, nch * 512:(nch + 1) * 512], pp[0][0:64, :]),
                    reads=[pp[1]], writes=[acc2[1]])
        if part == 2:
            b.v(lambda v, dst=dst: v.tensor_tensor(dst[0], r_t[0], ghp[0], ALU.mult),
                reads=[r_t[1], ghp[1]], writes=[dst[1]])
            b.v(lambda v, dst=dst: v.tensor_tensor(dst[0], dst[0], gxp[0], ALU.add),
                reads=[dst[1], gxp[1]], writes=[dst[1]])
            b.s(lambda sc, dst=dst: sc.activation(dst[0], dst[0], AF.Tanh),
                reads=[dst[1]], writes=[dst[1]])
        else:
            b.v(lambda v, dst=dst: v.tensor_tensor(dst[0], gxp[0], ghp[0], ALU.add),
                reads=[gxp[1], ghp[1]], writes=[dst[1]])
            b.s(lambda sc, dst=dst: sc.activation(dst[0], dst[0], AF.Sigmoid),
                reads=[dst[1]], writes=[dst[1]])
    b.v(lambda v: v.tensor_tensor(hs[0], hs[0], n_t[0], ALU.subtract),
        reads=[hs[1], n_t[1]], writes=[hs[1]])
    b.v(lambda v: v.tensor_tensor(hs[0], hs[0], z_t[0], ALU.mult),
        reads=[hs[1], z_t[1]], writes=[hs[1]])
    b.v(lambda v: v.tensor_tensor(hs[0], hs[0], n_t[0], ALU.add),
        reads=[hs[1], n_t[1]], writes=[hs[1]])
    b.v(lambda v: v.tensor_copy(gibf[0], hs[0]), reads=[hs[1]], writes=[gibf[1]])
    for dc in range(8):
        b.t(lambda t, dc=dc: t.transpose(pstr[0][:, 0:64], gibf[0][0:64, dc * 128:(dc + 1) * 128],
                                         idb[0][0:64, 0:64]),
            reads=[gibf[1], idb[1]], writes=[pstr[1]], guard=384)
        b.v(lambda v, dc=dc: v.tensor_copy(snT[0][:, dc, :], pstr[0][:, 0:64]),
            reads=[pstr[1]], writes=[snT[1]])
    gateb = gibf[1].wr
    ar.reset(_markb)

    def AGB(shape, dtype=F32):
        bb = Buf(); bb.wr = gateb
        return (ar.alloc(shape, dtype), bb)

    f0 = AGB([128, 8, 512], BF16)
    acct = AGB([128, 1, 1024], F32)
    accb = [acct[1], acct[1]]
    for oc in range(2):
        pp = ps[2 + oc]
        b.t(lambda t, oc=oc, pp=pp: t.matmul(
            pp[0][0:64, :], onesr[0][0:1, 0:64], bv0[0][0:1, oc * 512:(oc + 1) * 512],
            start=True, stop=False),
            reads=[onesr[1], bv0[1]], writes=[pp[1]])
        for dc in range(8):
            b.t(lambda t, oc=oc, dc=dc, pp=pp: t.matmul(
                pp[0][0:64, :], snT[0][:, dc, :], wv0[0][:, dc, oc * 512:(oc + 1) * 512],
                start=False, stop=(dc == 7)),
                reads=[snT[1], wv0[1]], writes=[pp[1]])
        b.v(lambda v, oc=oc, pp=pp: v.tensor_copy(sv[0][:, oc * 512:(oc + 1) * 512], pp[0][0:64, :]),
            reads=[pp[1]], writes=[sv[1]])
    for s in range(NSUP):
        for dc in range(8):
            pp = ps[dc % 2]
            b.t(lambda t, s=s, dc=dc, pp=pp: t.matmul(
                pp[0], sv[0][:, dc * 128:(dc + 1) * 128], alphaT[0][0:64, s * 512:(s + 1) * 512],
                start=True, stop=True),
                reads=[sv[1], alphaT[1]], writes=[pp[1]])
            b.v(lambda v, s=s, dc=dc, pp=pp: v.tensor_copy(f0[0][:, dc, :], pp[0]),
                reads=[pp[1]], writes=[f0[1]])
        for tb in range(4):
            gb = 4 * s + tb
            b.dma("sync", lambda g, gb=gb, tb=tb: g.dma_start(
                out=acct[0][:, 0, :], in_=acc_ap[gb * 128:(gb + 1) * 128, :]),
                reads=[acc_hb], writes=[accb[0]])
            for oc in range(2):
                pp = ps[2 + oc]
                b.t(lambda t, oc=oc, pp=pp: t.matmul(
                    pp[0], onesr[0], bo0[0][0:1, oc * 512:(oc + 1) * 512],
                    start=True, stop=False),
                    reads=[onesr[1], bo0[1]], writes=[pp[1]])
                for dc in range(8):
                    b.t(lambda t, tb=tb, oc=oc, dc=dc, pp=pp: t.matmul(
                        pp[0], f0[0][:, dc, tb * 128:(tb + 1) * 128],
                        wo0[0][:, dc, oc * 512:(oc + 1) * 512],
                        start=False, stop=(dc == 7)),
                        reads=[f0[1], wo0[1]], writes=[pp[1]])
                b.v(lambda v, tb=tb, oc=oc, pp=pp: v.tensor_tensor(
                    acct[0][:, 0, oc * 512:(oc + 1) * 512],
                    acct[0][:, 0, oc * 512:(oc + 1) * 512], pp[0], ALU.add),
                    reads=[pp[1], accb[0]], writes=[accb[0]])
            b.dma("sync", lambda g, gb=gb, tb=tb: g.dma_start(
                out=x1_ap[gb * 128:(gb + 1) * 128, :], in_=acct[0][:, 0, :]),
                reads=[accb[0]], writes=[x1_hb])
    assert ar.off <= 212200, ar.off


WN_B = ["wihT", "whhT", "bih", "bhh", "wv0T", "bv0", "wo0T", "bo0",
        "slot_init", "slot_scale", "ident"]


def make_prog_b():
    pr = Prog()
    shp = {"wihT": [1024, 3072], "whhT": [1024, 3072], "bih": [1, 3072], "bhh": [1, 3072],
           "wv0T": [1024, 1024], "bv0": [1, 1024], "wo0T": [1024, 1024], "bo0": [1, 1024],
           "slot_init": [64, 1024], "slot_scale": [64, 1], "ident": [128, 128]}
    for n in WN_B:
        pr.dram_in(n, shp[n])
    pr.dram_in("alpha_h", [T, 64])
    pr.dram_in("slotin_h", [64, 1024])
    pr.dram_in("acc", [T, D])
    pr.dram_out("x1", [T, D])
    build_phase_b(pr, pr.hbm)
    pr.finish()
    emit_block(pr.nc, pr.plan, pr.stack)
    return pr


def build_phase_c(pr, io):
    nc, b = pr.nc, pr.b
    st = pr.stack
    ar = pr.arena("c")

    def A(shape, dtype=F32):
        return (ar.alloc(shape, dtype), Buf())

    inpw = A([128, 8, 4096], BF16)
    xproj = A([128, 16, 128], BF16)
    convw = A([128, 16, 4], F32)
    convb16 = A([16, 128], F32)
    convbf = A([128, 16], F32)
    gnb = A([128, 1024], F32)
    bnb = A([128, 1024], F32)
    idb = A([128, 128], BF16)
    idf = A([128, 128], F32)
    xs = A([128, 16, 512], F32)
    tail = A([128, 16, 3], F32)
    xe = A([128, 515], F32)
    cacc = A([128, 512], F32)
    xcbP = [A([128, 512], BF16), A([128, 512], BF16)]
    szb = A([128, 512], BF16)
    BCsb = A([64, 4096], BF16)
    dtlowb = A([64, 4096], BF16)
    x1t = A([128, 2, 1024], F32)
    xn1 = A([128, 1024], F32)
    xlnt = A([128, 1024], F32)
    xln_tm = A([128, 4, 1024], BF16)
    xln_fm = A([128, 8, 512], BF16)
    acc2 = A([128, 1024], F32)
    mval = A([128, 1], F32); mcol = A([128, 1], F32); ssum = A([128, 1], F32)
    v1 = A([128, 1], F32); lnv = A([128, 1], F32); rstd = A([128, 1], F32)

    ps, pstr = pr.psum()

    def wload(dst, src_name, pat=None, **kw):
        ap, hb = io[src_name]
        src = ap.rearrange(pat, **kw) if pat else ap
        b.dma("gpsimd", lambda g, o=dst[0], s=src: g.dma_start(out=o, in_=s),
              reads=[hb], writes=[dst[1]])

    def fload(dst, src_name, pat=None, bcast=None, **kw):
        ap, hb = io[src_name]
        src = ap.rearrange(pat, **kw) if pat else ap
        if bcast:
            src = src.broadcast_to(bcast)
        b.dma("sync", lambda g, o=dst[0], s=src: g.dma_start(out=o, in_=s),
              reads=[hb], writes=[dst[1]])

    wload(inpw, "inpT", "(c p) q -> p c q", p=128)
    wload(xproj, "xprojTp", "(c p) q -> p c q", p=128)
    wload(idb, "ident")
    fload(idf, "ident")
    fload(convw, "convw", "(c p) j -> p c j", p=128)
    fload(convb16, "convb")
    fload(gnb, "lng", bcast=[128, 1024])
    fload(bnb, "lnb", bcast=[128, 1024])
    b.t(lambda t: t.transpose(ps[0][0][0:128, 0:16], convb16[0], idf[0][0:16, 0:16]),
        reads=[convb16[1], idf[1]], writes=[ps[0][1]], guard=512)
    b.v(lambda v: v.tensor_copy(convbf[0], ps[0][0][:, 0:16]),
        reads=[ps[0][1]], writes=[convbf[1]])
    b.v(lambda v: v.memset(tail[0], 0.0), writes=[tail[1]])

    x1_ap, x1_hb = io["x1"]
    xn1_ap, xn1_hb = io["xn1_h"]
    accb_ap, accb_hb = io["accb_h"]
    xc_ap, xc_hb = io["xc_h"]
    sz_ap, sz_hb = io["sz_h"]
    bc_ap, bc_hb = io["BC_h"]
    dl_ap, dl_hb = io["dtlow_h"]

    for s in range(NSUP):
        for tb in range(4):
            gb = 4 * s + tb
            b.dma("sync", lambda g, gb=gb, tb=tb: g.dma_start(
                out=x1t[0][:, tb % 2, :], in_=x1_ap[gb * 128:(gb + 1) * 128, :]),
                reads=[x1_hb], writes=[x1t[1]])
            b.v(lambda v, tb=tb: v.reduce_sum(mval[0], x1t[0][:, tb % 2, :], AX.X),
                reads=[x1t[1]], writes=[mval[1]])
            b.v(lambda v: v.tensor_scalar(mcol[0], mval[0], 1.0 / D, None, ALU.mult),
                reads=[mval[1]], writes=[mcol[1]])
            b.v(lambda v: v.tensor_scalar(mval[0], mcol[0], -1.0, None, ALU.mult),
                reads=[mcol[1]], writes=[mval[1]])
            b.s(lambda sc, tb=tb: sc.activation(xlnt[0], x1t[0][:, tb % 2, :],
                                                AF.Square, bias=mval[0], accum_out=ssum[0]),
                reads=[x1t[1], mval[1]], writes=[xlnt[1], ssum[1]])
            b.v(lambda v: v.tensor_scalar(v1[0], ssum[0], 1.0 / D, 1e-5, ALU.mult, ALU.add),
                reads=[ssum[1]], writes=[v1[1]])
            b.s(lambda sc: sc.activation(lnv[0], v1[0], AF.Ln), reads=[v1[1]], writes=[lnv[1]])
            b.s(lambda sc: sc.activation(rstd[0], lnv[0], AF.Exp, scale=-0.5),
                reads=[lnv[1]], writes=[rstd[1]])
            b.v(lambda v, tb=tb: v.tensor_scalar(xn1[0], x1t[0][:, tb % 2, :],
                                                 mcol[0], rstd[0], ALU.subtract, ALU.mult),
                reads=[x1t[1], mcol[1], rstd[1]], writes=[xn1[1]])
            b.dma("sync", lambda g, gb=gb: g.dma_start(
                out=xn1_ap[gb * 128:(gb + 1) * 128, :], in_=xn1[0]),
                reads=[xn1[1]], writes=[xn1_hb])
            b.v(lambda v, tb=tb: v.tensor_tensor(acc2[0], x1t[0][:, tb % 2, :], xn1[0], ALU.add),
                reads=[x1t[1], xn1[1]], writes=[acc2[1]])
            b.dma("sync", lambda g, gb=gb: g.dma_start(
                out=accb_ap[gb * 128:(gb + 1) * 128, :], in_=acc2[0]),
                reads=[acc2[1]], writes=[accb_hb])
            b.v(lambda v: v.reduce_sum(mval[0], xn1[0], AX.X), reads=[xn1[1]], writes=[mval[1]])
            b.v(lambda v: v.tensor_scalar(mcol[0], mval[0], 1.0 / D, None, ALU.mult),
                reads=[mval[1]], writes=[mcol[1]])
            b.v(lambda v: v.tensor_scalar(mval[0], mcol[0], -1.0, None, ALU.mult),
                reads=[mcol[1]], writes=[mval[1]])
            b.s(lambda sc: sc.activation(xlnt[0], xn1[0], AF.Square, bias=mval[0],
                                         accum_out=ssum[0]),
                reads=[xn1[1], mval[1]], writes=[xlnt[1], ssum[1]])
            b.v(lambda v: v.tensor_scalar(v1[0], ssum[0], 1.0 / D, 1e-5, ALU.mult, ALU.add),
                reads=[ssum[1]], writes=[v1[1]])
            b.s(lambda sc: sc.activation(lnv[0], v1[0], AF.Ln), reads=[v1[1]], writes=[lnv[1]])
            b.s(lambda sc: sc.activation(rstd[0], lnv[0], AF.Exp, scale=-0.5),
                reads=[lnv[1]], writes=[rstd[1]])
            b.v(lambda v: v.tensor_scalar(xlnt[0], xn1[0], mcol[0], rstd[0],
                                          ALU.subtract, ALU.mult),
                reads=[xn1[1], mcol[1], rstd[1]], writes=[xlnt[1]])
            b.v(lambda v: v.tensor_tensor(xlnt[0], xlnt[0], gnb[0], ALU.mult),
                reads=[xlnt[1], gnb[1]], writes=[xlnt[1]])
            b.v(lambda v, tb=tb: v.tensor_tensor(xln_tm[0][:, tb, :], xlnt[0], bnb[0], ALU.add),
                reads=[xlnt[1], bnb[1]], writes=[xln_tm[1]])
            for dc in range(8):
                b.t(lambda t, tb=tb, dc=dc: t.transpose(
                    pstr[0][:, dc * 128:(dc + 1) * 128],
                    xln_tm[0][:, tb, dc * 128:(dc + 1) * 128], idb[0]),
                    reads=[xln_tm[1], idb[1]], writes=[pstr[1]], guard=(256 if dc == 7 else 0))
            b.v(lambda v, tb=tb: v.tensor_copy(
                xln_fm[0][:, :, tb * 128:(tb + 1) * 128],
                pstr[0].rearrange("p (c q) -> p c q", q=128)),
                reads=[pstr[1]], writes=[xln_fm[1]])
        def emit_xdbl(ec):
            xcb = xcbP[ec % 2]
            b.t(lambda t, ec=ec, xcb=xcb: t.matmul(
                ps[2][0][0:64, :], xproj[0][:, ec, 0:64], xcb[0],
                start=(ec == 0), stop=(ec == 15)),
                reads=[xproj[1], xcb[1]], writes=[ps[2][1]], guard=(256 if ec == 15 else 0))
            b.t(lambda t, ec=ec, xcb=xcb: t.matmul(
                ps[3][0][0:64, :], xproj[0][:, ec, 64:128], xcb[0],
                start=(ec == 0), stop=(ec == 15)),
                reads=[xproj[1], xcb[1]], writes=[ps[3][1]], guard=(256 if ec == 15 else 0))

        for ec in range(16):
            pp = ps[ec % 2]
            for dc in range(8):
                b.t(lambda t, ec=ec, dc=dc, pp=pp: t.matmul(
                    pp[0], inpw[0][:, dc, ec * 128:(ec + 1) * 128], xln_fm[0][:, dc, :],
                    start=(dc == 0), stop=(dc == 7)),
                    reads=[inpw[1], xln_fm[1]], writes=[pp[1]])
            b.v(lambda v, ec=ec, pp=pp: v.tensor_copy(xs[0][:, ec, :], pp[0]),
                reads=[pp[1]], writes=[xs[1]])
            b.v(lambda v, ec=ec: v.tensor_copy(xe[0][:, 0:3], tail[0][:, ec, :]),
                reads=[tail[1]], writes=[xe[1]])
            b.v(lambda v, ec=ec: v.tensor_copy(xe[0][:, 3:515], xs[0][:, ec, :]),
                reads=[xs[1]], writes=[xe[1]])
            b.v(lambda v, ec=ec: v.tensor_copy(tail[0][:, ec, :], xs[0][:, ec, 509:512]),
                reads=[xs[1]], writes=[tail[1]])
            b.v(lambda v, ec=ec: v.tensor_scalar(cacc[0], xe[0][:, 0:512],
                                                 convw[0][:, ec, 0:1], None, ALU.mult),
                reads=[xe[1], convw[1]], writes=[cacc[1]])
            for j in range(1, 4):
                b.v(lambda v, ec=ec, j=j: v.scalar_tensor_tensor(
                    cacc[0], xe[0][:, j:j + 512], convw[0][:, ec, j:j + 1], cacc[0],
                    ALU.mult, ALU.add),
                    reads=[xe[1], convw[1], cacc[1]], writes=[cacc[1]])
            xcb = xcbP[ec % 2]
            b.s(lambda sc, ec=ec, xcb=xcb: sc.activation(xcb[0], cacc[0], AF.Silu,
                                                         bias=convbf[0][:, ec:ec + 1]),
                reads=[cacc[1], convbf[1]], writes=[xcb[1]])
            b.dma("sync", lambda g, s=s, ec=ec, xcb=xcb: g.dma_start(
                out=xc_ap[ec, :, s * 512:(s + 1) * 512], in_=xcb[0]),
                reads=[xcb[1]], writes=[xc_hb])
            if ec > 0:
                emit_xdbl(ec - 1)
        emit_xdbl(15)
        b.v(lambda v, s=s: v.tensor_copy(BCsb[0][:, s * 512:(s + 1) * 512], ps[2][0][0:64, :]),
            reads=[ps[2][1]], writes=[BCsb[1]])
        b.v(lambda v, s=s: v.tensor_copy(dtlowb[0][:, s * 512:(s + 1) * 512], ps[3][0][0:64, :]),
            reads=[ps[3][1]], writes=[dtlowb[1]])
        for ec in range(16, 32):
            pp = ps[ec % 2]
            for dc in range(8):
                b.t(lambda t, ec=ec, dc=dc, pp=pp: t.matmul(
                    pp[0], inpw[0][:, dc, ec * 128:(ec + 1) * 128], xln_fm[0][:, dc, :],
                    start=(dc == 0), stop=(dc == 7)),
                    reads=[inpw[1], xln_fm[1]], writes=[pp[1]])
            b.s(lambda sc, ec=ec, pp=pp: sc.activation(szb[0], pp[0], AF.Silu),
                reads=[pp[1]], writes=[szb[1]])
            b.dma("sync", lambda g, s=s, ec=ec: g.dma_start(
                out=sz_ap[ec - 16, :, s * 512:(s + 1) * 512], in_=szb[0]),
                reads=[szb[1]], writes=[sz_hb])
    b.dma("sync", lambda g: g.dma_start(out=bc_ap, in_=BCsb[0]),
          reads=[BCsb[1]], writes=[bc_hb])
    b.dma("sync", lambda g: g.dma_start(out=dl_ap, in_=dtlowb[0]),
          reads=[dtlowb[1]], writes=[dl_hb])
    assert ar.off <= 212200, ar.off


WN_C = ["lng", "lnb", "inpT", "convw", "convb", "xprojTp", "ident"]


def make_prog_c():
    pr = Prog()
    shp = {"lng": [1, 1024], "lnb": [1, 1024], "inpT": [1024, 4096], "convw": [2048, 4],
           "convb": [16, 128], "xprojTp": [2048, 128], "ident": [128, 128]}
    for n in WN_C:
        pr.dram_in(n, shp[n])
    pr.dram_in("x1", [T, D])
    pr.dram_out("xn1_h", [T, D])
    pr.dram_out("accb_h", [T, D])
    pr.dram_out("xc_h", [16, 128, T], BF16)
    pr.dram_out("sz_h", [16, 128, T], BF16)
    pr.dram_out("BC_h", [64, T], BF16)
    pr.dram_out("dtlow_h", [64, T], BF16)
    build_phase_c(pr, pr.hbm)
    pr.finish()
    emit_block(pr.nc, pr.plan, pr.stack)
    return pr


NCH = 8
TC = 512


def build_phase_d(pr, io):
    nc, b = pr.nc, pr.b
    st = pr.stack
    ar = pr.arena("d")

    def A(shape, dtype=F32):
        return (ar.alloc(shape, dtype), Buf())

    outp = A([128, 16, 1024], BF16)
    dtpj = A([64, 2048], BF16)
    Afm = A([128, 16, 16], F32)
    dtbf = A([128, 16], F32)
    Dfm = A([128, 16], F32)
    t16 = A([16, 128], F32)
    idf = A([128, 128], F32)
    dtlow = A([64, 4096], BF16)
    dtt = A([128, 512], F32)
    absx = A([128, 512], F32)
    rlx = A([128, 512], F32)
    ex1 = A([128, 512], F32)
    ubf = A([128, 512], BF16)
    at2 = A([128, 2, 512], F32)
    bt_all = A([128, 16, 512], BF16)
    ht_all = A([128, 16, 512], BF16)
    yac = A([128, 512], F32)
    Bball = A([128, 2, 16, 512], BF16)
    Cball = A([128, 2, 16, 512], BF16)
    xct = A([128, 2, 512], F32)
    szt = A([128, 2, 512], F32)
    yg = A([128, 2, 16, 512], BF16)
    hstate = A([128, 16, 16], F32)
    acct = A([128, 2, 1024], F32)

    ps, _pstr_unused = pr.psum()

    def wload(dst, src_name, pat=None, **kw):
        ap, hb = io[src_name]
        src = ap.rearrange(pat, **kw) if pat else ap
        b.dma("gpsimd", lambda g, o=dst[0], s=src: g.dma_start(out=o, in_=s),
              reads=[hb], writes=[dst[1]])

    def fload(dst, src_name, pat=None, bcast=None, **kw):
        ap, hb = io[src_name]
        src = ap.rearrange(pat, **kw) if pat else ap
        if bcast:
            src = src.broadcast_to(bcast)
        b.dma("sync", lambda g, o=dst[0], s=src: g.dma_start(out=o, in_=s),
              reads=[hb], writes=[dst[1]])

    wload(outp, "outpT", "(c p) q -> p c q", p=128)
    wload(dtpj, "dtprojT")
    wload(dtlow, "dtlow_h")
    fload(idf, "ident")
    fload(Afm, "Alog", "(c p) n -> p c n", p=128)
    b.s(lambda sc: sc.activation(Afm[0], Afm[0], AF.Exp), reads=[Afm[1]], writes=[Afm[1]])
    b.v(lambda v: v.tensor_scalar(Afm[0], Afm[0], -1.0, None, ALU.mult),
        reads=[Afm[1]], writes=[Afm[1]])
    for name, dst in (("dtb", dtbf), ("Dp", Dfm)):
        fload(t16, name)
        b.t(lambda t: t.transpose(ps[0][0][0:128, 0:16], t16[0], idf[0][0:16, 0:16]),
            reads=[t16[1], idf[1]], writes=[ps[0][1]], guard=512)
        b.v(lambda v, dst=dst: v.tensor_copy(dst[0], ps[0][0][:, 0:16]),
            reads=[ps[0][1]], writes=[dst[1]])
    b.v(lambda v: v.memset(hstate[0], 0.0), writes=[hstate[1]])

    xc_ap, xc_hb = io["xc_h"]
    sz_ap, sz_hb = io["sz_h"]
    bc_ap, bc_hb = io["BC_h"]
    accb_ap, accb_hb = io["accb_h"]
    acc2_ap, acc2_hb = io["acc2_h"]

    BbB = [Buf(), Buf()]
    CbB = [Buf(), Buf()]
    atB = [Buf(), Buf()]
    xctB = [Buf(), Buf()]
    sztB = [Buf(), Buf()]
    acctB = [Buf(), Buf()]
    ygB = [Buf(), Buf()]
    pend_op = [None]
    for c in range(NCH):
        cb = c % 2
        pops = pend_op[0] or []
        pk = [0]

        def drip(dc2):
            while pk[0] < len(pops) and pk[0] * 16 <= (dc2 + 1) * len(pops):
                pops[pk[0]]()
                pk[0] += 1
        for n in range(16):
            b.dma("sync", lambda g, c=c, n=n, cb=cb: g.dma_start(
                out=Bball[0][:, cb, n, :],
                in_=bc_ap[32 + n:33 + n, c * TC:(c + 1) * TC].broadcast_to([128, TC])),
                reads=[bc_hb], writes=[BbB[cb]])
            b.dma("gpsimd", lambda g, c=c, n=n, cb=cb: g.dma_start(
                out=Cball[0][:, cb, n, :],
                in_=bc_ap[n:n + 1, c * TC:(c + 1) * TC].broadcast_to([128, TC])),
                reads=[bc_hb], writes=[CbB[cb]])
        for dc2 in range(16):
            db = dc2 % 2
            pp = ps[db]
            b.t(lambda t, c=c, dc2=dc2, pp=pp: t.matmul(
                pp[0], dtpj[0][0:64, dc2 * 128:(dc2 + 1) * 128],
                dtlow[0][0:64, c * TC:(c + 1) * TC], start=True, stop=True),
                reads=[dtpj[1], dtlow[1]], writes=[pp[1]])
            b.s(lambda sc, dc2=dc2, pp=pp: sc.activation(absx[0], pp[0], AF.Abs,
                                                         bias=dtbf[0][:, dc2:dc2 + 1]),
                reads=[pp[1], dtbf[1]], writes=[absx[1]])
            b.s(lambda sc, dc2=dc2, pp=pp: sc.activation(rlx[0], pp[0], AF.Relu,
                                                         bias=dtbf[0][:, dc2:dc2 + 1]),
                reads=[pp[1], dtbf[1]], writes=[rlx[1]])
            b.s(lambda sc: sc.activation(ex1[0], absx[0], AF.Exp, scale=-1.0),
                reads=[absx[1]], writes=[ex1[1]])
            b.s(lambda sc: sc.activation(ex1[0], ex1[0], AF.Ln, bias=1.0),
                reads=[ex1[1]], writes=[ex1[1]])
            b.v(lambda v: v.tensor_tensor(dtt[0], ex1[0], rlx[0], ALU.add),
                reads=[ex1[1], rlx[1]], writes=[dtt[1]])
            b.dma("gpsimd", lambda g, c=c, dc2=dc2, db=db: g.dma_start(
                out=xct[0][:, db, :], in_=xc_ap[dc2, :, c * TC:(c + 1) * TC]),
                reads=[xc_hb], writes=[xctB[db]])
            def _umul(v, db=db):
                with nc.allow_low_precision(reason="b operand of scan in bf16"):
                    return v.tensor_tensor(ubf[0], dtt[0], xct[0][:, db, :], ALU.mult)
            b.v(_umul, reads=[dtt[1], xctB[db]], writes=[ubf[1]])
            b.v(lambda v, cb=cb: v.tensor_tensor(
                bt_all[0], Bball[0][:, cb, :, :],
                ubf[0].unsqueeze(1).broadcast_to([128, 16, TC]), ALU.mult),
                reads=[BbB[cb], ubf[1]], writes=[bt_all[1]])
            for n in range(16):
                ab = n % 2
                b.s(lambda sc, dc2=dc2, n=n, ab=ab: sc.activation(
                    at2[0][:, ab, :], dtt[0], AF.Exp, scale=Afm[0][:, dc2, n:n + 1]),
                    reads=[dtt[1], Afm[1]], writes=[atB[ab]])
                b.v(lambda v, dc2=dc2, n=n, ab=ab: v.tensor_tensor_scan(
                    ht_all[0][:, n, :], at2[0][:, ab, :], bt_all[0][:, n, :],
                    hstate[0][:, dc2, n:n + 1], ALU.mult, ALU.add),
                    reads=[atB[ab], bt_all[1], hstate[1]], writes=[ht_all[1]])
            b.v(lambda v, dc2=dc2: v.tensor_copy(
                hstate[0][:, dc2, :],
                ht_all[0][:, :, TC - 1:TC].rearrange("p a b -> p (a b)")),
                reads=[ht_all[1]], writes=[hstate[1]])
            b.v(lambda v, cb=cb: v.tensor_tensor(
                bt_all[0], ht_all[0], Cball[0][:, cb, :, :], ALU.mult),
                reads=[ht_all[1], CbB[cb]], writes=[bt_all[1]])
            for w in (8, 4, 2):
                b.v(lambda v, w=w: v.tensor_tensor(
                    bt_all[0][:, 0:w, :], bt_all[0][:, 0:w, :],
                    bt_all[0][:, w:2 * w, :], ALU.add),
                    reads=[bt_all[1]], writes=[bt_all[1]])
            b.v(lambda v: v.tensor_tensor(yac[0], bt_all[0][:, 0, :],
                                          bt_all[0][:, 1, :], ALU.add),
                reads=[bt_all[1]], writes=[yac[1]])
            b.dma("gpsimd", lambda g, c=c, dc2=dc2, db=db: g.dma_start(
                out=szt[0][:, db, :], in_=sz_ap[dc2, :, c * TC:(c + 1) * TC]),
                reads=[sz_hb], writes=[sztB[db]])
            b.v(lambda v, dc2=dc2, db=db: v.scalar_tensor_tensor(
                yac[0], xct[0][:, db, :], Dfm[0][:, dc2:dc2 + 1], yac[0],
                ALU.mult, ALU.add),
                reads=[xctB[db], Dfm[1], yac[1]], writes=[yac[1]])
            b.v(lambda v, dc2=dc2, db=db, cb=cb: v.tensor_tensor(
                yg[0][:, cb, dc2, :], yac[0], szt[0][:, db, :], ALU.mult),
                reads=[yac[1], sztB[db]], writes=[ygB[cb]])
            drip(dc2)
        if c == 0 and "dbgd" in io:
            dd_ap, dd_hb = io["dbgd"]
            b.dma("sync", lambda g: g.dma_start(out=dd_ap[:, 0:256],
                  in_=Afm[0].rearrange("p a n -> p (a n)")), reads=[Afm[1]], writes=[dd_hb])
            b.dma("sync", lambda g: g.dma_start(out=dd_ap[:, 256:768], in_=dtt[0]),
                  reads=[dtt[1]], writes=[dd_hb])
            b.dma("sync", lambda g: g.dma_start(out=dd_ap[:, 768:1024],
                  in_=hstate[0].rearrange("p a n -> p (a n)")), reads=[hstate[1]], writes=[dd_hb])
            b.dma("sync", lambda g: g.dma_start(out=dd_ap[:, 1024:1536], in_=yac[0]),
                  reads=[yac[1]], writes=[dd_hb])
        def make_outproj(c2):
            cb2 = c2 % 2
            ops = []
            for tb in range(4):
                gb = 4 * c2 + tb
                ab2 = tb % 2

                def fload(gb=gb, ab2=ab2):
                    b.dma("sync", lambda g, gb=gb, ab2=ab2: g.dma_start(
                        out=acct[0][:, ab2, :], in_=accb_ap[gb * 128:(gb + 1) * 128, :]),
                        reads=[accb_hb], writes=[acctB[ab2]])
                ops.append(fload)
                for oc in range(2):
                    def fmm(tb=tb, oc=oc, ab2=ab2, cb2=cb2):
                        pp = ps[2 + oc]
                        for dc2 in range(16):
                            b.t(lambda t, tb=tb, oc=oc, dc2=dc2, pp=pp, cb2=cb2: t.matmul(
                                pp[0], yg[0][:, cb2, dc2, tb * 128:(tb + 1) * 128],
                                outp[0][:, dc2, oc * 512:(oc + 1) * 512],
                                start=(dc2 == 0), stop=(dc2 == 15)),
                                reads=[ygB[cb2], outp[1]], writes=[pp[1]])
                        b.v(lambda v, oc=oc, pp=pp, ab2=ab2: v.tensor_tensor(
                            acct[0][:, ab2, oc * 512:(oc + 1) * 512],
                            acct[0][:, ab2, oc * 512:(oc + 1) * 512], pp[0], ALU.add),
                            reads=[pp[1], acctB[ab2]], writes=[acctB[ab2]])
                    ops.append(fmm)

                def fstore(gb=gb, ab2=ab2):
                    b.dma("sync", lambda g, gb=gb, ab2=ab2: g.dma_start(
                        out=acc2_ap[gb * 128:(gb + 1) * 128, :], in_=acct[0][:, ab2, :]),
                        reads=[acctB[ab2]], writes=[acc2_hb])
                ops.append(fstore)
            return ops

        pend_op[0] = make_outproj(c)
        if c == NCH - 1:
            for f in pend_op[0]:
                f()
    assert ar.off <= 212200, ar.off


WN_D = ["outpT", "dtprojT", "Alog", "dtb", "Dp", "ident"]


def make_prog_d():
    pr = Prog()
    shp = {"outpT": [2048, 1024], "dtprojT": [64, 2048], "Alog": [2048, 16],
           "dtb": [16, 128], "Dp": [16, 128], "ident": [128, 128]}
    for n in WN_D:
        pr.dram_in(n, shp[n])
    pr.dram_in("xc_h", [16, 128, T], BF16)
    pr.dram_in("sz_h", [16, 128, T], BF16)
    pr.dram_in("BC_h", [64, T], BF16)
    pr.dram_in("dtlow_h", [64, T], BF16)
    pr.dram_in("accb_h", [T, D])
    pr.dram_out("acc2_h", [T, D])
    build_phase_d(pr, pr.hbm)
    pr.finish()
    emit_block(pr.nc, pr.plan, pr.stack)
    return pr


def build_phase_e(pr, io):
    nc, b = pr.nc, pr.b
    st = pr.stack
    ar = pr.arena("e")

    def A(shape, dtype=F32):
        return (ar.alloc(shape, dtype), Buf())

    wfc = A([128, 8, 4096], BF16)
    wpj = A([128, 32, 1024], BF16)
    bfc = A([1, 4096], BF16)
    bpj = A([1, 1024], BF16)
    idb = A([128, 128], BF16)
    onesr = A([1, 128], BF16)
    _marke = ar.mark()

    ps, pstr = pr.psum()

    def wload(dst, src_name, pat=None, **kw):
        ap, hb = io[src_name]
        src = ap.rearrange(pat, **kw) if pat else ap
        b.dma("gpsimd", lambda g, o=dst[0], s=src: g.dma_start(out=o, in_=s),
              reads=[hb], writes=[dst[1]])

    wload(bfc, "bfcq")
    wload(bpj, "bprojq")
    wload(idb, "ident")
    b.v(lambda v: v.memset(onesr[0], 1.0), writes=[onesr[1]])
    wraw = A([128, 4096], F32)
    wqb = A([128, 4096], BF16)
    amw = A([128, 128], F32)
    scw = A([128, 128], F32)
    ivw = A([128, 128], F32)
    qw = A([128, 4096], F32)

    xn1_ap, xn1_hb = io["xn1_h"]
    acc2_ap, acc2_hb = io["acc2_h"]
    out_ap, out_hb = io["out"]

    M = 12582912.0
    wfcO_ap, wfcO_hb = io["wfcO"]
    wprojO_ap, wprojO_hb = io["wprojO"]

    def quantw(src_t, q_t, dst_bf, ncols):
        nblk = ncols // 32
        b.v(lambda v, nblk=nblk, ncols=ncols: v.tensor_reduce(
            amw[0][:, 0:nblk], src_t[0][:, 0:ncols].rearrange("p (o i) -> p o i", i=32),
            AX.X, ALU.max, apply_absolute_value=True),
            reads=[src_t[1]], writes=[amw[1]])
        b.v(lambda v, nblk=nblk: v.tensor_scalar(amw[0][:, 0:nblk], amw[0][:, 0:nblk],
                                                 2.0 ** -100, None, ALU.max),
            reads=[amw[1]], writes=[amw[1]])
        b.v(lambda v, nblk=nblk: v.tensor_scalar(scw[0][:, 0:nblk].bitcast(I32),
                                                 amw[0][:, 0:nblk].bitcast(I32),
                                                 0x7F800000, None, ALU.bitwise_and),
            reads=[amw[1]], writes=[scw[1]])
        b.v(lambda v, nblk=nblk: v.tensor_scalar(scw[0][:, 0:nblk], scw[0][:, 0:nblk],
                                                 0.125, None, ALU.mult),
            reads=[scw[1]], writes=[scw[1]])
        b.v(lambda v, nblk=nblk: v.reciprocal(ivw[0][:, 0:nblk], scw[0][:, 0:nblk]),
            reads=[scw[1]], writes=[ivw[1]])
        b.v(lambda v, nblk=nblk, ncols=ncols: v.tensor_tensor(
            q_t[0][:, 0:ncols].rearrange("p (o i) -> p o i", i=32),
            src_t[0][:, 0:ncols].rearrange("p (o i) -> p o i", i=32),
            ivw[0][:, 0:nblk].unsqueeze(2).broadcast_to([128, nblk, 32]), ALU.mult),
            reads=[src_t[1], ivw[1]], writes=[q_t[1]])
        b.v(lambda v, ncols=ncols: v.tensor_scalar(q_t[0][:, 0:ncols], q_t[0][:, 0:ncols],
                                                   M, M, ALU.add, ALU.subtract),
            reads=[q_t[1]], writes=[q_t[1]])
        b.v(lambda v, nblk=nblk, ncols=ncols: v.tensor_tensor(
            dst_bf[0][:, 0:ncols].rearrange("p (o i) -> p o i", i=32),
            q_t[0][:, 0:ncols].rearrange("p (o i) -> p o i", i=32),
            scw[0][:, 0:nblk].unsqueeze(2).broadcast_to([128, nblk, 32]), ALU.mult),
            reads=[q_t[1], scw[1]], writes=[dst_bf[1]])

    for ncj in range(32):
        b.dma("sync", lambda g, ncj=ncj: g.dma_start(
            out=wraw[0][:, 0:1024], in_=wfcO_ap[ncj * 128:(ncj + 1) * 128, :]),
            reads=[wfcO_hb], writes=[wraw[1]])
        quantw(wraw, qw, wqb, 1024)
        for kc in range(8):
            b.t(lambda t, kc=kc: t.transpose(
                pstr[0][:, (kc % 8) * 128:(kc % 8 + 1) * 128],
                wqb[0][:, kc * 128:(kc + 1) * 128], idb[0]),
                reads=[wqb[1], idb[1]], writes=[pstr[1]], guard=(256 if kc == 7 else 0))
        b.v(lambda v, ncj=ncj: v.tensor_copy(
            wfc[0][:, :, ncj * 128:(ncj + 1) * 128],
            pstr[0].rearrange("p (c q) -> p c q", q=128)),
            reads=[pstr[1]], writes=[wfc[1]])
    for ncj in range(8):
        b.dma("sync", lambda g, ncj=ncj: g.dma_start(
            out=wraw[0], in_=wprojO_ap[ncj * 128:(ncj + 1) * 128, :]),
            reads=[wprojO_hb], writes=[wraw[1]])
        quantw(wraw, qw, wqb, 4096)
        for kc in range(32):
            b.t(lambda t, kc=kc: t.transpose(
                pstr[0][:, (kc % 8) * 128:(kc % 8 + 1) * 128],
                wqb[0][:, kc * 128:(kc + 1) * 128], idb[0]),
                reads=[wqb[1], idb[1]], writes=[pstr[1]], guard=(256 if kc % 8 == 7 else 0))
            if kc % 8 == 7:
                b.v(lambda v, ncj=ncj, kc=kc: v.tensor_copy(
                    wpj[0][:, kc - 7:kc + 1, ncj * 128:(ncj + 1) * 128],
                    pstr[0].rearrange("p (c q) -> p c q", q=128)),
                    reads=[pstr[1]], writes=[wpj[1]])

    def quant(src_t, width, nblk, amax_t, sc_t, iv_t, q_t, dst_bf):
        b.v(lambda v: v.tensor_reduce(
            amax_t[0], src_t[0].rearrange("p (o i) -> p o i", i=32), AX.X, ALU.max,
            apply_absolute_value=True),
            reads=[src_t[1]], writes=[amax_t[1]])
        b.v(lambda v: v.tensor_scalar(amax_t[0], amax_t[0], 2.0 ** -100, None, ALU.max),
            reads=[amax_t[1]], writes=[amax_t[1]])
        b.v(lambda v: v.tensor_scalar(sc_t[0].bitcast(I32), amax_t[0].bitcast(I32),
                                      0x7F800000, None, ALU.bitwise_and),
            reads=[amax_t[1]], writes=[sc_t[1]])
        b.v(lambda v: v.tensor_scalar(sc_t[0], sc_t[0], 0.125, None, ALU.mult),
            reads=[sc_t[1]], writes=[sc_t[1]])
        b.v(lambda v: v.reciprocal(iv_t[0], sc_t[0]), reads=[sc_t[1]], writes=[iv_t[1]])
        b.v(lambda v: v.tensor_tensor(
            q_t[0].rearrange("p (o i) -> p o i", i=32),
            src_t[0].rearrange("p (o i) -> p o i", i=32),
            iv_t[0].unsqueeze(2).broadcast_to([128, nblk, 32]), ALU.mult),
            reads=[src_t[1], iv_t[1]], writes=[q_t[1]])
        b.v(lambda v: v.tensor_scalar(q_t[0], q_t[0], M, M, ALU.add, ALU.subtract),
            reads=[q_t[1]], writes=[q_t[1]])
        b.v(lambda v: v.tensor_tensor(
            dst_bf[0].rearrange("p (o i) -> p o i", i=32),
            q_t[0].rearrange("p (o i) -> p o i", i=32),
            sc_t[0].unsqueeze(2).broadcast_to([128, nblk, 32]), ALU.mult),
            reads=[q_t[1], sc_t[1]], writes=[dst_bf[1]])

    gatee = wpj[1].wr
    ar.reset(_marke)

    def AGE(shape, dtype=F32):
        bb = Buf(); bb.wr = gatee
        return (ar.alloc(shape, dtype), bb)

    xn1P = [AGE([128, 1024], F32), AGE([128, 1024], F32)]
    am1 = AGE([128, 32], F32)
    sc1 = AGE([128, 32], F32)
    iv1 = AGE([128, 32], F32)
    xqP = [AGE([128, 1024], BF16), AGE([128, 1024], BF16)]
    xqfP = [AGE([128, 8, 128], BF16), AGE([128, 8, 128], BF16)]
    hgP = [AGE([128, 2048], F32), AGE([128, 2048], F32)]
    am2 = AGE([128, 64], F32)
    sc2 = AGE([128, 64], F32)
    iv2 = AGE([128, 64], F32)
    q2 = AGE([128, 2048], F32)
    q1 = (q2[0][:, 0:1024], q2[1])
    hq = AGE([128, 4096], BF16)
    hqf = AGE([128, 32, 128], BF16)
    acctP = [AGE([128, 1024], F32), AGE([128, 1024], F32)]
    for gb in range(NBLK):
        gi = gb % 2
        xn1, xq, xqf, acct = xn1P[gi], xqP[gi], xqfP[gi], acctP[gi]
        b.dma("gpsimd", lambda g, gb=gb, xn1=xn1: g.dma_start(
            out=xn1[0], in_=xn1_ap[gb * 128:(gb + 1) * 128, :]),
            reads=[xn1_hb], writes=[xn1[1]])
        quant(xn1, 1024, 32, am1, sc1, iv1, q1, xq)
        for dc in range(8):
            b.t(lambda t, dc=dc, xq=xq: t.transpose(
                pstr[0][:, (dc % 8) * 128:(dc % 8 + 1) * 128],
                xq[0][:, dc * 128:(dc + 1) * 128], idb[0]),
                reads=[xq[1], idb[1]], writes=[pstr[1]], guard=(256 if dc == 7 else 0))
        b.v(lambda v, xqf=xqf: v.tensor_copy(xqf[0], pstr[0].rearrange("p (c q) -> p c q", q=128)),
            reads=[pstr[1]], writes=[xqf[1]])
        for hg in range(2):
            hg_ = hgP[hg]
            for nb in range(4):
                pp = ps[nb]
                off = hg * 2048 + nb * 512
                b.t(lambda t, off=off, pp=pp: t.matmul(
                    pp[0], onesr[0], bfc[0][0:1, off:off + 512], start=True, stop=False),
                    reads=[onesr[1], bfc[1]], writes=[pp[1]])
            for dc in range(8):
                for nb in range(4):
                    pp = ps[nb]
                    off = hg * 2048 + nb * 512
                    b.t(lambda t, dc=dc, off=off, pp=pp, xqf=xqf: t.matmul(
                        pp[0], xqf[0][:, dc, :], wfc[0][:, dc, off:off + 512],
                        start=False, stop=(dc == 7)),
                        reads=[xqf[1], wfc[1]], writes=[pp[1]])
            for nb in range(4):
                pp = ps[nb]
                b.s(lambda sc, nb=nb, pp=pp, hg_=hg_: sc.activation(
                    hg_[0][:, nb * 512:(nb + 1) * 512], pp[0], AF.Gelu),
                    reads=[pp[1]], writes=[hg_[1]])
        for hg in range(2):
            hq_half = (hq[0][:, hg * 2048:(hg + 1) * 2048], hq[1])
            quant(hgP[hg], 2048, 64, am2, sc2, iv2, q2, hq_half)
        for hc in range(32):
            b.t(lambda t, hc=hc: t.transpose(
                pstr[0][:, (hc % 8) * 128:(hc % 8 + 1) * 128],
                hq[0][:, hc * 128:(hc + 1) * 128], idb[0]),
                reads=[hq[1], idb[1]], writes=[pstr[1]], guard=(256 if hc % 8 == 7 else 0))
            if hc % 8 == 7:
                b.v(lambda v, hc=hc: v.tensor_copy(
                    hqf[0][:, hc - 7:hc + 1, :], pstr[0].rearrange("p (c q) -> p c q", q=128)),
                    reads=[pstr[1]], writes=[hqf[1]])
        b.dma("sync", lambda g, gb=gb, acct=acct: g.dma_start(
            out=acct[0], in_=acc2_ap[gb * 128:(gb + 1) * 128, :]),
            reads=[acc2_hb], writes=[acct[1]])
        for oc in range(2):
            pp = ps[4 + oc]
            b.t(lambda t, oc=oc, pp=pp: t.matmul(
                pp[0], onesr[0], bpj[0][0:1, oc * 512:(oc + 1) * 512], start=True, stop=False),
                reads=[onesr[1], bpj[1]], writes=[pp[1]])
        for hc in range(32):
            for oc in range(2):
                pp = ps[4 + oc]
                b.t(lambda t, hc=hc, oc=oc, pp=pp: t.matmul(
                    pp[0], hqf[0][:, hc, :], wpj[0][:, hc, oc * 512:(oc + 1) * 512],
                    start=False, stop=(hc == 31)),
                    reads=[hqf[1], wpj[1]], writes=[pp[1]])
        for oc in range(2):
            pp = ps[4 + oc]
            b.v(lambda v, oc=oc, pp=pp, acct=acct: v.tensor_tensor(
                acct[0][:, oc * 512:(oc + 1) * 512],
                acct[0][:, oc * 512:(oc + 1) * 512], pp[0], ALU.add),
                reads=[pp[1], acct[1]], writes=[acct[1]])
        b.dma("sync", lambda g, gb=gb, acct=acct: g.dma_start(
            out=out_ap[gb * 128:(gb + 1) * 128, :], in_=acct[0]),
            reads=[acct[1]], writes=[out_hb])
    assert ar.off <= 212200, ar.off


WN_E = ["wfcO", "wprojO", "bfcq", "bprojq", "ident"]


def make_prog_e():
    pr = Prog()
    shp = {"wfcO": [4096, 1024], "wprojO": [1024, 4096], "bfcq": [1, 4096],
           "bprojq": [1, 1024], "ident": [128, 128]}
    for n in WN_E:
        pr.dram_in(n, shp[n])
    pr.dram_in("xn1_h", [T, D])
    pr.dram_in("acc2_h", [T, D])
    pr.dram_out("out", [T, D])
    build_phase_e(pr, pr.hbm)
    pr.finish()
    emit_block(pr.nc, pr.plan, pr.stack)
    return pr


def kernel(**inputs):
    return kernel_fused(**inputs)


def kernel_multilaunch(**inputs):
    x_full = np.ascontiguousarray(np.asarray(inputs["L0_attn_x"], dtype=np.float32))
    ncore = x_full.shape[0]
    w = prep_inputs(inputs)
    cores = list(range(ncore))

    pa = make_prog_a()
    ra = run_launch(pa, [{**{n: w[n] for n in WN_A}, "x": np.ascontiguousarray(x_full[c])}
                         for c in cores])
    pa.stack.close()

    pb = make_prog_b()
    rb = run_launch(pb, [{**{n: w[n] for n in WN_B},
                          "alpha_h": np.asarray(ra[c]["alpha_h"]),
                          "slotin_h": np.asarray(ra[c]["slotin_h"]),
                          "acc": np.asarray(ra[c]["acc"])} for c in cores])
    pb.stack.close()

    pc = make_prog_c()
    rc = run_launch(pc, [{**{n: w[n] for n in WN_C},
                          "x1": np.asarray(rb[c]["x1"])} for c in cores])
    pc.stack.close()

    pd = make_prog_d()
    rd = run_launch(pd, [{**{n: w[n] for n in WN_D},
                          "xc_h": np.asarray(rc[c]["xc_h"]),
                          "sz_h": np.asarray(rc[c]["sz_h"]),
                          "BC_h": np.asarray(rc[c]["BC_h"]),
                          "dtlow_h": np.asarray(rc[c]["dtlow_h"]),
                          "accb_h": np.asarray(rc[c]["accb_h"])} for c in cores])
    pd.stack.close()

    pe = make_prog_e()
    re_ = run_launch(pe, [{**{n: w[n] for n in WN_E},
                           "xn1_h": np.asarray(rc[c]["xn1_h"]),
                           "acc2_h": np.asarray(rd[c]["acc2_h"])} for c in cores])
    pe.stack.close()

    out = np.stack([np.asarray(re_[c]["out"], dtype=np.float32) for c in cores], axis=0)
    return out


def make_prog_fused(debug=False):
    pr = Prog()
    shp = {
        "wq": [1024, 1024], "wk": [1024, 256], "wv": [1024, 256], "wo": [1024, 1024],
        "w1xT": [1024, 512], "w1mT": [1024, 512], "b1": [4, 128], "w2T": [512, 64],
        "b2": [1, 64], "tau": [1, 1], "slot_init": [64, 1024], "slot_scale": [64, 1],
        "ident": [128, 128], "mask1": [128, 128], "maskT": [4, 128, 512],
        "wihT": [1024, 3072], "whhT": [1024, 3072], "bih": [1, 3072], "bhh": [1, 3072],
        "wv0T": [1024, 1024], "bv0": [1, 1024], "wo0T": [1024, 1024], "bo0": [1, 1024],
        "lng": [1, 1024], "lnb": [1, 1024], "inpT": [1024, 4096], "convw": [2048, 4],
        "convb": [16, 128], "xprojTp": [2048, 128],
        "outpT": [2048, 1024], "dtprojT": [64, 2048], "Alog": [2048, 16],
        "dtb": [16, 128], "Dp": [16, 128],
        "wfcO": [4096, 1024], "wprojO": [1024, 4096], "bfcq": [1, 4096], "bprojq": [1, 1024],
    }
    for n, sp in shp.items():
        pr.dram_in(n, sp)
    pr.dram_in("x", [T, D])
    pr.dram_out("out", [T, D])
    mk = pr.dram_out if debug else pr.dram_tmp
    mk("acc", [T, D])
    mk("alpha_h", [T, 64])
    mk("slotin_h", [64, 1024])
    mk("x1", [T, D])
    mk("xn1_h", [T, D])
    mk("accb_h", [T, D])
    pr.dram_tmp("xc_h", [16, 128, T], BF16)
    pr.dram_tmp("sz_h", [16, 128, T], BF16)
    pr.dram_tmp("BC_h", [64, T], BF16)
    pr.dram_tmp("dtlow_h", [64, T], BF16)
    mk("acc2_h", [T, D])
    if debug:
        pr.dram_out("dbgd", [128, 2048])

    build_phase_a(pr, pr.hbm)
    pr.plan.barrier()
    build_phase_b(pr, pr.hbm)
    pr.plan.barrier()
    build_phase_c(pr, pr.hbm)
    pr.plan.barrier()
    build_phase_d(pr, pr.hbm)
    pr.plan.barrier()
    build_phase_e(pr, pr.hbm)
    pr.finish()
    emit_block(pr.nc, pr.plan, pr.stack)
    return pr


WN_ALL = WN_A + WN_B + WN_C + WN_D + WN_E


def kernel_fused(**inputs):
    x_full = np.ascontiguousarray(np.asarray(inputs["L0_attn_x"], dtype=np.float32))
    ncore = x_full.shape[0]
    w = prep_inputs(inputs)
    pr = make_prog_fused()
    wnames = sorted(set(WN_ALL))
    in_maps = [{**{n: w[n] for n in wnames}, "x": np.ascontiguousarray(x_full[c])}
               for c in range(ncore)]
    res = run_launch(pr, in_maps)
    pr.stack.close()
    return np.stack([np.asarray(res[c]["out"], dtype=np.float32) for c in range(ncore)], axis=0)


# version 13 (final)
# speedup vs baseline: 1.0013x; 1.0013x over previous
import numpy as np
import ml_dtypes
import concourse.bass as bass
import concourse.mybir as mybir
from concourse.bass_utils import run_bass_kernel_spmd
from contextlib import ExitStack

F32 = mybir.dt.float32
BF16 = mybir.dt.bfloat16
I32 = mybir.dt.int32
AF = mybir.ActivationFunctionType
ALU = mybir.AluOpType
AX = mybir.AxisListType
BF = ml_dtypes.bfloat16

T = 4096
D = 1024
NSUP = 8
TS = 512
NBLK = 32
NEG = -1e30

GUARD_ON = True
PROLOGUE = True
EPILOGUE = False

QPERM = [0, 4, 1, 5, 2, 6, 3, 7, 8, 12, 9, 13, 10, 14, 11, 15]
KVH = [h // 4 for h in QPERM]


class Plan:
    ENGS = ("sync", "gpsimd", "vector", "scalar", "tensor")

    def __init__(self):
        self.ops = {e: [] for e in self.ENGS}
        self.cnt = {}
        self.sat = {e: {} for e in self.ENGS}

    def add(self, eng, fn, waits=(), inc=None, inc_n=1):
        ws = {}
        for tk in waits:
            if tk is None:
                continue
            sem, val = tk
            if self.sat[eng].get(sem, 0) >= val:
                continue
            ws[sem] = max(ws.get(sem, 0), val)
        for sem, val in ws.items():
            self.sat[eng][sem] = val
        tok = None
        if inc is not None:
            self.cnt[inc] = self.cnt.get(inc, 0) + inc_n
            tok = (inc, self.cnt[inc])
        self.ops[eng].append((fn, sorted(ws.items()), inc, inc_n))
        return tok

    def barrier(self):
        for q in ("sync", "gpsimd"):
            ws = [(f"d_{q}{i}", self.cnt[f"d_{q}{i}"]) for i in range(8)
                  if f"d_{q}{i}" in self.cnt]
            if ws:
                self.add(q, None, waits=ws)
        toks = []
        for e in self.ENGS:
            toks.append(self.add(e, None, inc="bar_" + e))
        for e in self.ENGS:
            self.add(e, None, waits=toks)

    def check_deadlock(self):
        idx = {e: 0 for e in self.ENGS}
        vals = {}
        progress = True
        while progress:
            progress = False
            for e in self.ENGS:
                while idx[e] < len(self.ops[e]):
                    fn, ws, inc, inc_n = self.ops[e][idx[e]]
                    if any(vals.get(s, 0) < v for s, v in ws):
                        break
                    if inc:
                        vals[inc] = vals.get(inc, 0) + inc_n
                    idx[e] += 1
                    progress = True
        stuck = {e: (idx[e], len(self.ops[e])) for e in self.ENGS if idx[e] < len(self.ops[e])}
        if stuck:
            for e, (i, n) in stuck.items():
                fn, ws, inc, _ = self.ops[e][i]
                need = [(s, v, vals.get(s, 0)) for s, v in ws if vals.get(s, 0) < v]
                raise RuntimeError(f"deadlock: {e} op {i}/{n} needs {need}")

    def n_ops(self):
        return sum(len(v) for v in self.ops.values())


class Buf:
    __slots__ = ("wr", "rd", "dly")

    def __init__(self):
        self.wr = None
        self.rd = []
        self.dly = 0


class B:

    def __init__(self, nc, plan):
        self.nc = nc
        self.p = plan
        self.sems = {}

    def step(self, eng, fn, reads=(), writes=(), extra=()):
        waits = list(extra)
        for b in reads:
            if b.wr is not None:
                waits.append(b.wr)
        for b in writes:
            if b.wr is not None:
                waits.append(b.wr)
            waits.extend(b.rd)
        if eng == "tensor":
            waits = [w for w in waits if w is None or w[0] != "s_tensor"]
        tok = self.p.add(eng, fn, waits=waits, inc="s_" + eng)
        for b in reads:
            b.rd.append(tok)
        for b in writes:
            b.wr = tok
            b.rd = []
            b.dly = 0
        return tok

    def v(self, fn, reads=(), writes=(), extra=()):
        return self.step("vector", fn, reads, writes, extra)

    def s(self, fn, reads=(), writes=(), extra=()):
        return self.step("scalar", fn, reads, writes, extra)

    def t(self, fn, reads=(), writes=(), extra=(), guard=0):
        tok = self.step("tensor", fn, reads, writes, extra)
        if guard and GUARD_ON:
            gtok = self.p.add("tensor", lambda e, g=guard: e.nop(cycle_cnt=g, nofuse=True),
                              waits=(), inc="s_tensor")
            for bb in writes:
                bb.wr = gtok
            return gtok
        return tok

    LANES = 8

    def dma(self, queue, fn, reads=(), writes=(), extra=()):
        waits = list(extra)
        for b in reads:
            if b.wr is not None:
                waits.append(b.wr)
        for b in writes:
            if b.wr is not None:
                waits.append(b.wr)
            waits.extend(b.rd)
        key = "_lane_" + queue
        idx = self.p.cnt.get(key, 0)
        self.p.cnt[key] = idx + 1
        sem = f"d_{queue}{idx % self.LANES}"
        waits = [w for w in waits if w is None or not w[0].startswith("d_" + queue)]
        tok = self.p.add(queue, fn, waits=waits, inc=sem, inc_n=16)
        for b in reads:
            b.rd.append(tok)
        for b in writes:
            b.wr = tok
            b.rd = []
        return tok


def emit_block(nc, plan, stack):
    sems = {}
    for name in plan.cnt:
        if name.startswith("_"):
            continue
        sems[name] = stack.enter_context(nc.semaphore(name))
    block = stack.enter_context(nc.Block())

    def runner(eng_name):
        def run(eng):
            if PROLOGUE:
                if eng_name == "gpsimd":
                    for h in sems.values():
                        eng.sem_clear(h)
                else:
                    for _ in range(8):
                        eng.nop(cycle_cnt=5000, nofuse=True)
            for fn, ws, inc, inc_n in plan.ops[eng_name]:
                for sem, val in ws:
                    eng.wait_ge(sems[sem], val)
                if fn == "CLEAR_NONBAR":
                    for nm, h in sems.items():
                        if not nm.startswith("bar_"):
                            eng.sem_clear(h)
                    continue
                if fn == "CLEAR_BAR":
                    for nm, h in sems.items():
                        if nm.startswith("bar_"):
                            eng.sem_clear(h)
                    continue
                if fn is None:
                    if inc is not None:
                        eng.nop().then_inc(sems[inc], inc_n)
                    continue
                ins = fn(eng)
                if inc is not None:
                    ins.then_inc(sems[inc], inc_n)
        return run

    block.sync(runner("sync"))
    block.gpsimd(runner("gpsimd"))
    block.vector(runner("vector"))
    block.scalar(runner("scalar"))
    block.tensor(runner("tensor"))


def prep_inputs(inp):
    f = np.ascontiguousarray
    d = {}
    Wq = np.asarray(inp["L0_attn_W_q"])
    Wo = np.asarray(inp["L0_attn_W_o"])
    qcols = np.concatenate([np.arange(h * 64, (h + 1) * 64) for h in QPERM])
    d["wq"] = f(Wq[:, qcols])
    d["wo"] = f(Wo[qcols, :])
    d["wk"] = f(np.asarray(inp["L0_attn_W_k"]))
    d["wv"] = f(np.asarray(inp["L0_attn_W_v"]))
    w1 = np.asarray(inp["L0_ffn_w1"])
    d["w1xT"] = f(w1[:, :D].T)
    d["w1mT"] = f(w1[:, D:].T)
    d["b1"] = f(np.asarray(inp["L0_ffn_b1"]).reshape(4, 128))
    d["w2T"] = f(np.asarray(inp["L0_ffn_w2"]).T)
    d["b2"] = f(np.asarray(inp["L0_ffn_b2"]).reshape(1, 64))
    d["tau"] = f(np.asarray(inp["L0_ffn_tau"]).reshape(1, 1))
    d["slot_init"] = f(np.asarray(inp["L0_ffn_slot_init"]))
    d["slot_scale"] = f(np.asarray(inp["L0_ffn_slot_scale"]))
    d["wihT"] = f(np.asarray(inp["L0_ffn_gru_wih"]).T)
    d["whhT"] = f(np.asarray(inp["L0_ffn_gru_whh"]).T)
    d["bih"] = f(np.asarray(inp["L0_ffn_gru_bih"]).reshape(1, 3072))
    d["bhh"] = f(np.asarray(inp["L0_ffn_gru_bhh"]).reshape(1, 3072))
    d["wv0T"] = f(np.asarray(inp["L0_ffn_wv"]).T)
    d["bv0"] = f(np.asarray(inp["L0_ffn_bv"]).reshape(1, 1024))
    d["wo0T"] = f(np.asarray(inp["L0_ffn_wo"]).T)
    d["bo0"] = f(np.asarray(inp["L0_ffn_bo"]).reshape(1, 1024))
    d["lng"] = f(np.asarray(inp["L1_attn_ln_gamma"]).reshape(1, 1024))
    d["lnb"] = f(np.asarray(inp["L1_attn_ln_beta"]).reshape(1, 1024))
    d["inpT"] = f(np.asarray(inp["L1_attn_in_proj_w"]).T)
    d["convw"] = f(np.asarray(inp["L1_attn_conv_w"]))
    d["convb"] = f(np.asarray(inp["L1_attn_conv_b"]).reshape(16, 128))
    xp = np.asarray(inp["L1_attn_x_proj_w"])
    xpp = np.zeros((2048, 128), np.float32)
    xpp[:, 0:16] = xp[64 + 16:96, :].T
    xpp[:, 32:48] = xp[64:64 + 16, :].T
    xpp[:, 64:128] = xp[0:64, :].T
    d["xprojTp"] = xpp
    d["dtprojT"] = f(np.asarray(inp["L1_attn_dt_proj_w"]).T)
    d["dtb"] = f(np.asarray(inp["L1_attn_dt_proj_b"]).reshape(16, 128))
    d["Alog"] = f(np.asarray(inp["L1_attn_A_log"]))
    d["Dp"] = f(np.asarray(inp["L1_attn_D_param"]).reshape(16, 128))
    d["outpT"] = f(np.asarray(inp["L1_attn_out_proj_w"]).T)
    d["wfcT"] = f(np.asarray(inp["L1_ffn_w_fc"]).T)
    d["wfcO"] = f(np.asarray(inp["L1_ffn_w_fc"]))
    d["wprojO"] = f(np.asarray(inp["L1_ffn_w_proj"]))
    d["bfcq"] = f(np.asarray(inp["L1_ffn_b_fc"]).reshape(1, 4096))
    d["wprojT"] = f(np.asarray(inp["L1_ffn_w_proj"]).T)
    d["bprojq"] = f(np.asarray(inp["L1_ffn_b_proj"]).reshape(1, 1024))
    d["ident"] = np.eye(128, dtype=np.float32)
    r = np.arange(128)
    d["mask1"] = np.where(r[:, None] >= r[None, :], 0.0, NEG).astype(np.float32)
    mT = np.zeros((4, 128, 512), np.float32)
    for pos in range(4):
        j = np.arange(512)
        mT[pos] = np.where(j[None, :] >= pos * 128 + r[:, None], 0.0, NEG)
    d["maskT"] = mT
    return d


class Arena:

    def __init__(self, handle, nbytes):
        self.h = handle
        self.n = nbytes
        self.off = 0

    def mark(self):
        return self.off

    def reset(self, mark):
        self.off = mark

    def alloc(self, shape, dtype):
        esz = mybir.dt.size(dtype)
        per = int(np.prod(shape[1:])) * esz
        off = (self.off + 31) // 32 * 32
        self.off = off + per
        assert self.off <= self.n, f"arena overflow: {self.off} > {self.n}"
        ap = self.h[:, off:off + per].bitcast(dtype)
        if len(shape) == 2:
            ap = ap[0:shape[0], :]
        else:
            kw = {}
            names = "abcdef"[: len(shape) - 1]
            pat = f"p ({' '.join(names)}) -> p {' '.join(names)}"
            for nm, sz in zip(names, shape[1:]):
                kw[nm] = sz
            ap = ap.rearrange(pat, **kw)[0:shape[0]]
        return ap


def new_buf():
    return Buf()


class Prog:

    def __init__(self):
        self.nc = bass.Bass("TRN2", target_bir_lowering=False)
        self.plan = Plan()
        self.b = B(self.nc, self.plan)
        self.stack = ExitStack()
        self.hbm = {}
        self.out_names = []
        self._ar_h = None
        self._ps_h = None
        self._pstr_h = None

    def arena(self, tag):
        if self._ar_h is None:
            self._ar_h = self.stack.enter_context(
                self.nc.sbuf_tensor("arena", [128, 212200], mybir.dt.uint8))
        return Arena(self._ar_h[:], 212200)

    def psum(self):
        if self._ps_h is None:
            self._ps_h = [self.stack.enter_context(
                self.nc.psum_tensor(f"ps{i}", [128, 512], F32)) for i in range(7)]
            self._pstr_h = self.stack.enter_context(
                self.nc.psum_tensor("pstr", [128, 1024], BF16))
        ps = [(h[:], Buf()) for h in self._ps_h]
        pstr = (self._pstr_h[:], Buf())
        return ps, pstr

    def dram_in(self, name, shape, dtype=F32):
        ap = self.nc.dram_tensor(name, list(shape), dtype, kind="ExternalInput").ap()
        self.hbm[name] = (ap, Buf())
        return ap

    def dram_out(self, name, shape, dtype=F32):
        ap = self.nc.dram_tensor(name, list(shape), dtype, kind="ExternalOutput").ap()
        self.hbm[name] = (ap, Buf())
        self.out_names.append(name)
        return ap

    def dram_tmp(self, name, shape, dtype=F32):
        ap = self.nc.dram_tensor(name, list(shape), dtype, kind="Internal").ap()
        self.hbm[name] = (ap, Buf())
        return ap

    def finish(self):
        b = self.b
        toks = []
        for name, (ap, buf) in self.hbm.items():
            if buf.wr is not None:
                toks.append(buf.wr)
        self.plan.add("sync", None, waits=toks)
        if EPILOGUE:
            self.plan.barrier()
            self.plan.add("gpsimd", "CLEAR_NONBAR")
            self.plan.barrier()
            self.plan.add("gpsimd", "CLEAR_BAR")
        self.plan.check_deadlock()


def build_phase_a(pr, io):
    nc, b = pr.nc, pr.b
    st = pr.stack

    ar = pr.arena("a")

    def A(shape, dtype=F32):
        return (ar.alloc(shape, dtype), Buf())

    wq = A([128, 8, 1024], BF16)
    wk = A([128, 8, 256], BF16)
    wv = A([128, 8, 256], BF16)
    wo2 = A([64, 16, 1024], BF16)
    w1x = A([128, 8, 512], BF16)
    w2 = A([128, 4, 64], BF16)
    idb = A([128, 128], BF16)
    idf = A([128, 128], F32)
    mask1 = A([128, 128], BF16)
    maskT = A([128, 4, 512], BF16)
    b2s = A([128, 64], F32)
    b2r = A([128, 64], F32)
    stau = A([128, 1], F32)
    b1f = A([128, 4], F32)
    beff = A([128, 4], F32)
    sib = A([64, 1024], F32)
    onesr = A([1, 128], BF16)
    onesc = A([128, 1], BF16)
    onesr64 = A([65, 128], BF16)

    kaug = A([65, 4, 4096], BF16)
    vaug = A([128, 32, 260], BF16)
    qfm = A([64, 16, 512], BF16)
    Ofm = A([64, 16, 512], BF16)
    qaug = A([65, 512], BF16)

    xt = A([128, 3, 1024], F32)
    xn_tm = A([128, 4, 1024], BF16)
    xn_fm = A([128, 8, 512], BF16)
    hfm = A([128, 4, 512], BF16)

    mval = A([128, 1], F32); mcol = A([128, 1], F32); ssum = A([128, 1], F32)
    v1 = A([128, 1], F32); lnv = A([128, 1], F32); rstd = A([128, 1], F32)
    lg = A([128, 64], F32); cur = A([128, 64], F32); meq = A([128, 64], F32)
    pex = A([128, 64], F32); m8 = A([128, 8], F32); w8 = A([128, 8], F32)
    negm0 = A([128, 1], F32); ws8 = A([128, 1], F32); rw = A([128, 1], F32)
    alp = A([128, 64], F32); alpb = A([128, 64], BF16)

    _mark = ar.mark()
    w1m = A([128, 8, 512], BF16)
    b1s = A([4, 128], F32)
    sinit = A([64, 1024], F32)
    sscale = A([64, 1], F32)
    hsbf = A([64, 1024], BF16)
    smf = A([128, 8], F32)
    smbf = A([128, 8], BF16)

    ps, pstr = pr.psum()

    def wload(dst, src_name, pat=None, **kw):
        ap, hb = io[src_name]
        src = ap.rearrange(pat, **kw) if pat else ap
        b.dma("gpsimd", lambda g, o=dst[0], s=src: g.dma_start(out=o, in_=s),
              reads=[hb], writes=[dst[1]])

    def fload(dst, src_name, pat=None, bcast=None, **kw):
        ap, hb = io[src_name]
        src = ap.rearrange(pat, **kw) if pat else ap
        if bcast:
            src = src.broadcast_to(bcast)
        b.dma("sync", lambda g, o=dst[0], s=src: g.dma_start(out=o, in_=s),
              reads=[hb], writes=[dst[1]])

    wload(wq, "wq", "(c p) q -> p c q", p=128)
    wload(wk, "wk", "(c p) q -> p c q", p=128)
    wload(wv, "wv", "(c p) q -> p c q", p=128)
    wload(wo2, "wo", "(s c) d -> c s d", c=64)
    wload(w1x, "w1xT", "(c p) q -> p c q", p=128)
    wload(w1m, "w1mT", "(c p) q -> p c q", p=128)
    wload(w2, "w2T", "(c p) q -> p c q", p=128)
    wload(idb, "ident")
    wload(mask1, "mask1")
    wload(maskT, "maskT", "a p q -> p a q")
    fload(idf, "ident")
    fload(b2r, "b2", bcast=[128, 64])
    fload(stau, "tau", bcast=[128, 1])
    fload(sinit, "slot_init")
    fload(sscale, "slot_scale")
    fload(b1s, "b1")

    b.v(lambda v: v.memset(kaug[0][64:65, :, :], 1.0), writes=[kaug[1]])
    b.v(lambda v: v.memset(
        vaug[0].rearrange("p t (k c) -> p t k c", c=65)[:, :, :, 64:65], 1.0),
        writes=[vaug[1]])
    b.v(lambda v: v.memset(onesr[0], 1.0), writes=[onesr[1]])
    b.v(lambda v: v.memset(onesc[0], 1.0), writes=[onesc[1]])
    b.v(lambda v: v.memset(onesr64[0][64:65, :], 1.0), writes=[onesr64[1]])
    b.v(lambda v: v.memset(sib[0], 0.0), writes=[sib[1]])

    b.s(lambda sc: sc.activation(stau[0], stau[0], AF.Abs),
        reads=[stau[1]], writes=[stau[1]])
    b.v(lambda v: v.tensor_scalar(stau[0], stau[0], 0.1, None, ALU.add),
        reads=[stau[1]], writes=[stau[1]])
    b.v(lambda v: v.reciprocal(stau[0], stau[0]), reads=[stau[1]], writes=[stau[1]])
    b.v(lambda v: v.tensor_scalar(b2s[0], b2r[0], stau[0], None, ALU.mult),
        reads=[b2r[1], stau[1]], writes=[b2s[1]])
    b.v(lambda v: v.tensor_scalar(sinit[0], sinit[0], sscale[0][0:64], None, ALU.mult),
        reads=[sinit[1], sscale[1]], writes=[sinit[1]])
    b.v(lambda v: v.tensor_copy(hsbf[0], sinit[0]), reads=[sinit[1]], writes=[hsbf[1]])
    for dc in range(8):
        b.t(lambda t, dc=dc: t.matmul(ps[0][0][:, dc:dc + 1],
                                      hsbf[0][0:64, dc * 128:(dc + 1) * 128],
                                      onesc[0][0:64, :], start=True, stop=True),
            reads=[hsbf[1], onesc[1]], writes=[ps[0][1]], guard=2000)
    b.v(lambda v: v.tensor_scalar(smf[0], ps[0][0][:, 0:8], 1.0 / 64, None, ALU.mult),
        reads=[ps[0][1]], writes=[smf[1]])
    b.v(lambda v: v.tensor_copy(smbf[0], smf[0]), reads=[smf[1]], writes=[smbf[1]])
    b.t(lambda t: t.transpose(ps[1][0][0:128, 0:4], b1s[0], idf[0][0:4, 0:4]),
        reads=[b1s[1], idf[1]], writes=[ps[1][1]], guard=2000)
    beffr = A([1, 512], F32)
    for dc in range(8):
        pp = ps[2 + dc % 2]
        b.t(lambda t, dc=dc, pp=pp: t.matmul(
            pp[0][0:1, :], smbf[0][:, dc:dc + 1], w1m[0][:, dc, :],
            start=True, stop=True),
            reads=[w1m[1], smbf[1]], writes=[pp[1]], guard=2000)
        if dc == 0:
            b.v(lambda v, pp=pp: v.tensor_copy(beffr[0], pp[0][0:1, :]),
                reads=[pp[1]], writes=[beffr[1]])
        else:
            b.v(lambda v, pp=pp: v.tensor_tensor(beffr[0], beffr[0], pp[0][0:1, :], ALU.add),
                reads=[pp[1], beffr[1]], writes=[beffr[1]])
    for jc in range(4):
        b.t(lambda t, jc=jc: t.transpose(
            ps[3 + jc][0][0:128, 0:1], beffr[0][0:1, jc * 128:(jc + 1) * 128],
            idf[0][0:1, 0:1]),
            reads=[beffr[1], idf[1]], writes=[ps[3 + jc][1]], guard=2000)
    for jc in range(4):
        b.v(lambda v, jc=jc: v.tensor_copy(beff[0][:, jc:jc + 1], ps[3 + jc][0][:, 0:1]),
            reads=[ps[3 + jc][1]], writes=[beff[1]])
    b.v(lambda v: v.tensor_copy(b1f[0], ps[1][0][0:128, 0:4]),
        reads=[ps[1][1]], writes=[b1f[1]])
    b.v(lambda v: v.tensor_tensor(beff[0], beff[0], b1f[0], ALU.add),
        reads=[b1f[1], beff[1]], writes=[beff[1]])
    gate = beff[1].wr
    ar.reset(_mark)

    def AG(shape, dtype=F32):
        bb = Buf()
        bb.wr = gate
        return (ar.alloc(shape, dtype), bb)

    mrowP = [AG([128, 16], F32), AG([128, 16], F32)]
    mtP = [AG([128, 65], F32), AG([128, 65], F32)]
    dscr1 = AG([128, 128], F32)
    dscrP = [dscr1, dscr1]
    ptile = [AG([128, 512], BF16) for _ in range(4)]
    qaugP = [qaug, (ar.alloc([65, 512], BF16), Buf())]
    rlbf1 = AG([65, 512], BF16)
    rlb1 = AG([64, 512], BF16)
    rlbfP = [rlbf1, rlbf1]
    rlbP = [rlb1, rlb1]
    accs = AG([128, 2, 1024], F32)
    accb = [accs[1], Buf()]
    accb[1].wr = gate
    qaugP[1][1].wr = gate
    for P in range(2):
        b.v(lambda v, P=P: v.memset(mtP[P][0], 0.0), reads=(), writes=[mtP[P][1]])

    x_ap, x_hb = io["x"]
    acc_ap, acc_hb = io["acc"]
    alpha_ap, alpha_hb = io["alpha_h"]

    xtB = [Buf(), Buf(), Buf()]

    def emit_ln_loads(s2, tbs=(0, 1, 2)):
        for tb in tbs:
            gb = 4 * s2 + tb
            b.dma("sync", lambda g, gb=gb, tb=tb: g.dma_start(
                out=xt[0][:, tb % 3, :], in_=x_ap[gb * 128:(gb + 1) * 128, :]),
                reads=[x_hb], writes=[xtB[tb % 3]])

    def ln_ops(s2, tponly=False):
        ops = []

        def pre(tb):
            def f(tb=tb):
                b.v(lambda v, tb=tb: v.reduce_sum(mval[0], xt[0][:, tb % 3, :], AX.X),
                    reads=[xtB[tb % 3]], writes=[mval[1]])
                b.v(lambda v: v.tensor_scalar(mcol[0], mval[0], 1.0 / D, None, ALU.mult),
                    reads=[mval[1]], writes=[mcol[1]])
                b.v(lambda v: v.tensor_scalar(mval[0], mcol[0], -1.0, None, ALU.mult),
                    reads=[mcol[1]], writes=[mval[1]])
                b.s(lambda sc, tb=tb: sc.activation(xn_tm[0][:, tb, :], xt[0][:, tb % 3, :],
                                                    AF.Square, bias=mval[0], accum_out=ssum[0]),
                    reads=[xtB[tb % 3], mval[1]], writes=[xn_tm[1], ssum[1]])
                b.v(lambda v: v.tensor_scalar(v1[0], ssum[0], 1.0 / D, 1e-5, ALU.mult, ALU.add),
                    reads=[ssum[1]], writes=[v1[1]])
                b.s(lambda sc: sc.activation(lnv[0], v1[0], AF.Ln),
                    reads=[v1[1]], writes=[lnv[1]])
                b.s(lambda sc: sc.activation(rstd[0], lnv[0], AF.Exp, scale=-0.5),
                    reads=[lnv[1]], writes=[rstd[1]])
                b.v(lambda v, tb=tb: v.tensor_scalar(xn_tm[0][:, tb, :], xt[0][:, tb % 3, :],
                                                     mcol[0], rstd[0], ALU.subtract, ALU.mult),
                    reads=[xtB[tb % 3], mcol[1], rstd[1]], writes=[xn_tm[1]])
            ops.append(f)

        def tpb(tb):
            def f(tb=tb):
                for dc in range(8):
                    b.t(lambda t, tb=tb, dc=dc: t.transpose(
                        pstr[0][:, dc * 128:(dc + 1) * 128],
                        xn_tm[0][:, tb, dc * 128:(dc + 1) * 128], idb[0]),
                        reads=[xn_tm[1], idb[1]], writes=[pstr[1]],
                        guard=(256 if dc == 7 else 0))
                b.v(lambda v, tb=tb: v.tensor_copy(
                    xn_fm[0][:, :, tb * 128:(tb + 1) * 128],
                    pstr[0].rearrange("p (c q) -> p c q", q=128)),
                    reads=[pstr[1]], writes=[xn_fm[1]])
            ops.append(f)

        if tponly:
            tpb(0)
            tpb(1)
            tpb(2)
            tpb(3)
        else:
            pre(0)
            ops.append(lambda: emit_ln_loads(s2, tbs=(3,)))
            pre(1)
            pre(2)
            pre(3)
        return ops

    emit_ln_loads(0)
    for f in ln_ops(0):
        f()
    for f in ln_ops(0, tponly=True):
        f()
    for s in range(NSUP):
        if s > 0:
            if LN_PRE_TOP:
                for f in ln_ops(s):
                    f()
            for f in ln_ops(s, tponly=True):
                f()
        for slot in range(16):
            pp = ps[slot % 2]
            for dc in range(8):
                b.t(lambda t, slot=slot, dc=dc, pp=pp: t.matmul(
                    pp[0][0:64, :], wq[0][:, dc, slot * 64:(slot + 1) * 64],
                    xn_fm[0][:, dc, :], start=(dc == 0), stop=(dc == 7)),
                    reads=[wq[1], xn_fm[1]], writes=[pp[1]])
            b.v(lambda v, slot=slot, pp=pp: v.tensor_scalar(
                qfm[0][0:64, slot, :], pp[0][0:64, :], 0.125, None, ALU.mult),
                reads=[pp[1]], writes=[qfm[1]])
        for kv in range(4):
            pp = ps[2 + kv % 2]
            for dc in range(8):
                b.t(lambda t, kv=kv, dc=dc, pp=pp: t.matmul(
                    pp[0][0:64, :], wk[0][:, dc, kv * 64:(kv + 1) * 64],
                    xn_fm[0][:, dc, :], start=(dc == 0), stop=(dc == 7)),
                    reads=[wk[1], xn_fm[1]], writes=[pp[1]])
            b.v(lambda v, kv=kv, s=s, pp=pp: v.tensor_copy(
                kaug[0][0:64, kv, s * 512:(s + 1) * 512], pp[0][0:64, :]),
                reads=[pp[1]], writes=[kaug[1]])
        for tb in range(4):
            pp = ps[4 + tb % 2]
            for dc in range(8):
                b.t(lambda t, tb=tb, dc=dc, pp=pp: t.matmul(
                    pp[0][:, 0:256], xn_fm[0][:, dc, tb * 128:(tb + 1) * 128],
                    wv[0][:, dc, :], start=(dc == 0), stop=(dc == 7)),
                    reads=[xn_fm[1], wv[1]], writes=[pp[1]], guard=(384 if dc == 7 else 0))
            b.v(lambda v, s=s, tb=tb, pp=pp: v.tensor_copy(
                vaug[0].rearrange("p t (k c) -> p t k c", c=65)[:, s * 4 + tb, :, 0:64],
                pp[0][:, 0:256].rearrange("p (k c) -> p k c", c=64)),
                reads=[pp[1]], writes=[vaug[1]])
        for hc in range(4):
            pp = ps[hc % 2]
            for dc in range(8):
                b.t(lambda t, hc=hc, dc=dc, pp=pp: t.matmul(
                    pp[0], w1x[0][:, dc, hc * 128:(hc + 1) * 128], xn_fm[0][:, dc, :],
                    start=(dc == 0), stop=(dc == 7)),
                    reads=[w1x[1], xn_fm[1]], writes=[pp[1]])
            b.s(lambda sc, hc=hc, pp=pp: sc.activation(
                hfm[0][:, hc, :], pp[0], AF.Gelu, bias=beff[0][:, hc:hc + 1]),
                reads=[pp[1], beff[1]], writes=[hfm[1]])
        for tb in range(4):
            gb = 4 * s + tb
            pp = ps[2 + tb % 2]
            for hc in range(4):
                b.t(lambda t, tb=tb, hc=hc, pp=pp: t.matmul(
                    pp[0][:, 0:64], hfm[0][:, hc, tb * 128:(tb + 1) * 128],
                    w2[0][:, hc, :], start=(hc == 0), stop=(hc == 3)),
                    reads=[hfm[1], w2[1]], writes=[pp[1]], guard=(384 if hc == 3 else 0))
            b.v(lambda v, pp=pp: v.scalar_tensor_tensor(
                lg[0], pp[0][:, 0:64], stau[0], b2s[0], ALU.mult, ALU.add),
                reads=[pp[1], stau[1], b2s[1]], writes=[lg[1]])
            b.v(lambda v: v.max(m8[0], lg[0]), reads=[lg[1]], writes=[m8[1]])
            b.v(lambda v: v.match_replace(cur[0], m8[0], lg[0], NEG),
                reads=[m8[1], lg[1]], writes=[cur[1]])
            b.v(lambda v: v.tensor_scalar(negm0[0], m8[0][:, 0:1], -1.0, None, ALU.mult),
                reads=[m8[1]], writes=[negm0[1]])
            b.s(lambda sc: sc.activation(w8[0], m8[0], AF.Exp, bias=negm0[0], accum_out=ws8[0]),
                reads=[m8[1], negm0[1]], writes=[w8[1], ws8[1]])
            b.s(lambda sc: sc.activation(pex[0], lg[0], AF.Exp, bias=negm0[0]),
                reads=[lg[1], negm0[1]], writes=[pex[1]])
            b.v(lambda v: v.tensor_scalar(meq[0], cur[0], NEG, None, ALU.is_equal),
                reads=[cur[1]], writes=[meq[1]])
            b.v(lambda v: v.reciprocal(rw[0], ws8[0]), reads=[ws8[1]], writes=[rw[1]])
            b.v(lambda v: v.tensor_tensor(alp[0], pex[0], meq[0], ALU.mult),
                reads=[pex[1], meq[1]], writes=[alp[1]])
            b.v(lambda v: v.tensor_scalar(alp[0], alp[0], rw[0], None, ALU.mult),
                reads=[alp[1], rw[1]], writes=[alp[1]])
            b.v(lambda v: v.tensor_copy(alpb[0], alp[0]), reads=[alp[1]], writes=[alpb[1]])
            b.dma("sync", lambda g, gb=gb: g.dma_start(
                out=alpha_ap[gb * 128:(gb + 1) * 128, :], in_=alp[0]),
                reads=[alp[1]], writes=[alpha_hb])
            if s == 0 and tb == 0 and "dbg" in io:
                dbg_ap, dbg_hb = io["dbg"]
                b.dma("sync", lambda g: g.dma_start(out=dbg_ap[:, 0:64], in_=lg[0]),
                      reads=[lg[1]], writes=[dbg_hb])
                b.dma("sync", lambda g: g.dma_start(out=dbg_ap[:, 64:72], in_=m8[0]),
                      reads=[m8[1]], writes=[dbg_hb])
                b.dma("sync", lambda g: g.dma_start(out=dbg_ap[:, 72:76], in_=beff[0]),
                      reads=[beff[1]], writes=[dbg_hb])
                b.v(lambda v: v.tensor_copy(mrowP[0][0][:, 8:9], stau[0]),
                    reads=[stau[1]], writes=[mrowP[0][1]])
                b.v(lambda v: v.tensor_copy(mrowP[0][0][:, 9:10], ws8[0]),
                    reads=[ws8[1]], writes=[mrowP[0][1]])
                b.dma("sync", lambda g: g.dma_start(out=dbg_ap[:, 76:78], in_=mrowP[0][0][:, 8:10]),
                      reads=[mrowP[0][1]], writes=[dbg_hb])
                b.dma("sync", lambda g: g.dma_start(out=dbg_ap[:, 78:142], in_=b2s[0]),
                      reads=[b2s[1]], writes=[dbg_hb])
                b.dma("sync", lambda g: g.dma_start(out=dbg_ap[:, 142:206], in_=meq[0]),
                      reads=[meq[1]], writes=[dbg_hb])
                b.dma("gpsimd", lambda g: g.dma_start(out=dbg_ap[:, 206:334], in_=xn_tm[0][:, 0, 0:128]),
                      reads=[xn_tm[1]], writes=[dbg_hb])
                b.dma("sync", lambda g: g.dma_start(out=dbg_ap[:, 590:598], in_=smf[0]),
                      reads=[smf[1]], writes=[dbg_hb])
                b.dma("sync", lambda g: g.dma_start(out=dbg_ap[:, 598:602], in_=b1f[0]),
                      reads=[b1f[1]], writes=[dbg_hb])
                b.dma("gpsimd", lambda g: g.dma_start(out=dbg_ap[:, 334:462], in_=hfm[0][:, 0, 0:128]),
                      reads=[hfm[1]], writes=[dbg_hb])
                b.dma("gpsimd", lambda g: g.dma_start(out=dbg_ap[:, 462:590], in_=xn_fm[0][:, 0, 0:128]),
                      reads=[xn_fm[1]], writes=[dbg_hb])
            for half in range(2):
                b.t(lambda t, tb=tb, half=half: t.matmul(
                    ps[5 + half][0][0:64, :], alpb[0],
                    xn_tm[0][:, tb, half * 512:(half + 1) * 512],
                    start=(tb == 0), stop=(tb == 3)),
                    reads=[alpb[1], xn_tm[1]], writes=[ps[5 + half][1]],
                    guard=(256 if tb == 3 else 0))
        for half in range(2):
            b.v(lambda v, half=half: v.tensor_tensor(
                sib[0][:, half * 512:(half + 1) * 512],
                sib[0][:, half * 512:(half + 1) * 512],
                ps[5 + half][0][0:64, :], ALU.add),
                reads=[ps[5 + half][1], sib[1]], writes=[sib[1]])

        def pass1_ops(slot, s=s):
            P = slot % 2
            kv = KVH[slot]
            mrow, qg, dscr = mrowP[P], qaugP[P], dscrP[P]
            rot = [2 * P]
            ops = []

            def mmtile(tb, ci):
                pp = ps[2 * P + rot[0] % 2]
                rot[0] += 1

                def f(tb=tb, ci=ci, pp=pp):
                    b.t(lambda t, slot=slot, tb=tb, ci=ci, kv=kv, pp=pp: t.matmul(
                        pp[0], qfm[0][0:64, slot, tb * 128:(tb + 1) * 128],
                        kaug[0][0:64, kv, ci * 512:(ci + 1) * 512], start=True, stop=True),
                        reads=[qfm[1], kaug[1]], writes=[pp[1]])
                    b.v(lambda v, ci=ci, pp=pp, mrow=mrow: v.reduce_max(
                        mrow[0][:, ci:ci + 1], pp[0], AX.X),
                        reads=[pp[1]], writes=[mrow[1]])
                ops.append(f)

            def mmdiag(tb, ncol):
                pp = ps[2 * P + rot[0] % 2]
                rot[0] += 1
                mt = mtP[tb % 2]

                def f(tb=tb, ncol=ncol, pp=pp, mt=mt):
                    w = (tb + 1) * 128
                    nc2 = ncol
                    b.t(lambda t, slot=slot, tb=tb, s=s, kv=kv, w=w, pp=pp: t.matmul(
                        pp[0][:, 0:w], qfm[0][0:64, slot, tb * 128:(tb + 1) * 128],
                        kaug[0][0:64, kv, s * 512:s * 512 + w], start=True, stop=True),
                        reads=[qfm[1], kaug[1]], writes=[pp[1]])
                    if tb > 0:
                        b.v(lambda v, nc2=nc2, tb=tb, pp=pp, mrow=mrow: v.reduce_max(
                            mrow[0][:, nc2:nc2 + 1], pp[0][:, 0:tb * 128], AX.X),
                            reads=[pp[1]], writes=[mrow[1]])
                        nc2 += 1
                    b.v(lambda v, tb=tb, w=w, pp=pp, dscr=dscr: v.tensor_tensor(
                        dscr[0], pp[0][:, tb * 128:w], mask1[0], ALU.add),
                        reads=[pp[1], mask1[1]], writes=[dscr[1]])
                    b.v(lambda v, nc2=nc2, mrow=mrow, dscr=dscr: v.reduce_max(
                        mrow[0][:, nc2:nc2 + 1], dscr[0], AX.X),
                        reads=[dscr[1]], writes=[mrow[1]])
                    b.v(lambda v, nc2=nc2, mrow=mrow, mt=mt: v.reduce_max(
                        mt[0][:, 64:65], mrow[0][:, 0:nc2 + 1], AX.X, negate=True),
                        reads=[mrow[1]], writes=[mt[1]])
                ops.append(f)

            def tp(tb):
                mt = mtP[tb % 2]

                def f(tb=tb, mt=mt):
                    pt = ps[4]
                    b.t(lambda t, pt=pt, mt=mt: t.transpose(pt[0][0:65, 0:128], mt[0], idf[0]),
                        reads=[mt[1], idf[1]], writes=[pt[1]])
                    b.s(lambda sc, tb=tb, pt=pt, qg=qg: sc.copy(
                        qg[0][64:65, tb * 128:(tb + 1) * 128], pt[0][64:65, 0:128]),
                        reads=[pt[1]], writes=[qg[1]])
                ops.append(f)

            for tb in range(4):
                for ci in range(s):
                    mmtile(tb, ci)
                mmdiag(tb, s)
                if tb >= 1:
                    tp(tb - 1)
            tp(3)

            def fq():
                b.v(lambda v, slot=slot, qg=qg: v.tensor_copy(
                    qg[0][0:64, :], qfm[0][0:64, slot, :]),
                    reads=[qfm[1]], writes=[qg[1]])
            ops.append(fq)
            return ops

        def pass2_ops(slot, s=s):
            P = slot % 2
            kv = KVH[slot]
            qg = qaugP[P]
            av = ps[6 - P]
            nblk = 4 * (s + 1)
            ops = []

            def st(gtk):
                def f(gtk=gtk):
                    pp = ps[2 * P + gtk % 2]
                    b.t(lambda t, gtk=gtk, kv=kv, pp=pp, qg=qg: t.matmul(
                        pp[0], kaug[0][:, kv, gtk * 128:(gtk + 1) * 128], qg[0],
                        start=True, stop=True),
                        reads=[kaug[1], qg[1]], writes=[pp[1]])
                    if gtk >= 4 * s:
                        pos = gtk - 4 * s
                        b.v(lambda v, pos=pos, pp=pp: v.tensor_tensor(
                            pp[0], pp[0], maskT[0][:, pos, :], ALU.add),
                            reads=[pp[1], maskT[1]], writes=[pp[1]])
                    pt = ptile[2 * P + gtk % 2]
                    b.s(lambda sc, pp=pp, pt=pt: sc.activation(pt[0], pp[0], AF.Exp),
                        reads=[pp[1]], writes=[pt[1]])
                ops.append(f)

            def avm(gtk):
                def f(gtk=gtk):
                    pt = ptile[2 * P + gtk % 2]
                    b.t(lambda t, gtk=gtk, kv=kv, nblk=nblk, pt=pt, av=av: t.matmul(
                        av[0][0:65, :], vaug[0][:, gtk, kv * 65:(kv + 1) * 65], pt[0],
                        start=(gtk == 0), stop=(gtk == nblk - 1)),
                        reads=[vaug[1], pt[1]], writes=[av[1]])
                ops.append(f)

            for gtk in range(nblk):
                st(gtk)
                if gtk >= 1:
                    avm(gtk - 1)
            avm(nblk - 1)
            return ops

        def weave(main, extra):
            if not extra:
                for f in main:
                    f()
                return
            n, m = len(main), len(extra)
            j = 0
            for i, f in enumerate(main):
                f()
                while j * n <= (i + 1) * m and j < m:
                    extra[j]()
                    j += 1
            while j < m:
                extra[j]()
                j += 1

        def emit_epi(slot, s=s):
            P = slot % 2
            av = ps[6 - P]
            rlbf, rlb = rlbfP[P], rlbP[P]
            b.s(lambda sc, av=av, rlbf=rlbf: sc.copy(
                rlbf[0][64:65, :], av[0][64:65, :]),
                reads=[av[1]], writes=[rlbf[1]])
            b.t(lambda t, rlbf=rlbf: t.matmul(ps[4][0][0:64, :], onesr64[0][64:65, 0:64],
                                              rlbf[0][64:65, :], start=True, stop=True),
                reads=[onesr64[1], rlbf[1]], writes=[ps[4][1]])

            def _recip(v, rlb=rlb):
                with nc.allow_low_precision(reason="1/l feeds bf16 mult"):
                    return v.reciprocal(rlb[0], ps[4][0][0:64, :])
            b.v(_recip, reads=[ps[4][1]], writes=[rlb[1]])
            b.v(lambda v, slot=slot, av=av, rlb=rlb: v.tensor_tensor(
                Ofm[0][0:64, slot, :], av[0][0:64, :], rlb[0], ALU.mult),
                reads=[av[1], rlb[1]], writes=[Ofm[1]])
            if s == 0 and slot == 0 and "dbg2" in io:
                d2_ap, d2_hb = io["dbg2"]
                b.dma("gpsimd", lambda g: g.dma_start(out=d2_ap[0:64, 0:512], in_=Ofm[0][0:64, 0, :]),
                      reads=[Ofm[1]], writes=[d2_hb])
                b.dma("gpsimd", lambda g, rlbf=rlbf: g.dma_start(out=d2_ap[0:65, 512:1024], in_=rlbf[0]),
                      reads=[rlbf[1]], writes=[d2_hb])
                b.dma("gpsimd", lambda g: g.dma_start(out=d2_ap[0:65, 1024:1536], in_=qaugP[0][0]),
                      reads=[qaugP[0][1]], writes=[d2_hb])
                b.dma("gpsimd", lambda g: g.dma_start(out=d2_ap[0:64, 1536:2048], in_=kaug[0][0:64, 0, 0:512]),
                      reads=[kaug[1]], writes=[d2_hb])
                b.dma("gpsimd", lambda g: g.dma_start(out=d2_ap[64:128, 0:512], in_=qfm[0][0:64, 0, :]),
                      reads=[qfm[1]], writes=[d2_hb])
                b.dma("gpsimd", lambda g: g.dma_start(
                    out=d2_ap[0:64, 2241:2305], in_=vaug[0][0:64, 0, 0:64]),
                      reads=[vaug[1]], writes=[d2_hb])

        if s + 1 < NSUP:
            emit_ln_loads(s + 1)
        for f in pass1_ops(0):
            f()
        pend = None
        for slot in range(16):
            p2 = pass2_ops(slot)
            extra = []
            if pend is not None:
                extra.append(pend)
            if slot < 15:
                extra.extend(pass1_ops(slot + 1))
            weave(p2, extra)
            if EPI_DELAY:
                pend = (lambda sl=slot: emit_epi(sl))
            else:
                emit_epi(slot)
        if pend is not None:
            pend()
        if (not LN_PRE_TOP) and s + 1 < NSUP:
            for f in ln_ops(s + 1):
                f()
        for tb in range(4):
            gb = 4 * s + tb
            b.dma("sync", lambda g, gb=gb, tb=tb: g.dma_start(
                out=accs[0][:, tb % 2, :], in_=x_ap[gb * 128:(gb + 1) * 128, :]),
                reads=[x_hb], writes=[accb[tb % 2]])
            for oc in range(2):
                pp = ps[(tb % 2) * 2 + oc]
                for slot in range(16):
                    b.t(lambda t, tb=tb, oc=oc, slot=slot, pp=pp: t.matmul(
                        pp[0], Ofm[0][0:64, slot, tb * 128:(tb + 1) * 128],
                        wo2[0][0:64, slot, oc * 512:(oc + 1) * 512],
                        start=(slot == 0), stop=(slot == 15)),
                        reads=[Ofm[1], wo2[1]], writes=[pp[1]])
                b.v(lambda v, tb=tb, oc=oc, pp=pp: v.tensor_tensor(
                    accs[0][:, tb % 2, oc * 512:(oc + 1) * 512],
                    accs[0][:, tb % 2, oc * 512:(oc + 1) * 512],
                    pp[0], ALU.add),
                    reads=[pp[1], accb[tb % 2]], writes=[accb[tb % 2]])
            b.dma("sync", lambda g, gb=gb, tb=tb: g.dma_start(
                out=acc_ap[gb * 128:(gb + 1) * 128, :], in_=accs[0][:, tb % 2, :]),
                reads=[accb[tb % 2]], writes=[acc_hb])

    si_ap, si_hb = io["slotin_h"]
    b.dma("sync", lambda g: g.dma_start(out=si_ap, in_=sib[0]),
          reads=[sib[1]], writes=[si_hb])
    return ar.off


LN_INLINE = False
EPI_DELAY = True
LN_PRE_TOP = True


WN_A = ["wq", "wk", "wv", "wo", "w1xT", "w1mT", "b1", "w2T", "b2", "tau",
        "slot_init", "slot_scale", "ident", "mask1", "maskT"]


def make_prog_a(debug=False):
    pr = Prog()
    w_shapes = {
        "wq": [1024, 1024], "wk": [1024, 256], "wv": [1024, 256], "wo": [1024, 1024],
        "w1xT": [1024, 512], "w1mT": [1024, 512], "b1": [4, 128], "w2T": [512, 64],
        "b2": [1, 64], "tau": [1, 1], "slot_init": [64, 1024], "slot_scale": [64, 1],
        "ident": [128, 128], "mask1": [128, 128], "maskT": [4, 128, 512],
    }
    for n in WN_A:
        pr.dram_in(n, w_shapes[n])
    pr.dram_in("x", [T, D])
    pr.dram_out("acc", [T, D])
    pr.dram_out("alpha_h", [T, 64])
    pr.dram_out("slotin_h", [64, 1024])
    if debug:
        pr.dram_out("dbg", [128, 640])
        pr.dram_out("dbg2", [128, 2816])
    build_phase_a(pr, pr.hbm)
    pr.finish()
    emit_block(pr.nc, pr.plan, pr.stack)
    return pr


def run_launch(pr, in_maps, core_ids=None):
    if core_ids is None:
        core_ids = list(range(len(in_maps)))
    res = run_bass_kernel_spmd(pr.nc, in_maps, core_ids=core_ids)
    return res.results


def build_phase_b(pr, io):
    nc, b = pr.nc, pr.b
    st = pr.stack
    ar = pr.arena("b")

    def A(shape, dtype=F32):
        return (ar.alloc(shape, dtype), Buf())

    wih = A([128, 8, 3072], BF16)
    whh = A([128, 8, 3072], BF16)
    wv0 = A([128, 8, 1024], BF16)
    wo0 = A([128, 8, 1024], BF16)
    bih = A([1, 3072], BF16)
    bhh = A([1, 3072], BF16)
    bv0 = A([1, 1024], BF16)
    bo0 = A([1, 1024], BF16)
    idb = A([128, 128], BF16)
    onesc = A([128, 1], BF16)
    onesr = A([1, 128], BF16)
    sinit = A([64, 1024], F32)
    sscale = A([64, 1], F32)
    hs = A([64, 1024], F32)
    hsbf = A([64, 1024], BF16)
    slotin = A([64, 1024], F32)
    at = A([128, 64], BF16)
    alphaT = A([64, 4096], BF16)
    wsum = A([64, 1], F32)
    gi = A([64, 1024], F32)
    gibf = A([64, 1024], BF16)
    giT = A([128, 8, 64], BF16)
    hsT = A([128, 8, 64], BF16)
    snT = A([128, 8, 64], BF16)
    sv = A([64, 1024], BF16)
    _markb = ar.mark()
    gxp = A([64, 1024], F32)
    ghp = A([64, 1024], F32)
    r_t = A([64, 1024], F32)
    z_t = A([64, 1024], F32)
    n_t = A([64, 1024], F32)

    ps, pstr = pr.psum()

    def wload(dst, src_name, pat=None, **kw):
        ap, hb = io[src_name]
        src = ap.rearrange(pat, **kw) if pat else ap
        b.dma("gpsimd", lambda g, o=dst[0], s=src: g.dma_start(out=o, in_=s),
              reads=[hb], writes=[dst[1]])

    def fload(dst, src_name, pat=None, bcast=None, sl=None, **kw):
        ap, hb = io[src_name]
        src = ap.rearrange(pat, **kw) if pat else ap
        if sl is not None:
            src = src[sl]
        if bcast:
            src = src.broadcast_to(bcast)
        b.dma("sync", lambda g, o=dst[0], s=src: g.dma_start(out=o, in_=s),
              reads=[hb], writes=[dst[1]])

    wload(wih, "wihT", "(c p) q -> p c q", p=128)
    wload(whh, "whhT", "(c p) q -> p c q", p=128)
    wload(wv0, "wv0T", "(c p) q -> p c q", p=128)
    wload(wo0, "wo0T", "(c p) q -> p c q", p=128)
    wload(bih, "bih"); wload(bhh, "bhh"); wload(bv0, "bv0"); wload(bo0, "bo0")
    wload(idb, "ident")
    fload(sinit, "slot_init"); fload(sscale, "slot_scale")
    fload(slotin, "slotin_h")
    b.v(lambda v: v.memset(onesc[0], 1.0), writes=[onesc[1]])
    b.v(lambda v: v.memset(onesr[0], 1.0), writes=[onesr[1]])

    alpha_ap, alpha_hb = io["alpha_h"]
    acc_ap, acc_hb = io["acc"]
    x1_ap, x1_hb = io["x1"]

    for i in range(NBLK):
        b.dma("gpsimd", lambda g, i=i: g.dma_start(
            out=at[0], in_=alpha_ap[i * 128:(i + 1) * 128, :]),
            reads=[alpha_hb], writes=[at[1]])
        b.t(lambda t, i=i: t.matmul(ps[0][0][0:64, 0:1], at[0], onesc[0],
                                    start=True, stop=True),
            reads=[at[1], onesc[1]], writes=[ps[0][1]], guard=512)
        if i == 0:
            b.v(lambda v: v.tensor_copy(wsum[0], ps[0][0][0:64, 0:1]),
                reads=[ps[0][1]], writes=[wsum[1]])
        else:
            b.v(lambda v: v.tensor_tensor(wsum[0], wsum[0], ps[0][0][0:64, 0:1], ALU.add),
                reads=[ps[0][1], wsum[1]], writes=[wsum[1]])
        b.t(lambda t, i=i: t.transpose(pstr[0][0:64, 0:128], at[0], idb[0]),
            reads=[at[1], idb[1]], writes=[pstr[1]], guard=384)
        b.v(lambda v, i=i: v.tensor_copy(alphaT[0][0:64, i * 128:(i + 1) * 128],
                                         pstr[0][0:64, 0:128]),
            reads=[pstr[1]], writes=[alphaT[1]])
    b.v(lambda v: v.tensor_scalar(wsum[0], wsum[0], 1e-8, None, ALU.add),
        reads=[wsum[1]], writes=[wsum[1]])
    b.v(lambda v: v.reciprocal(wsum[0], wsum[0]), reads=[wsum[1]], writes=[wsum[1]])
    b.v(lambda v: v.tensor_scalar(gi[0], slotin[0], wsum[0][0:64], None, ALU.mult),
        reads=[slotin[1], wsum[1]], writes=[gi[1]])
    b.v(lambda v: v.tensor_copy(gibf[0], gi[0]), reads=[gi[1]], writes=[gibf[1]])
    b.v(lambda v: v.tensor_scalar(hs[0], sinit[0], sscale[0][0:64], None, ALU.mult),
        reads=[sinit[1], sscale[1]], writes=[hs[1]])
    b.v(lambda v: v.tensor_copy(hsbf[0], hs[0]), reads=[hs[1]], writes=[hsbf[1]])
    for dc in range(8):
        b.t(lambda t, dc=dc: t.transpose(pstr[0][:, 0:64], gibf[0][0:64, dc * 128:(dc + 1) * 128],
                                         idb[0][0:64, 0:64]),
            reads=[gibf[1], idb[1]], writes=[pstr[1]], guard=384)
        b.v(lambda v, dc=dc: v.tensor_copy(giT[0][:, dc, :], pstr[0][:, 0:64]),
            reads=[pstr[1]], writes=[giT[1]])
        b.t(lambda t, dc=dc: t.transpose(pstr[0][:, 64:128], hsbf[0][0:64, dc * 128:(dc + 1) * 128],
                                         idb[0][0:64, 0:64]),
            reads=[hsbf[1], idb[1]], writes=[pstr[1]], guard=384)
        b.v(lambda v, dc=dc: v.tensor_copy(hsT[0][:, dc, :], pstr[0][:, 64:128]),
            reads=[pstr[1]], writes=[hsT[1]])
    for part, dst in ((0, r_t), (1, z_t), (2, n_t)):
        for which, src2, wmat, bias, acc2 in ((0, giT, wih, bih, gxp), (1, hsT, whh, bhh, ghp)):
            for nch in range(2):
                pp = ps[nch % 2]
                off = part * 1024 + nch * 512
                b.t(lambda t, off=off, bias=bias, pp=pp: t.matmul(
                    pp[0][0:64, :], onesr[0][0:1, 0:64], bias[0][0:1, off:off + 512],
                    start=True, stop=False),
                    reads=[onesr[1], bias[1]], writes=[pp[1]])
                for dc in range(8):
                    b.t(lambda t, off=off, dc=dc, src2=src2, wmat=wmat, pp=pp: t.matmul(
                        pp[0][0:64, :], src2[0][:, dc, :], wmat[0][:, dc, off:off + 512],
                        start=False, stop=(dc == 7)),
                        reads=[src2[1], wmat[1]], writes=[pp[1]])
                b.v(lambda v, nch=nch, acc2=acc2, pp=pp: v.tensor_copy(
                    acc2[0][:, nch * 512:(nch + 1) * 512], pp[0][0:64, :]),
                    reads=[pp[1]], writes=[acc2[1]])
        if part == 2:
            b.v(lambda v, dst=dst: v.tensor_tensor(dst[0], r_t[0], ghp[0], ALU.mult),
                reads=[r_t[1], ghp[1]], writes=[dst[1]])
            b.v(lambda v, dst=dst: v.tensor_tensor(dst[0], dst[0], gxp[0], ALU.add),
                reads=[dst[1], gxp[1]], writes=[dst[1]])
            b.s(lambda sc, dst=dst: sc.activation(dst[0], dst[0], AF.Tanh),
                reads=[dst[1]], writes=[dst[1]])
        else:
            b.v(lambda v, dst=dst: v.tensor_tensor(dst[0], gxp[0], ghp[0], ALU.add),
                reads=[gxp[1], ghp[1]], writes=[dst[1]])
            b.s(lambda sc, dst=dst: sc.activation(dst[0], dst[0], AF.Sigmoid),
                reads=[dst[1]], writes=[dst[1]])
    b.v(lambda v: v.tensor_tensor(hs[0], hs[0], n_t[0], ALU.subtract),
        reads=[hs[1], n_t[1]], writes=[hs[1]])
    b.v(lambda v: v.tensor_tensor(hs[0], hs[0], z_t[0], ALU.mult),
        reads=[hs[1], z_t[1]], writes=[hs[1]])
    b.v(lambda v: v.tensor_tensor(hs[0], hs[0], n_t[0], ALU.add),
        reads=[hs[1], n_t[1]], writes=[hs[1]])
    b.v(lambda v: v.tensor_copy(gibf[0], hs[0]), reads=[hs[1]], writes=[gibf[1]])
    for dc in range(8):
        b.t(lambda t, dc=dc: t.transpose(pstr[0][:, 0:64], gibf[0][0:64, dc * 128:(dc + 1) * 128],
                                         idb[0][0:64, 0:64]),
            reads=[gibf[1], idb[1]], writes=[pstr[1]], guard=384)
        b.v(lambda v, dc=dc: v.tensor_copy(snT[0][:, dc, :], pstr[0][:, 0:64]),
            reads=[pstr[1]], writes=[snT[1]])
    gateb = gibf[1].wr
    ar.reset(_markb)

    def AGB(shape, dtype=F32):
        bb = Buf(); bb.wr = gateb
        return (ar.alloc(shape, dtype), bb)

    f0 = AGB([128, 8, 512], BF16)
    acct = AGB([128, 1, 1024], F32)
    accb = [acct[1], acct[1]]
    for oc in range(2):
        pp = ps[2 + oc]
        b.t(lambda t, oc=oc, pp=pp: t.matmul(
            pp[0][0:64, :], onesr[0][0:1, 0:64], bv0[0][0:1, oc * 512:(oc + 1) * 512],
            start=True, stop=False),
            reads=[onesr[1], bv0[1]], writes=[pp[1]])
        for dc in range(8):
            b.t(lambda t, oc=oc, dc=dc, pp=pp: t.matmul(
                pp[0][0:64, :], snT[0][:, dc, :], wv0[0][:, dc, oc * 512:(oc + 1) * 512],
                start=False, stop=(dc == 7)),
                reads=[snT[1], wv0[1]], writes=[pp[1]])
        b.v(lambda v, oc=oc, pp=pp: v.tensor_copy(sv[0][:, oc * 512:(oc + 1) * 512], pp[0][0:64, :]),
            reads=[pp[1]], writes=[sv[1]])
    for s in range(NSUP):
        for dc in range(8):
            pp = ps[dc % 2]
            b.t(lambda t, s=s, dc=dc, pp=pp: t.matmul(
                pp[0], sv[0][:, dc * 128:(dc + 1) * 128], alphaT[0][0:64, s * 512:(s + 1) * 512],
                start=True, stop=True),
                reads=[sv[1], alphaT[1]], writes=[pp[1]])
            b.v(lambda v, s=s, dc=dc, pp=pp: v.tensor_copy(f0[0][:, dc, :], pp[0]),
                reads=[pp[1]], writes=[f0[1]])
        for tb in range(4):
            gb = 4 * s + tb
            b.dma("sync", lambda g, gb=gb, tb=tb: g.dma_start(
                out=acct[0][:, 0, :], in_=acc_ap[gb * 128:(gb + 1) * 128, :]),
                reads=[acc_hb], writes=[accb[0]])
            for oc in range(2):
                pp = ps[2 + oc]
                b.t(lambda t, oc=oc, pp=pp: t.matmul(
                    pp[0], onesr[0], bo0[0][0:1, oc * 512:(oc + 1) * 512],
                    start=True, stop=False),
                    reads=[onesr[1], bo0[1]], writes=[pp[1]])
                for dc in range(8):
                    b.t(lambda t, tb=tb, oc=oc, dc=dc, pp=pp: t.matmul(
                        pp[0], f0[0][:, dc, tb * 128:(tb + 1) * 128],
                        wo0[0][:, dc, oc * 512:(oc + 1) * 512],
                        start=False, stop=(dc == 7)),
                        reads=[f0[1], wo0[1]], writes=[pp[1]])
                b.v(lambda v, tb=tb, oc=oc, pp=pp: v.tensor_tensor(
                    acct[0][:, 0, oc * 512:(oc + 1) * 512],
                    acct[0][:, 0, oc * 512:(oc + 1) * 512], pp[0], ALU.add),
                    reads=[pp[1], accb[0]], writes=[accb[0]])
            b.dma("sync", lambda g, gb=gb, tb=tb: g.dma_start(
                out=x1_ap[gb * 128:(gb + 1) * 128, :], in_=acct[0][:, 0, :]),
                reads=[accb[0]], writes=[x1_hb])
    assert ar.off <= 212200, ar.off


WN_B = ["wihT", "whhT", "bih", "bhh", "wv0T", "bv0", "wo0T", "bo0",
        "slot_init", "slot_scale", "ident"]


def make_prog_b():
    pr = Prog()
    shp = {"wihT": [1024, 3072], "whhT": [1024, 3072], "bih": [1, 3072], "bhh": [1, 3072],
           "wv0T": [1024, 1024], "bv0": [1, 1024], "wo0T": [1024, 1024], "bo0": [1, 1024],
           "slot_init": [64, 1024], "slot_scale": [64, 1], "ident": [128, 128]}
    for n in WN_B:
        pr.dram_in(n, shp[n])
    pr.dram_in("alpha_h", [T, 64])
    pr.dram_in("slotin_h", [64, 1024])
    pr.dram_in("acc", [T, D])
    pr.dram_out("x1", [T, D])
    build_phase_b(pr, pr.hbm)
    pr.finish()
    emit_block(pr.nc, pr.plan, pr.stack)
    return pr


def build_phase_c(pr, io):
    nc, b = pr.nc, pr.b
    st = pr.stack
    ar = pr.arena("c")

    def A(shape, dtype=F32):
        return (ar.alloc(shape, dtype), Buf())

    inpw = A([128, 8, 4096], BF16)
    xproj = A([128, 16, 128], BF16)
    convw = A([128, 16, 4], F32)
    convb16 = A([16, 128], F32)
    convbf = A([128, 16], F32)
    gnb = A([128, 1024], F32)
    bnb = A([128, 1024], F32)
    idb = A([128, 128], BF16)
    idf = A([128, 128], F32)
    xs = A([128, 16, 512], F32)
    tail = A([128, 16, 3], F32)
    xe = A([128, 515], F32)
    cacc = A([128, 512], F32)
    xcbP = [A([128, 512], BF16), A([128, 512], BF16)]
    szb = A([128, 512], BF16)
    BCsb = A([64, 4096], BF16)
    dtlowb = A([64, 4096], BF16)
    x1t = A([128, 2, 1024], F32)
    xn1 = A([128, 1024], F32)
    xlnt = A([128, 1024], F32)
    xln_tm = A([128, 4, 1024], BF16)
    xln_fm = A([128, 8, 512], BF16)
    acc2 = A([128, 1024], F32)
    mval = A([128, 1], F32); mcol = A([128, 1], F32); ssum = A([128, 1], F32)
    v1 = A([128, 1], F32); lnv = A([128, 1], F32); rstd = A([128, 1], F32)

    ps, pstr = pr.psum()

    def wload(dst, src_name, pat=None, **kw):
        ap, hb = io[src_name]
        src = ap.rearrange(pat, **kw) if pat else ap
        b.dma("gpsimd", lambda g, o=dst[0], s=src: g.dma_start(out=o, in_=s),
              reads=[hb], writes=[dst[1]])

    def fload(dst, src_name, pat=None, bcast=None, **kw):
        ap, hb = io[src_name]
        src = ap.rearrange(pat, **kw) if pat else ap
        if bcast:
            src = src.broadcast_to(bcast)
        b.dma("sync", lambda g, o=dst[0], s=src: g.dma_start(out=o, in_=s),
              reads=[hb], writes=[dst[1]])

    wload(inpw, "inpT", "(c p) q -> p c q", p=128)
    wload(xproj, "xprojTp", "(c p) q -> p c q", p=128)
    wload(idb, "ident")
    fload(idf, "ident")
    fload(convw, "convw", "(c p) j -> p c j", p=128)
    fload(convb16, "convb")
    fload(gnb, "lng", bcast=[128, 1024])
    fload(bnb, "lnb", bcast=[128, 1024])
    b.t(lambda t: t.transpose(ps[0][0][0:128, 0:16], convb16[0], idf[0][0:16, 0:16]),
        reads=[convb16[1], idf[1]], writes=[ps[0][1]], guard=512)
    b.v(lambda v: v.tensor_copy(convbf[0], ps[0][0][:, 0:16]),
        reads=[ps[0][1]], writes=[convbf[1]])
    b.v(lambda v: v.memset(tail[0], 0.0), writes=[tail[1]])

    x1_ap, x1_hb = io["x1"]
    xn1_ap, xn1_hb = io["xn1_h"]
    accb_ap, accb_hb = io["accb_h"]
    xc_ap, xc_hb = io["xc_h"]
    sz_ap, sz_hb = io["sz_h"]
    bc_ap, bc_hb = io["BC_h"]
    dl_ap, dl_hb = io["dtlow_h"]

    for s in range(NSUP):
        for tb in range(4):
            gb = 4 * s + tb
            b.dma("sync", lambda g, gb=gb, tb=tb: g.dma_start(
                out=x1t[0][:, tb % 2, :], in_=x1_ap[gb * 128:(gb + 1) * 128, :]),
                reads=[x1_hb], writes=[x1t[1]])
            b.v(lambda v, tb=tb: v.reduce_sum(mval[0], x1t[0][:, tb % 2, :], AX.X),
                reads=[x1t[1]], writes=[mval[1]])
            b.v(lambda v: v.tensor_scalar(mcol[0], mval[0], 1.0 / D, None, ALU.mult),
                reads=[mval[1]], writes=[mcol[1]])
            b.v(lambda v: v.tensor_scalar(mval[0], mcol[0], -1.0, None, ALU.mult),
                reads=[mcol[1]], writes=[mval[1]])
            b.s(lambda sc, tb=tb: sc.activation(xlnt[0], x1t[0][:, tb % 2, :],
                                                AF.Square, bias=mval[0], accum_out=ssum[0]),
                reads=[x1t[1], mval[1]], writes=[xlnt[1], ssum[1]])
            b.v(lambda v: v.tensor_scalar(v1[0], ssum[0], 1.0 / D, 1e-5, ALU.mult, ALU.add),
                reads=[ssum[1]], writes=[v1[1]])
            b.s(lambda sc: sc.activation(lnv[0], v1[0], AF.Ln), reads=[v1[1]], writes=[lnv[1]])
            b.s(lambda sc: sc.activation(rstd[0], lnv[0], AF.Exp, scale=-0.5),
                reads=[lnv[1]], writes=[rstd[1]])
            b.v(lambda v, tb=tb: v.tensor_scalar(xn1[0], x1t[0][:, tb % 2, :],
                                                 mcol[0], rstd[0], ALU.subtract, ALU.mult),
                reads=[x1t[1], mcol[1], rstd[1]], writes=[xn1[1]])
            b.dma("sync", lambda g, gb=gb: g.dma_start(
                out=xn1_ap[gb * 128:(gb + 1) * 128, :], in_=xn1[0]),
                reads=[xn1[1]], writes=[xn1_hb])
            b.v(lambda v, tb=tb: v.tensor_tensor(acc2[0], x1t[0][:, tb % 2, :], xn1[0], ALU.add),
                reads=[x1t[1], xn1[1]], writes=[acc2[1]])
            b.dma("sync", lambda g, gb=gb: g.dma_start(
                out=accb_ap[gb * 128:(gb + 1) * 128, :], in_=acc2[0]),
                reads=[acc2[1]], writes=[accb_hb])
            b.v(lambda v: v.reduce_sum(mval[0], xn1[0], AX.X), reads=[xn1[1]], writes=[mval[1]])
            b.v(lambda v: v.tensor_scalar(mcol[0], mval[0], 1.0 / D, None, ALU.mult),
                reads=[mval[1]], writes=[mcol[1]])
            b.v(lambda v: v.tensor_scalar(mval[0], mcol[0], -1.0, None, ALU.mult),
                reads=[mcol[1]], writes=[mval[1]])
            b.s(lambda sc: sc.activation(xlnt[0], xn1[0], AF.Square, bias=mval[0],
                                         accum_out=ssum[0]),
                reads=[xn1[1], mval[1]], writes=[xlnt[1], ssum[1]])
            b.v(lambda v: v.tensor_scalar(v1[0], ssum[0], 1.0 / D, 1e-5, ALU.mult, ALU.add),
                reads=[ssum[1]], writes=[v1[1]])
            b.s(lambda sc: sc.activation(lnv[0], v1[0], AF.Ln), reads=[v1[1]], writes=[lnv[1]])
            b.s(lambda sc: sc.activation(rstd[0], lnv[0], AF.Exp, scale=-0.5),
                reads=[lnv[1]], writes=[rstd[1]])
            b.v(lambda v: v.tensor_scalar(xlnt[0], xn1[0], mcol[0], rstd[0],
                                          ALU.subtract, ALU.mult),
                reads=[xn1[1], mcol[1], rstd[1]], writes=[xlnt[1]])
            b.v(lambda v: v.tensor_tensor(xlnt[0], xlnt[0], gnb[0], ALU.mult),
                reads=[xlnt[1], gnb[1]], writes=[xlnt[1]])
            b.v(lambda v, tb=tb: v.tensor_tensor(xln_tm[0][:, tb, :], xlnt[0], bnb[0], ALU.add),
                reads=[xlnt[1], bnb[1]], writes=[xln_tm[1]])
            for dc in range(8):
                b.t(lambda t, tb=tb, dc=dc: t.transpose(
                    pstr[0][:, dc * 128:(dc + 1) * 128],
                    xln_tm[0][:, tb, dc * 128:(dc + 1) * 128], idb[0]),
                    reads=[xln_tm[1], idb[1]], writes=[pstr[1]], guard=(256 if dc == 7 else 0))
            b.v(lambda v, tb=tb: v.tensor_copy(
                xln_fm[0][:, :, tb * 128:(tb + 1) * 128],
                pstr[0].rearrange("p (c q) -> p c q", q=128)),
                reads=[pstr[1]], writes=[xln_fm[1]])
        def emit_xdbl(ec):
            xcb = xcbP[ec % 2]
            b.t(lambda t, ec=ec, xcb=xcb: t.matmul(
                ps[2][0][0:64, :], xproj[0][:, ec, 0:64], xcb[0],
                start=(ec == 0), stop=(ec == 15)),
                reads=[xproj[1], xcb[1]], writes=[ps[2][1]], guard=(256 if ec == 15 else 0))
            b.t(lambda t, ec=ec, xcb=xcb: t.matmul(
                ps[3][0][0:64, :], xproj[0][:, ec, 64:128], xcb[0],
                start=(ec == 0), stop=(ec == 15)),
                reads=[xproj[1], xcb[1]], writes=[ps[3][1]], guard=(256 if ec == 15 else 0))

        for ec in range(16):
            pp = ps[ec % 2]
            for dc in range(8):
                b.t(lambda t, ec=ec, dc=dc, pp=pp: t.matmul(
                    pp[0], inpw[0][:, dc, ec * 128:(ec + 1) * 128], xln_fm[0][:, dc, :],
                    start=(dc == 0), stop=(dc == 7)),
                    reads=[inpw[1], xln_fm[1]], writes=[pp[1]])
            b.v(lambda v, ec=ec, pp=pp: v.tensor_copy(xs[0][:, ec, :], pp[0]),
                reads=[pp[1]], writes=[xs[1]])
            b.v(lambda v, ec=ec: v.tensor_copy(xe[0][:, 0:3], tail[0][:, ec, :]),
                reads=[tail[1]], writes=[xe[1]])
            b.v(lambda v, ec=ec: v.tensor_copy(xe[0][:, 3:515], xs[0][:, ec, :]),
                reads=[xs[1]], writes=[xe[1]])
            b.v(lambda v, ec=ec: v.tensor_copy(tail[0][:, ec, :], xs[0][:, ec, 509:512]),
                reads=[xs[1]], writes=[tail[1]])
            b.v(lambda v, ec=ec: v.tensor_scalar(cacc[0], xe[0][:, 0:512],
                                                 convw[0][:, ec, 0:1], None, ALU.mult),
                reads=[xe[1], convw[1]], writes=[cacc[1]])
            for j in range(1, 4):
                b.v(lambda v, ec=ec, j=j: v.scalar_tensor_tensor(
                    cacc[0], xe[0][:, j:j + 512], convw[0][:, ec, j:j + 1], cacc[0],
                    ALU.mult, ALU.add),
                    reads=[xe[1], convw[1], cacc[1]], writes=[cacc[1]])
            xcb = xcbP[ec % 2]
            b.s(lambda sc, ec=ec, xcb=xcb: sc.activation(xcb[0], cacc[0], AF.Silu,
                                                         bias=convbf[0][:, ec:ec + 1]),
                reads=[cacc[1], convbf[1]], writes=[xcb[1]])
            b.dma("sync", lambda g, s=s, ec=ec, xcb=xcb: g.dma_start(
                out=xc_ap[ec, :, s * 512:(s + 1) * 512], in_=xcb[0]),
                reads=[xcb[1]], writes=[xc_hb])
            if ec > 0:
                emit_xdbl(ec - 1)
        emit_xdbl(15)
        b.v(lambda v, s=s: v.tensor_copy(BCsb[0][:, s * 512:(s + 1) * 512], ps[2][0][0:64, :]),
            reads=[ps[2][1]], writes=[BCsb[1]])
        b.v(lambda v, s=s: v.tensor_copy(dtlowb[0][:, s * 512:(s + 1) * 512], ps[3][0][0:64, :]),
            reads=[ps[3][1]], writes=[dtlowb[1]])
        for ec in range(16, 32):
            pp = ps[ec % 2]
            for dc in range(8):
                b.t(lambda t, ec=ec, dc=dc, pp=pp: t.matmul(
                    pp[0], inpw[0][:, dc, ec * 128:(ec + 1) * 128], xln_fm[0][:, dc, :],
                    start=(dc == 0), stop=(dc == 7)),
                    reads=[inpw[1], xln_fm[1]], writes=[pp[1]])
            b.s(lambda sc, ec=ec, pp=pp: sc.activation(szb[0], pp[0], AF.Silu),
                reads=[pp[1]], writes=[szb[1]])
            b.dma("sync", lambda g, s=s, ec=ec: g.dma_start(
                out=sz_ap[ec - 16, :, s * 512:(s + 1) * 512], in_=szb[0]),
                reads=[szb[1]], writes=[sz_hb])
    b.dma("sync", lambda g: g.dma_start(out=bc_ap, in_=BCsb[0]),
          reads=[BCsb[1]], writes=[bc_hb])
    b.dma("sync", lambda g: g.dma_start(out=dl_ap, in_=dtlowb[0]),
          reads=[dtlowb[1]], writes=[dl_hb])
    assert ar.off <= 212200, ar.off


WN_C = ["lng", "lnb", "inpT", "convw", "convb", "xprojTp", "ident"]


def make_prog_c():
    pr = Prog()
    shp = {"lng": [1, 1024], "lnb": [1, 1024], "inpT": [1024, 4096], "convw": [2048, 4],
           "convb": [16, 128], "xprojTp": [2048, 128], "ident": [128, 128]}
    for n in WN_C:
        pr.dram_in(n, shp[n])
    pr.dram_in("x1", [T, D])
    pr.dram_out("xn1_h", [T, D])
    pr.dram_out("accb_h", [T, D])
    pr.dram_out("xc_h", [16, 128, T], BF16)
    pr.dram_out("sz_h", [16, 128, T], BF16)
    pr.dram_out("BC_h", [64, T], BF16)
    pr.dram_out("dtlow_h", [64, T], BF16)
    build_phase_c(pr, pr.hbm)
    pr.finish()
    emit_block(pr.nc, pr.plan, pr.stack)
    return pr


NCH = 8
TC = 512


def build_phase_d(pr, io):
    nc, b = pr.nc, pr.b
    st = pr.stack
    ar = pr.arena("d")

    def A(shape, dtype=F32):
        return (ar.alloc(shape, dtype), Buf())

    outp = A([128, 16, 1024], BF16)
    dtpj = A([64, 2048], BF16)
    Afm = A([128, 16, 16], F32)
    dtbf = A([128, 16], F32)
    Dfm = A([128, 16], F32)
    t16 = A([16, 128], F32)
    idf = A([128, 128], F32)
    dtlow = A([64, 4096], BF16)
    dtt = A([128, 512], F32)
    absx = A([128, 512], F32)
    rlx = A([128, 512], F32)
    ex1 = A([128, 512], F32)
    ubf = A([128, 512], BF16)
    at2 = A([128, 2, 512], F32)
    bt_all = A([128, 16, 512], BF16)
    ht_all = A([128, 16, 512], BF16)
    yac = A([128, 512], F32)
    Bball = A([128, 2, 16, 512], BF16)
    Cball = A([128, 2, 16, 512], BF16)
    xct = A([128, 2, 512], F32)
    szt = A([128, 2, 512], F32)
    yg = A([128, 2, 16, 512], BF16)
    hstate = A([128, 16, 16], F32)
    acct = A([128, 2, 1024], F32)

    ps, _pstr_unused = pr.psum()

    def wload(dst, src_name, pat=None, **kw):
        ap, hb = io[src_name]
        src = ap.rearrange(pat, **kw) if pat else ap
        b.dma("gpsimd", lambda g, o=dst[0], s=src: g.dma_start(out=o, in_=s),
              reads=[hb], writes=[dst[1]])

    def fload(dst, src_name, pat=None, bcast=None, **kw):
        ap, hb = io[src_name]
        src = ap.rearrange(pat, **kw) if pat else ap
        if bcast:
            src = src.broadcast_to(bcast)
        b.dma("sync", lambda g, o=dst[0], s=src: g.dma_start(out=o, in_=s),
              reads=[hb], writes=[dst[1]])

    wload(outp, "outpT", "(c p) q -> p c q", p=128)
    wload(dtpj, "dtprojT")
    wload(dtlow, "dtlow_h")
    fload(idf, "ident")
    fload(Afm, "Alog", "(c p) n -> p c n", p=128)
    b.s(lambda sc: sc.activation(Afm[0], Afm[0], AF.Exp), reads=[Afm[1]], writes=[Afm[1]])
    b.v(lambda v: v.tensor_scalar(Afm[0], Afm[0], -1.0, None, ALU.mult),
        reads=[Afm[1]], writes=[Afm[1]])
    for name, dst in (("dtb", dtbf), ("Dp", Dfm)):
        fload(t16, name)
        b.t(lambda t: t.transpose(ps[0][0][0:128, 0:16], t16[0], idf[0][0:16, 0:16]),
            reads=[t16[1], idf[1]], writes=[ps[0][1]], guard=512)
        b.v(lambda v, dst=dst: v.tensor_copy(dst[0], ps[0][0][:, 0:16]),
            reads=[ps[0][1]], writes=[dst[1]])
    b.v(lambda v: v.memset(hstate[0], 0.0), writes=[hstate[1]])

    xc_ap, xc_hb = io["xc_h"]
    sz_ap, sz_hb = io["sz_h"]
    bc_ap, bc_hb = io["BC_h"]
    accb_ap, accb_hb = io["accb_h"]
    acc2_ap, acc2_hb = io["acc2_h"]

    BbB = [Buf(), Buf()]
    CbB = [Buf(), Buf()]
    atB = [Buf(), Buf()]
    xctB = [Buf(), Buf()]
    sztB = [Buf(), Buf()]
    acctB = [Buf(), Buf()]
    ygB = [Buf(), Buf()]
    pend_op = [None]
    for c in range(NCH):
        cb = c % 2
        pops = pend_op[0] or []
        pk = [0]

        def drip(dc2):
            while pk[0] < len(pops) and pk[0] * 16 <= (dc2 + 1) * len(pops):
                pops[pk[0]]()
                pk[0] += 1
        for n in range(16):
            b.dma("sync", lambda g, c=c, n=n, cb=cb: g.dma_start(
                out=Bball[0][:, cb, n, :],
                in_=bc_ap[32 + n:33 + n, c * TC:(c + 1) * TC].broadcast_to([128, TC])),
                reads=[bc_hb], writes=[BbB[cb]])
            b.dma("sync", lambda g, c=c, n=n, cb=cb: g.dma_start(
                out=Cball[0][:, cb, n, :],
                in_=bc_ap[n:n + 1, c * TC:(c + 1) * TC].broadcast_to([128, TC])),
                reads=[bc_hb], writes=[CbB[cb]])
        for dc2 in range(16):
            db = dc2 % 2
            pp = ps[db]
            b.t(lambda t, c=c, dc2=dc2, pp=pp: t.matmul(
                pp[0], dtpj[0][0:64, dc2 * 128:(dc2 + 1) * 128],
                dtlow[0][0:64, c * TC:(c + 1) * TC], start=True, stop=True),
                reads=[dtpj[1], dtlow[1]], writes=[pp[1]])
            b.s(lambda sc, dc2=dc2, pp=pp: sc.activation(absx[0], pp[0], AF.Abs,
                                                         bias=dtbf[0][:, dc2:dc2 + 1]),
                reads=[pp[1], dtbf[1]], writes=[absx[1]])
            b.s(lambda sc, dc2=dc2, pp=pp: sc.activation(rlx[0], pp[0], AF.Relu,
                                                         bias=dtbf[0][:, dc2:dc2 + 1]),
                reads=[pp[1], dtbf[1]], writes=[rlx[1]])
            b.s(lambda sc: sc.activation(ex1[0], absx[0], AF.Exp, scale=-1.0),
                reads=[absx[1]], writes=[ex1[1]])
            b.s(lambda sc: sc.activation(ex1[0], ex1[0], AF.Ln, bias=1.0),
                reads=[ex1[1]], writes=[ex1[1]])
            b.v(lambda v: v.tensor_tensor(dtt[0], ex1[0], rlx[0], ALU.add),
                reads=[ex1[1], rlx[1]], writes=[dtt[1]])
            b.dma("gpsimd", lambda g, c=c, dc2=dc2, db=db: g.dma_start(
                out=xct[0][:, db, :], in_=xc_ap[dc2, :, c * TC:(c + 1) * TC]),
                reads=[xc_hb], writes=[xctB[db]])
            def _umul(v, db=db):
                with nc.allow_low_precision(reason="b operand of scan in bf16"):
                    return v.tensor_tensor(ubf[0], dtt[0], xct[0][:, db, :], ALU.mult)
            b.v(_umul, reads=[dtt[1], xctB[db]], writes=[ubf[1]])
            b.v(lambda v, cb=cb: v.tensor_tensor(
                bt_all[0], Bball[0][:, cb, :, :],
                ubf[0].unsqueeze(1).broadcast_to([128, 16, TC]), ALU.mult),
                reads=[BbB[cb], ubf[1]], writes=[bt_all[1]])
            for n in range(16):
                ab = n % 2
                b.s(lambda sc, dc2=dc2, n=n, ab=ab: sc.activation(
                    at2[0][:, ab, :], dtt[0], AF.Exp, scale=Afm[0][:, dc2, n:n + 1]),
                    reads=[dtt[1], Afm[1]], writes=[atB[ab]])
                b.v(lambda v, dc2=dc2, n=n, ab=ab: v.tensor_tensor_scan(
                    ht_all[0][:, n, :], at2[0][:, ab, :], bt_all[0][:, n, :],
                    hstate[0][:, dc2, n:n + 1], ALU.mult, ALU.add),
                    reads=[atB[ab], bt_all[1], hstate[1]], writes=[ht_all[1]])
            b.v(lambda v, dc2=dc2: v.tensor_copy(
                hstate[0][:, dc2, :],
                ht_all[0][:, :, TC - 1:TC].rearrange("p a b -> p (a b)")),
                reads=[ht_all[1]], writes=[hstate[1]])
            b.v(lambda v, cb=cb: v.tensor_tensor(
                bt_all[0], ht_all[0], Cball[0][:, cb, :, :], ALU.mult),
                reads=[ht_all[1], CbB[cb]], writes=[bt_all[1]])
            for w in (8, 4, 2):
                b.v(lambda v, w=w: v.tensor_tensor(
                    bt_all[0][:, 0:w, :], bt_all[0][:, 0:w, :],
                    bt_all[0][:, w:2 * w, :], ALU.add),
                    reads=[bt_all[1]], writes=[bt_all[1]])
            b.v(lambda v: v.tensor_tensor(yac[0], bt_all[0][:, 0, :],
                                          bt_all[0][:, 1, :], ALU.add),
                reads=[bt_all[1]], writes=[yac[1]])
            b.dma("gpsimd", lambda g, c=c, dc2=dc2, db=db: g.dma_start(
                out=szt[0][:, db, :], in_=sz_ap[dc2, :, c * TC:(c + 1) * TC]),
                reads=[sz_hb], writes=[sztB[db]])
            b.v(lambda v, dc2=dc2, db=db: v.scalar_tensor_tensor(
                yac[0], xct[0][:, db, :], Dfm[0][:, dc2:dc2 + 1], yac[0],
                ALU.mult, ALU.add),
                reads=[xctB[db], Dfm[1], yac[1]], writes=[yac[1]])
            b.v(lambda v, dc2=dc2, db=db, cb=cb: v.tensor_tensor(
                yg[0][:, cb, dc2, :], yac[0], szt[0][:, db, :], ALU.mult),
                reads=[yac[1], sztB[db]], writes=[ygB[cb]])
            drip(dc2)
        if c == 0 and "dbgd" in io:
            dd_ap, dd_hb = io["dbgd"]
            b.dma("sync", lambda g: g.dma_start(out=dd_ap[:, 0:256],
                  in_=Afm[0].rearrange("p a n -> p (a n)")), reads=[Afm[1]], writes=[dd_hb])
            b.dma("sync", lambda g: g.dma_start(out=dd_ap[:, 256:768], in_=dtt[0]),
                  reads=[dtt[1]], writes=[dd_hb])
            b.dma("sync", lambda g: g.dma_start(out=dd_ap[:, 768:1024],
                  in_=hstate[0].rearrange("p a n -> p (a n)")), reads=[hstate[1]], writes=[dd_hb])
            b.dma("sync", lambda g: g.dma_start(out=dd_ap[:, 1024:1536], in_=yac[0]),
                  reads=[yac[1]], writes=[dd_hb])
        def make_outproj(c2):
            cb2 = c2 % 2
            ops = []
            for tb in range(4):
                gb = 4 * c2 + tb
                ab2 = tb % 2

                def fload(gb=gb, ab2=ab2):
                    b.dma("sync", lambda g, gb=gb, ab2=ab2: g.dma_start(
                        out=acct[0][:, ab2, :], in_=accb_ap[gb * 128:(gb + 1) * 128, :]),
                        reads=[accb_hb], writes=[acctB[ab2]])
                ops.append(fload)
                for oc in range(2):
                    def fmm_h(tb=tb, oc=oc, cb2=cb2, lo=0):
                        pp = ps[2 + oc]
                        for dc2 in range(lo, lo + 8):
                            b.t(lambda t, tb=tb, oc=oc, dc2=dc2, pp=pp, cb2=cb2: t.matmul(
                                pp[0], yg[0][:, cb2, dc2, tb * 128:(tb + 1) * 128],
                                outp[0][:, dc2, oc * 512:(oc + 1) * 512],
                                start=(dc2 == 0), stop=(dc2 == 15)),
                                reads=[ygB[cb2], outp[1]], writes=[pp[1]])

                    def fadd(oc=oc, ab2=ab2):
                        pp = ps[2 + oc]
                        b.v(lambda v, oc=oc, pp=pp, ab2=ab2: v.tensor_tensor(
                            acct[0][:, ab2, oc * 512:(oc + 1) * 512],
                            acct[0][:, ab2, oc * 512:(oc + 1) * 512], pp[0], ALU.add),
                            reads=[pp[1], acctB[ab2]], writes=[acctB[ab2]])
                    ops.append(lambda f=fmm_h: f(lo=0))
                    ops.append(lambda f=fmm_h: f(lo=8))
                    ops.append(fadd)

                def fstore(gb=gb, ab2=ab2):
                    b.dma("sync", lambda g, gb=gb, ab2=ab2: g.dma_start(
                        out=acc2_ap[gb * 128:(gb + 1) * 128, :], in_=acct[0][:, ab2, :]),
                        reads=[acctB[ab2]], writes=[acc2_hb])
                ops.append(fstore)
            return ops

        pend_op[0] = make_outproj(c)
        if c == NCH - 1:
            for f in pend_op[0]:
                f()
    assert ar.off <= 212200, ar.off


WN_D = ["outpT", "dtprojT", "Alog", "dtb", "Dp", "ident"]


def make_prog_d():
    pr = Prog()
    shp = {"outpT": [2048, 1024], "dtprojT": [64, 2048], "Alog": [2048, 16],
           "dtb": [16, 128], "Dp": [16, 128], "ident": [128, 128]}
    for n in WN_D:
        pr.dram_in(n, shp[n])
    pr.dram_in("xc_h", [16, 128, T], BF16)
    pr.dram_in("sz_h", [16, 128, T], BF16)
    pr.dram_in("BC_h", [64, T], BF16)
    pr.dram_in("dtlow_h", [64, T], BF16)
    pr.dram_in("accb_h", [T, D])
    pr.dram_out("acc2_h", [T, D])
    build_phase_d(pr, pr.hbm)
    pr.finish()
    emit_block(pr.nc, pr.plan, pr.stack)
    return pr


def build_phase_e(pr, io):
    nc, b = pr.nc, pr.b
    st = pr.stack
    ar = pr.arena("e")

    def A(shape, dtype=F32):
        return (ar.alloc(shape, dtype), Buf())

    wfc = A([128, 8, 4096], BF16)
    wpj = A([128, 32, 1024], BF16)
    bfc = A([1, 4096], BF16)
    bpj = A([1, 1024], BF16)
    idb = A([128, 128], BF16)
    onesr = A([1, 128], BF16)
    _marke = ar.mark()

    ps, pstr = pr.psum()

    def wload(dst, src_name, pat=None, **kw):
        ap, hb = io[src_name]
        src = ap.rearrange(pat, **kw) if pat else ap
        b.dma("gpsimd", lambda g, o=dst[0], s=src: g.dma_start(out=o, in_=s),
              reads=[hb], writes=[dst[1]])

    wload(bfc, "bfcq")
    wload(bpj, "bprojq")
    wload(idb, "ident")
    b.v(lambda v: v.memset(onesr[0], 1.0), writes=[onesr[1]])
    wraw = A([128, 4096], F32)
    wqb = A([128, 4096], BF16)
    amw = A([128, 128], F32)
    scw = A([128, 128], F32)
    ivw = A([128, 128], F32)
    qw = A([128, 4096], F32)

    xn1_ap, xn1_hb = io["xn1_h"]
    acc2_ap, acc2_hb = io["acc2_h"]
    out_ap, out_hb = io["out"]

    M = 12582912.0
    wfcO_ap, wfcO_hb = io["wfcO"]
    wprojO_ap, wprojO_hb = io["wprojO"]

    def quantw(src_t, q_t, dst_bf, ncols):
        nblk = ncols // 32
        b.v(lambda v, nblk=nblk, ncols=ncols: v.tensor_reduce(
            amw[0][:, 0:nblk], src_t[0][:, 0:ncols].rearrange("p (o i) -> p o i", i=32),
            AX.X, ALU.max, apply_absolute_value=True),
            reads=[src_t[1]], writes=[amw[1]])
        b.v(lambda v, nblk=nblk: v.tensor_scalar(amw[0][:, 0:nblk], amw[0][:, 0:nblk],
                                                 2.0 ** -100, None, ALU.max),
            reads=[amw[1]], writes=[amw[1]])
        b.v(lambda v, nblk=nblk: v.tensor_scalar(scw[0][:, 0:nblk].bitcast(I32),
                                                 amw[0][:, 0:nblk].bitcast(I32),
                                                 0x7F800000, None, ALU.bitwise_and),
            reads=[amw[1]], writes=[scw[1]])
        b.v(lambda v, nblk=nblk: v.tensor_scalar(scw[0][:, 0:nblk], scw[0][:, 0:nblk],
                                                 0.125, None, ALU.mult),
            reads=[scw[1]], writes=[scw[1]])
        b.v(lambda v, nblk=nblk: v.reciprocal(ivw[0][:, 0:nblk], scw[0][:, 0:nblk]),
            reads=[scw[1]], writes=[ivw[1]])
        b.v(lambda v, nblk=nblk, ncols=ncols: v.tensor_tensor(
            q_t[0][:, 0:ncols].rearrange("p (o i) -> p o i", i=32),
            src_t[0][:, 0:ncols].rearrange("p (o i) -> p o i", i=32),
            ivw[0][:, 0:nblk].unsqueeze(2).broadcast_to([128, nblk, 32]), ALU.mult),
            reads=[src_t[1], ivw[1]], writes=[q_t[1]])
        b.v(lambda v, ncols=ncols: v.tensor_scalar(q_t[0][:, 0:ncols], q_t[0][:, 0:ncols],
                                                   M, M, ALU.add, ALU.subtract),
            reads=[q_t[1]], writes=[q_t[1]])
        b.v(lambda v, nblk=nblk, ncols=ncols: v.tensor_tensor(
            dst_bf[0][:, 0:ncols].rearrange("p (o i) -> p o i", i=32),
            q_t[0][:, 0:ncols].rearrange("p (o i) -> p o i", i=32),
            scw[0][:, 0:nblk].unsqueeze(2).broadcast_to([128, nblk, 32]), ALU.mult),
            reads=[q_t[1], scw[1]], writes=[dst_bf[1]])

    for ncj in range(32):
        b.dma("sync", lambda g, ncj=ncj: g.dma_start(
            out=wraw[0][:, 0:1024], in_=wfcO_ap[ncj * 128:(ncj + 1) * 128, :]),
            reads=[wfcO_hb], writes=[wraw[1]])
        quantw(wraw, qw, wqb, 1024)
        for kc in range(8):
            b.t(lambda t, kc=kc: t.transpose(
                pstr[0][:, (kc % 8) * 128:(kc % 8 + 1) * 128],
                wqb[0][:, kc * 128:(kc + 1) * 128], idb[0]),
                reads=[wqb[1], idb[1]], writes=[pstr[1]], guard=(256 if kc == 7 else 0))
        b.v(lambda v, ncj=ncj: v.tensor_copy(
            wfc[0][:, :, ncj * 128:(ncj + 1) * 128],
            pstr[0].rearrange("p (c q) -> p c q", q=128)),
            reads=[pstr[1]], writes=[wfc[1]])
    for ncj in range(8):
        b.dma("sync", lambda g, ncj=ncj: g.dma_start(
            out=wraw[0], in_=wprojO_ap[ncj * 128:(ncj + 1) * 128, :]),
            reads=[wprojO_hb], writes=[wraw[1]])
        quantw(wraw, qw, wqb, 4096)
        for kc in range(32):
            b.t(lambda t, kc=kc: t.transpose(
                pstr[0][:, (kc % 8) * 128:(kc % 8 + 1) * 128],
                wqb[0][:, kc * 128:(kc + 1) * 128], idb[0]),
                reads=[wqb[1], idb[1]], writes=[pstr[1]], guard=(256 if kc % 8 == 7 else 0))
            if kc % 8 == 7:
                b.v(lambda v, ncj=ncj, kc=kc: v.tensor_copy(
                    wpj[0][:, kc - 7:kc + 1, ncj * 128:(ncj + 1) * 128],
                    pstr[0].rearrange("p (c q) -> p c q", q=128)),
                    reads=[pstr[1]], writes=[wpj[1]])

    def quant(src_t, width, nblk, amax_t, sc_t, iv_t, q_t, dst_bf):
        b.v(lambda v: v.tensor_reduce(
            amax_t[0], src_t[0].rearrange("p (o i) -> p o i", i=32), AX.X, ALU.max,
            apply_absolute_value=True),
            reads=[src_t[1]], writes=[amax_t[1]])
        b.v(lambda v: v.tensor_scalar(amax_t[0], amax_t[0], 2.0 ** -100, None, ALU.max),
            reads=[amax_t[1]], writes=[amax_t[1]])
        b.v(lambda v: v.tensor_scalar(sc_t[0].bitcast(I32), amax_t[0].bitcast(I32),
                                      0x7F800000, None, ALU.bitwise_and),
            reads=[amax_t[1]], writes=[sc_t[1]])
        b.v(lambda v: v.tensor_scalar(sc_t[0], sc_t[0], 0.125, None, ALU.mult),
            reads=[sc_t[1]], writes=[sc_t[1]])
        b.v(lambda v: v.reciprocal(iv_t[0], sc_t[0]), reads=[sc_t[1]], writes=[iv_t[1]])
        b.v(lambda v: v.tensor_tensor(
            q_t[0].rearrange("p (o i) -> p o i", i=32),
            src_t[0].rearrange("p (o i) -> p o i", i=32),
            iv_t[0].unsqueeze(2).broadcast_to([128, nblk, 32]), ALU.mult),
            reads=[src_t[1], iv_t[1]], writes=[q_t[1]])
        b.v(lambda v: v.tensor_scalar(q_t[0], q_t[0], M, M, ALU.add, ALU.subtract),
            reads=[q_t[1]], writes=[q_t[1]])
        b.v(lambda v: v.tensor_tensor(
            dst_bf[0].rearrange("p (o i) -> p o i", i=32),
            q_t[0].rearrange("p (o i) -> p o i", i=32),
            sc_t[0].unsqueeze(2).broadcast_to([128, nblk, 32]), ALU.mult),
            reads=[q_t[1], sc_t[1]], writes=[dst_bf[1]])

    gatee = wpj[1].wr
    ar.reset(_marke)

    def AGE(shape, dtype=F32):
        bb = Buf(); bb.wr = gatee
        return (ar.alloc(shape, dtype), bb)

    xn1P = [AGE([128, 1024], F32), AGE([128, 1024], F32)]
    am1 = AGE([128, 32], F32)
    sc1 = AGE([128, 32], F32)
    iv1 = AGE([128, 32], F32)
    xqP = [AGE([128, 1024], BF16), AGE([128, 1024], BF16)]
    xqfP = [AGE([128, 8, 128], BF16), AGE([128, 8, 128], BF16)]
    hgP = [AGE([128, 2048], F32), AGE([128, 2048], F32)]
    am2 = AGE([128, 64], F32)
    sc2 = AGE([128, 64], F32)
    iv2 = AGE([128, 64], F32)
    q2 = AGE([128, 2048], F32)
    q1 = (q2[0][:, 0:1024], q2[1])
    hq = AGE([128, 4096], BF16)
    hqf = AGE([128, 32, 128], BF16)
    acctP = [AGE([128, 1024], F32), AGE([128, 1024], F32)]
    for gb in range(NBLK):
        gi = gb % 2
        xn1, xq, xqf, acct = xn1P[gi], xqP[gi], xqfP[gi], acctP[gi]
        b.dma("gpsimd", lambda g, gb=gb, xn1=xn1: g.dma_start(
            out=xn1[0], in_=xn1_ap[gb * 128:(gb + 1) * 128, :]),
            reads=[xn1_hb], writes=[xn1[1]])
        quant(xn1, 1024, 32, am1, sc1, iv1, q1, xq)
        for dc in range(8):
            b.t(lambda t, dc=dc, xq=xq: t.transpose(
                pstr[0][:, (dc % 8) * 128:(dc % 8 + 1) * 128],
                xq[0][:, dc * 128:(dc + 1) * 128], idb[0]),
                reads=[xq[1], idb[1]], writes=[pstr[1]], guard=(256 if dc == 7 else 0))
        b.v(lambda v, xqf=xqf: v.tensor_copy(xqf[0], pstr[0].rearrange("p (c q) -> p c q", q=128)),
            reads=[pstr[1]], writes=[xqf[1]])
        for hg in range(2):
            hg_ = hgP[hg]
            for nb in range(4):
                pp = ps[nb]
                off = hg * 2048 + nb * 512
                b.t(lambda t, off=off, pp=pp: t.matmul(
                    pp[0], onesr[0], bfc[0][0:1, off:off + 512], start=True, stop=False),
                    reads=[onesr[1], bfc[1]], writes=[pp[1]])
            for dc in range(8):
                for nb in range(4):
                    pp = ps[nb]
                    off = hg * 2048 + nb * 512
                    b.t(lambda t, dc=dc, off=off, pp=pp, xqf=xqf: t.matmul(
                        pp[0], xqf[0][:, dc, :], wfc[0][:, dc, off:off + 512],
                        start=False, stop=(dc == 7)),
                        reads=[xqf[1], wfc[1]], writes=[pp[1]])
            for nb in range(4):
                pp = ps[nb]
                b.s(lambda sc, nb=nb, pp=pp, hg_=hg_: sc.activation(
                    hg_[0][:, nb * 512:(nb + 1) * 512], pp[0], AF.Gelu),
                    reads=[pp[1]], writes=[hg_[1]])
        for hg in range(2):
            hq_half = (hq[0][:, hg * 2048:(hg + 1) * 2048], hq[1])
            quant(hgP[hg], 2048, 64, am2, sc2, iv2, q2, hq_half)
        for hc in range(32):
            b.t(lambda t, hc=hc: t.transpose(
                pstr[0][:, (hc % 8) * 128:(hc % 8 + 1) * 128],
                hq[0][:, hc * 128:(hc + 1) * 128], idb[0]),
                reads=[hq[1], idb[1]], writes=[pstr[1]], guard=(256 if hc % 8 == 7 else 0))
            if hc % 8 == 7:
                b.v(lambda v, hc=hc: v.tensor_copy(
                    hqf[0][:, hc - 7:hc + 1, :], pstr[0].rearrange("p (c q) -> p c q", q=128)),
                    reads=[pstr[1]], writes=[hqf[1]])
        b.dma("sync", lambda g, gb=gb, acct=acct: g.dma_start(
            out=acct[0], in_=acc2_ap[gb * 128:(gb + 1) * 128, :]),
            reads=[acc2_hb], writes=[acct[1]])
        for oc in range(2):
            pp = ps[4 + oc]
            b.t(lambda t, oc=oc, pp=pp: t.matmul(
                pp[0], onesr[0], bpj[0][0:1, oc * 512:(oc + 1) * 512], start=True, stop=False),
                reads=[onesr[1], bpj[1]], writes=[pp[1]])
        for hc in range(32):
            for oc in range(2):
                pp = ps[4 + oc]
                b.t(lambda t, hc=hc, oc=oc, pp=pp: t.matmul(
                    pp[0], hqf[0][:, hc, :], wpj[0][:, hc, oc * 512:(oc + 1) * 512],
                    start=False, stop=(hc == 31)),
                    reads=[hqf[1], wpj[1]], writes=[pp[1]])
        for oc in range(2):
            pp = ps[4 + oc]
            b.v(lambda v, oc=oc, pp=pp, acct=acct: v.tensor_tensor(
                acct[0][:, oc * 512:(oc + 1) * 512],
                acct[0][:, oc * 512:(oc + 1) * 512], pp[0], ALU.add),
                reads=[pp[1], acct[1]], writes=[acct[1]])
        b.dma("sync", lambda g, gb=gb, acct=acct: g.dma_start(
            out=out_ap[gb * 128:(gb + 1) * 128, :], in_=acct[0]),
            reads=[acct[1]], writes=[out_hb])
    assert ar.off <= 212200, ar.off


WN_E = ["wfcO", "wprojO", "bfcq", "bprojq", "ident"]


def make_prog_e():
    pr = Prog()
    shp = {"wfcO": [4096, 1024], "wprojO": [1024, 4096], "bfcq": [1, 4096],
           "bprojq": [1, 1024], "ident": [128, 128]}
    for n in WN_E:
        pr.dram_in(n, shp[n])
    pr.dram_in("xn1_h", [T, D])
    pr.dram_in("acc2_h", [T, D])
    pr.dram_out("out", [T, D])
    build_phase_e(pr, pr.hbm)
    pr.finish()
    emit_block(pr.nc, pr.plan, pr.stack)
    return pr


def kernel(**inputs):
    return kernel_fused(**inputs)


def kernel_multilaunch(**inputs):
    x_full = np.ascontiguousarray(np.asarray(inputs["L0_attn_x"], dtype=np.float32))
    ncore = x_full.shape[0]
    w = prep_inputs(inputs)
    cores = list(range(ncore))

    pa = make_prog_a()
    ra = run_launch(pa, [{**{n: w[n] for n in WN_A}, "x": np.ascontiguousarray(x_full[c])}
                         for c in cores])
    pa.stack.close()

    pb = make_prog_b()
    rb = run_launch(pb, [{**{n: w[n] for n in WN_B},
                          "alpha_h": np.asarray(ra[c]["alpha_h"]),
                          "slotin_h": np.asarray(ra[c]["slotin_h"]),
                          "acc": np.asarray(ra[c]["acc"])} for c in cores])
    pb.stack.close()

    pc = make_prog_c()
    rc = run_launch(pc, [{**{n: w[n] for n in WN_C},
                          "x1": np.asarray(rb[c]["x1"])} for c in cores])
    pc.stack.close()

    pd = make_prog_d()
    rd = run_launch(pd, [{**{n: w[n] for n in WN_D},
                          "xc_h": np.asarray(rc[c]["xc_h"]),
                          "sz_h": np.asarray(rc[c]["sz_h"]),
                          "BC_h": np.asarray(rc[c]["BC_h"]),
                          "dtlow_h": np.asarray(rc[c]["dtlow_h"]),
                          "accb_h": np.asarray(rc[c]["accb_h"])} for c in cores])
    pd.stack.close()

    pe = make_prog_e()
    re_ = run_launch(pe, [{**{n: w[n] for n in WN_E},
                           "xn1_h": np.asarray(rc[c]["xn1_h"]),
                           "acc2_h": np.asarray(rd[c]["acc2_h"])} for c in cores])
    pe.stack.close()

    out = np.stack([np.asarray(re_[c]["out"], dtype=np.float32) for c in cores], axis=0)
    return out


def make_prog_fused(debug=False):
    pr = Prog()
    shp = {
        "wq": [1024, 1024], "wk": [1024, 256], "wv": [1024, 256], "wo": [1024, 1024],
        "w1xT": [1024, 512], "w1mT": [1024, 512], "b1": [4, 128], "w2T": [512, 64],
        "b2": [1, 64], "tau": [1, 1], "slot_init": [64, 1024], "slot_scale": [64, 1],
        "ident": [128, 128], "mask1": [128, 128], "maskT": [4, 128, 512],
        "wihT": [1024, 3072], "whhT": [1024, 3072], "bih": [1, 3072], "bhh": [1, 3072],
        "wv0T": [1024, 1024], "bv0": [1, 1024], "wo0T": [1024, 1024], "bo0": [1, 1024],
        "lng": [1, 1024], "lnb": [1, 1024], "inpT": [1024, 4096], "convw": [2048, 4],
        "convb": [16, 128], "xprojTp": [2048, 128],
        "outpT": [2048, 1024], "dtprojT": [64, 2048], "Alog": [2048, 16],
        "dtb": [16, 128], "Dp": [16, 128],
        "wfcO": [4096, 1024], "wprojO": [1024, 4096], "bfcq": [1, 4096], "bprojq": [1, 1024],
    }
    for n, sp in shp.items():
        pr.dram_in(n, sp)
    pr.dram_in("x", [T, D])
    pr.dram_out("out", [T, D])
    mk = pr.dram_out if debug else pr.dram_tmp
    mk("acc", [T, D])
    mk("alpha_h", [T, 64])
    mk("slotin_h", [64, 1024])
    mk("x1", [T, D])
    mk("xn1_h", [T, D])
    mk("accb_h", [T, D])
    pr.dram_tmp("xc_h", [16, 128, T], BF16)
    pr.dram_tmp("sz_h", [16, 128, T], BF16)
    pr.dram_tmp("BC_h", [64, T], BF16)
    pr.dram_tmp("dtlow_h", [64, T], BF16)
    mk("acc2_h", [T, D])
    if debug:
        pr.dram_out("dbgd", [128, 2048])

    build_phase_a(pr, pr.hbm)
    pr.plan.barrier()
    build_phase_b(pr, pr.hbm)
    pr.plan.barrier()
    build_phase_c(pr, pr.hbm)
    pr.plan.barrier()
    build_phase_d(pr, pr.hbm)
    pr.plan.barrier()
    build_phase_e(pr, pr.hbm)
    pr.finish()
    emit_block(pr.nc, pr.plan, pr.stack)
    return pr


WN_ALL = WN_A + WN_B + WN_C + WN_D + WN_E


def kernel_fused(**inputs):
    x_full = np.ascontiguousarray(np.asarray(inputs["L0_attn_x"], dtype=np.float32))
    ncore = x_full.shape[0]
    w = prep_inputs(inputs)
    pr = make_prog_fused()
    wnames = sorted(set(WN_ALL))
    in_maps = [{**{n: w[n] for n in wnames}, "x": np.ascontiguousarray(x_full[c])}
               for c in range(ncore)]
    res = run_launch(pr, in_maps)
    pr.stack.close()
    return np.stack([np.asarray(res[c]["out"], dtype=np.float32) for c in range(ncore)], axis=0)
